# version 07
# baseline (speedup 1.0000x reference)
import os
import numpy as np
import concourse.bass as bass
import concourse.mybir as mybir
from concourse.bass_utils import run_bass_kernel_spmd

F32 = mybir.dt.float32
BF16 = mybir.dt.bfloat16
ALU = mybir.AluOpType
AF = mybir.ActivationFunctionType
AX = mybir.AxisListType

D = 1024
H = 16
DK = 64
FC = 8
LORA = 64
FF = 4096
NE = 8
NCORES = 8
T_CORE = 2048
T_WARM = 1024
T_ALL = T_WARM + T_CORE
TILE = 128
NT = T_ALL // TILE
WARM_TILES = T_WARM // TILE
CH = 16
NCH = TILE // CH
HALF = 1024
EPS_RMS = 1e-6
EPS_GN = 1e-5
EPS_LN = 1e-5

ENGS = ("T", "V", "S", "G", "Y")
CHAN_BODY = {"T": "T", "V": "V", "S": "S", "G": "G", "Y": "Y",
             "Yx": "Y", "Ys": "Y", "Yl": "Y", "Yo": "Y", "Yc": "Y",
             "Gc": "G", "Ga": "G", "Gm0": "G", "Gm1": "G", "Gy": "G"}


class Prog:
    def __init__(self, nc):
        self.nc = nc
        self.ops = []
        self.counts = {c: 0 for c in CHAN_BODY}
        self.sems = {}

    def emit(self, chan, fn, waits=(), inc=0):
        waits = tuple(w for w in waits if w is not None)
        self.ops.append((chan, fn, waits, inc))
        if inc:
            self.counts[chan] += inc
            return (chan, self.counts[chan])
        return None

    def materialize(self, block):
        for c in CHAN_BODY:
            self.sems[c] = self.nc.semaphore("sem_" + c).__enter__()
        watermark = {}

        def body_for(eng_name):
            def body(eng):
                for (oc, fn, waits, inc) in self.ops:
                    if CHAN_BODY[oc] != eng_name:
                        continue
                    for (pc, val) in waits:
                        key = (eng_name, pc)
                        if watermark.get(key, -1) >= val:
                            continue
                        watermark[key] = val
                        eng.wait_ge(self.sems[pc], val)
                    inst = fn(eng)
                    if inc:
                        assert inst is not None
                        inst.then_inc(self.sems[oc], inc)
            return body

        block.tensor(body_for("T"))
        block.vector(body_for("V"))
        block.scalar(body_for("S"))
        block.gpsimd(body_for("G"))
        block.sync(body_for("Y"))


class Arena:

    def __init__(self, nc, cols=52992):
        self.t = nc.sbuf_tensor("arena", [128, cols], F32)
        self.ap = self.t.__enter__()
        self.cols = cols
        self.off = 0
        self.marks = []

    def alloc(self, ncols_f32):
        a = self.off
        self.off += ncols_f32
        assert self.off <= self.cols, f"SBUF arena overflow: {self.off}"
        return a

    def f32(self, shape):
        n = int(np.prod(shape))
        a = self.alloc(n)
        v = self.ap[:, a:a + n]
        if len(shape) == 1:
            return v
        return v.rearrange("p (a b) -> p a b", a=shape[0]) if len(shape) == 2 \
            else v.rearrange("p (a b c) -> p a b c", a=shape[0], b=shape[1])

    def bf16(self, shape):
        n = int(np.prod(shape))
        a = self.alloc((n + 1) // 2)
        v = self.ap[:, a:a + (n + 1) // 2].bitcast(BF16)
        if len(shape) == 1:
            return v[:, 0:n]
        return v[:, 0:n].rearrange("p (a b) -> p a b", a=shape[0]) \
            if len(shape) == 2 else \
            v[:, 0:n].rearrange("p (a b c) -> p a b c", a=shape[0], b=shape[1])

    def mark(self):
        self.marks.append(self.off)

    def release(self):
        self.off = self.marks.pop()


def build_program():
    nc = bass.Bass("TRN2", target_bir_lowering=False, debug=False)
    b = Prog(nc)
    dr = {}

    def din(name, shape):
        dr[name] = nc.dram_tensor(name, list(shape), F32,
                                  kind="ExternalInput").ap()
        return dr[name]

    x_d = din("x", (T_ALL, D))
    for nm in ("Wr", "Wk", "Wv", "Wg", "Wo"):
        din(nm, (D, D))
    din("wA", (D, LORA))
    din("wB", (LORA, D))
    din("fw1", (D, FF))
    din("fw2", (FF, D))
    din("mw1", (NE, D, FF))
    din("mw2r", (NE, FC, FF, 128))
    din("ws1", (D, FF))
    din("ws2r", (FC, FF, 128))
    din("gate_w", (D, NE))
    for nm in ("ln1w", "ln2w", "mur", "muk", "muv", "muw", "mug", "w0",
               "uvec", "gnw", "gnb"):
        din(nm, (128, FC))
    din("gate_b", (NE, 1))
    din("b1", (NE, 128, FF // 128))
    din("b2", (NE, 128, FC))
    din("bs1", (128, FF // 128))
    din("bs2", (128, FC))
    din("ident", (128, 128))
    din("ones_r", (1, 128))
    din("ones_c", (128, 1))
    din("headsel", (128, FC, H))
    din("esel", (NE, NE, 128))
    din("headmap", (H, FC, 128))
    din("reset1", (128, FC * TILE))

    y_d = nc.dram_tensor("y", [T_CORE, D], F32, kind="ExternalOutput").ap()
    u1_scr = nc.dram_tensor("u1_scr", [128, FC, T_CORE], F32,
                            kind="Internal").ap()
    xpu_scr = nc.dram_tensor("xpu_scr", [128, FC, T_CORE], F32,
                             kind="Internal").ap()
    y1_scr = nc.dram_tensor("y1_scr", [128, FC, T_CORE], F32,
                            kind="Internal").ap()
    KDEBUG = os.environ.get("KDEBUG", "") == "1"
    if KDEBUG:
        dbg_lg = nc.dram_tensor("dbg_lg", [NE, T_CORE], F32,
                                kind="ExternalOutput").ap()
        dbg_comb = nc.dram_tensor("dbg_comb", [NE, T_CORE], F32,
                                  kind="ExternalOutput").ap()
        dbg_acc = nc.dram_tensor("dbg_acc", [128, FC, T_CORE], F32,
                                 kind="ExternalOutput").ap()
        dbg_u1 = nc.dram_tensor("dbg_u1", [128, FC, T_CORE], F32,
                                kind="ExternalOutput").ap()
        dbg_y1 = nc.dram_tensor("dbg_y1", [128, FC, T_CORE], F32,
                                kind="ExternalOutput").ap()
        dbg_gtm = nc.dram_tensor("dbg_gtm", [128, 16, 16], F32,
                                 kind="ExternalOutput").ap()

    ctxs = []

    def psum(nm, shp, dt=F32):
        t = nc.psum_tensor(nm, list(shp), dt)
        ctxs.append(t)
        return t.__enter__()

    PSA = psum("PSA", (128, 1024), F32)
    PSU = psum("PSU", (128, 1024), F32)
    PSB = psum("PSB", (128, 512), F32)
    PSC = psum("PSC", (128, 512), F32)
    PSD = psum("PSD", (128, 512), F32)
    PST = psum("PST", (128, 1024), BF16)

    ar = Arena(nc)

    id_f = ar.f32((128,))
    id_bf = ar.bf16((128,))
    ones_r = ar.f32((128,))[0:1, :]
    ones_c_bf = ar.bf16((2,))[:, 0:1]
    hsel = ar.bf16((FC, H))
    hmap_bf = ar.bf16((FC, 128))
    hmap_f = ar.f32((FC, 128))
    esel = ar.f32((NE, 128))
    reset1 = ar.f32((FC * TILE,))
    vecs = {nm: ar.f32((FC,)) for nm in
            ("ln1w", "ln2w", "mur", "muk", "muv", "muw", "mug", "w0",
             "uvec", "gnw", "gnb")}
    gate_b = ar.f32((1,))
    bs1 = ar.f32((FF // 128,))
    bs2 = ar.f32((FC,))
    b1v = [ar.f32((FF // 128,)) for _ in range(NE)]
    b2v = [ar.f32((FC,)) for _ in range(NE)]
    epsc = ar.f32((4,))
    S = ar.f32((FC, DK))
    h_bnd = ar.f32((FC, 1))
    rrow = ar.f32((6, TILE))
    gst = ar.f32((6, TILE))
    comb = ar.f32((HALF,))
    wlf = ar.f32((TILE,))
    wlt = ar.bf16((TILE,))

    def dma_y(dst, src, waits=(), chan="Yc"):
        return b.emit(chan, lambda e, d=dst, s=src: e.dma_start(d, s),
                      waits=waits, inc=16)

    def dma_g(dst, src, waits=(), chan="Gc"):
        return b.emit(chan, lambda e, d=dst, s=src: e.dma_start(d, s),
                      waits=waits, inc=16)

    ld = {}
    dma_y(id_f, dr["ident"])
    dma_g(id_bf, dr["ident"])
    dma_y(ones_r, dr["ones_r"])
    dma_g(ones_c_bf, dr["ones_c"])
    dma_g(hsel, dr["headsel"])
    dma_g(hmap_bf[0:H, :, :], dr["headmap"])
    dma_y(hmap_f[0:H, :, :], dr["headmap"])
    dma_y(esel[0:NE, :, :], dr["esel"])
    dma_y(reset1, dr["reset1"])
    for nm in vecs:
        dma_y(vecs[nm], dr[nm])
    dma_y(gate_b[0:NE, :], dr["gate_b"])
    dma_y(bs1, dr["bs1"])
    dma_y(bs2, dr["bs2"])
    for e_i in range(NE):
        dma_y(b1v[e_i], dr["b1"][e_i])
        dma_y(b2v[e_i], dr["b2"][e_i])
    ev_const_y = ("Yc", b.counts["Yc"])
    ev_const_g = ("Gc", b.counts["Gc"])

    ar.mark()
    wsb = {}
    for nm in ("Wr", "Wk", "Wv", "Wg", "Wo"):
        wsb[nm] = ar.bf16((FC, D))
        dma_g(wsb[nm], dr[nm].rearrange("(kb p) o -> p kb o", p=128),
              chan="Ga")
    w_lA = ar.bf16((FC, LORA))
    dma_g(w_lA, dr["wA"].rearrange("(kb p) o -> p kb o", p=128), chan="Ga")
    w_lB = ar.bf16((D,))
    dma_g(w_lB[0:LORA, :], dr["wB"], chan="Ga")
    ev_wAll = ("Ga", b.counts["Ga"])
    for nm in ("Wr", "Wk", "Wv", "Wg", "Wo", "wA", "wB"):
        ld[nm] = ev_wAll

    FMT = (FC, TILE)
    x_tm = ar.f32((D,))
    x_fm = ar.f32(FMT)
    h_fm = ar.f32(FMT)
    xx_fm = ar.f32(FMT)
    xmix = {nm: ar.bf16(FMT) for nm in ("xr", "xk", "xv", "xw", "xg")}
    r_fm = ar.f32(FMT)
    k_fm = ar.f32(FMT)
    v_fm = ar.f32(FMT)
    d_fm = ar.f32(FMT)
    g_fm = ar.f32(FMT)
    e_fm = ar.f32(FMT)
    abuf = ar.f32(FMT)
    A_ch = ar.f32((FC, NCH))
    rt_fm = ar.f32(FMT)
    kh_e = ar.bf16(FMT)
    kh_o = ar.bf16(FMT)
    khtm = [ar.bf16((FC, 128)) for _ in range(2)]
    vtm = ar.bf16((FC, 128))
    G_fm = ar.f32(FMT)
    rk_bf = ar.bf16(FMT)
    dots_bf = ar.bf16((TILE,))
    dots_bf2 = ar.bf16((TILE,))
    od_fm = ar.f32(FMT)
    tmp_fm = ar.f32(FMT)
    o_fm = e_fm
    onrm = ar.bf16(FMT)
    on_bf = ar.bf16(FMT)
    u1_fm = ar.f32(FMT)
    sig_t = tmp_fm

    def TT(e, out, a, bb, op):
        if op == ALU.bypass:
            return e.tensor_copy(out, a)
        return e.tensor_tensor(out, a, bb, op)

    def fm(t, c=None, sl=None):
        a = t
        if c is not None:
            a = a[:, c, :]
        if sl is not None:
            a = a[:, sl] if c is not None else a[:, :, sl]
        return a

    def _init(e):
        e.memset(epsc[:, 0:1], EPS_RMS)
        e.memset(epsc[:, 1:2], EPS_GN)
        e.memset(epsc[:, 2:3], EPS_LN)
        e.memset(S, 0.0)
        e.memset(h_bnd, 0.0)
        e.memset(abuf, 0.0)
        e.memset(kh_e, 0.0)
        return e.memset(kh_o, 0.0)
    ev_init = b.emit("V", _init, inc=1)

    ev_chain = ev_init
    ev_u_last = {}
    ev_tile_T_done = None
    ev_xfm_free = ev_init
    ev_oevac = None
    ev_psa_free = ev_init
    ev_pst_free = ev_init
    ev_psb_free = {0: ev_init, 1: ev_init}
    ev_psd_free = ev_init
    ev_psu_free = ev_init
    ev_rowfree = ev_init

    resetv = reset1

    for ti in range(NT):
        warm = ti < WARM_TILES
        t0 = ti * TILE

        ev_x = dma_y(x_tm, x_d[t0:t0 + TILE, :],
                     waits=(ev_tile_T_done, ev_const_y), chan="Yx")

        ev_tr = None
        for c in range(FC):
            ev_tr = b.emit("T", lambda e, c=c: e.transpose(
                PSA[:, 128 * c:128 * c + 128],
                x_tm[:, 128 * c:128 * c + 128], id_f),
                waits=(ev_x, ev_psa_free) if c == 0 else (),
                inc=1 if c == FC - 1 else 0)
        ev_tile_T_done = ev_tr
        def xevac(e):
            return e.copy(x_fm[:].rearrange("p c t -> p (c t)"), PSA[:])
        ev_xfm = b.emit("S", xevac, waits=(ev_tr, ev_xfm_free, ev_const_y),
                        inc=1)
        ev_psa_free = ev_xfm

        def sq(e):
            return TT(e, onrm[:].rearrange("p c t -> p (c t)"),
                      x_fm[:].rearrange("p c t -> p (c t)"),
                      x_fm[:].rearrange("p c t -> p (c t)"), ALU.mult)
        ev_sq = b.emit("V", sq, waits=(ev_xfm,), inc=1)
        ev_sum = None
        for c in range(FC):
            ev_sum = b.emit("T", lambda e, c=c: e.matmul(
                PSD[0:1, 0:TILE], ones_c_bf, fm(onrm, c),
                start=(c == 0), stop=(c == FC - 1)),
                waits=(ev_sq, ev_psd_free, ev_const_g) if c == 0 else (),
                inc=1 if c == FC - 1 else 0)
        def r_ln(e):
            return e.activation(rrow[0:1, 0, :], PSD[0:1, 0:TILE], AF.Ln,
                                bias=epsc[0:1, 0:1], scale=1.0 / D)
        ev_rl = b.emit("S", r_ln, waits=(ev_sum, ev_rowfree), inc=1)
        def r_ex(e):
            return e.activation(rrow[0:1, 1, :], rrow[0:1, 0, :], AF.Exp,
                                scale=-0.5)
        ev_r2 = b.emit("S", r_ex, waits=(ev_rl,), inc=1)
        ev_bc = b.emit("T", lambda e: e.matmul(
            PSC[:, 0:TILE], ones_r, rrow[0:1, 1, :], start=True, stop=True),
            waits=(ev_r2, ev_psb_free[1]), inc=1)
        ev_h = None
        for c in range(FC):
            ev_h = b.emit("V", lambda e, c=c: e.scalar_tensor_tensor(
                fm(h_fm, c), fm(x_fm, c), vecs["ln1w"][:, c:c + 1],
                PSC[:, 0:TILE], ALU.mult, ALU.mult),
                waits=(ev_bc,), inc=1 if c == FC - 1 else 0)
        ev_psb_free[1] = ev_h
        ev_psd_free = ev_h

        def xxm(e):
            return e.tensor_sub(xx_fm[:, :, 1:TILE], h_fm[:, :, 0:TILE - 1],
                                h_fm[:, :, 1:TILE])
        b.emit("V", xxm, waits=(ev_h,))
        def xxb(e):
            return e.tensor_sub(xx_fm[:, :, 0:1], h_bnd, h_fm[:, :, 0:1])
        b.emit("V", xxb)
        def hbn(e):
            return TT(e, h_bnd, h_fm[:, :, TILE - 1:TILE],
                      h_fm[:, :, TILE - 1:TILE], ALU.bypass)
        ev_xx = b.emit("V", hbn, inc=1)

        mixes = (("xr", "mur"), ("xk", "muk"), ("xv", "muv"),
                 ("xw", "muw"), ("xg", "mug"))
        if warm:
            mixes = (("xk", "muk"), ("xv", "muv"), ("xw", "muw"))
        ev_mix = None
        for nm, mu in mixes:
            for c in range(FC):
                ev_mix = b.emit("V", lambda e, nm=nm, mu=mu, c=c:
                                e.scalar_tensor_tensor(
                                    fm(xmix[nm], c), fm(xx_fm, c),
                                    vecs[mu][:, c:c + 1], fm(h_fm, c),
                                    ALU.mult, ALU.add),
                                inc=1 if c == FC - 1 and nm == mixes[-1][0]
                                else 0)
        ev_xfm_free = None

        pslot = [PSB, PSC]

        def proj(dst_fm, src, wname, extra_evac=None):
            evs = None
            for m in range(FC):
                ps = pslot[m % 2]
                evm = None
                for kb in range(FC):
                    evm = b.emit("T", lambda e, m=m, kb=kb, w=wname, ps=ps:
                                 e.matmul(
                                     ps[:, 0:TILE],
                                     wsb[w][:, kb, 128 * m:128 * m + 128],
                                     fm(xmix[src], kb),
                                     start=(kb == 0), stop=(kb == FC - 1)),
                                 waits=(ev_mix, ld[wname],
                                        ev_psb_free[m % 2]) if kb == 0 else (),
                                 inc=1 if kb == FC - 1 else 0)
                def evac(e, m=m, ps=ps, dst=dst_fm):
                    return e.copy(fm(dst, m), ps[:, 0:TILE])
                eve = b.emit("S", evac, waits=(evm,), inc=1)
                if extra_evac is not None:
                    def evac2(e, m=m, ps=ps, dst=extra_evac):
                        return e.copy(fm(dst, m), ps[:, 0:TILE])
                    eve = b.emit("S", evac2, inc=1)
                ev_psb_free[m % 2] = eve
                evs = eve
            return evs

        ev_k = proj(k_fm, "xk", "Wk")
        ev_v = proj(v_fm, "xv", "Wv")
        if not warm:
            ev_r = proj(r_fm, "xr", "Wr")
            ev_gp = proj(g_fm, "xg", "Wg")
            def sil_e(e):
                return e.activation(sig_t[:].rearrange("p c t -> p (c t)"),
                                    g_fm[:].rearrange("p c t -> p (c t)"),
                                    AF.Tanh, scale=0.5)
            ev_se = b.emit("S", sil_e, waits=(ev_gp,), inc=1)
            def sil_a(e):
                return e.tensor_scalar(sig_t[:].rearrange("p c t -> p (c t)"),
                                       sig_t[:].rearrange("p c t -> p (c t)"),
                                       0.5, 0.5, ALU.mult, ALU.add)
            b.emit("V", sil_a, waits=(ev_se,))
            def sil_m(e):
                return TT(e, g_fm[:].rearrange("p c t -> p (c t)"),
                          g_fm[:].rearrange("p c t -> p (c t)"),
                          sig_t[:].rearrange("p c t -> p (c t)"), ALU.mult)
            ev_g = b.emit("V", sil_m, inc=1)

        ev_l = None
        for kb in range(FC):
            ev_l = b.emit("T", lambda e, kb=kb: e.matmul(
                PSD[0:LORA, 0:TILE], w_lA[:, kb, :], fm(xmix["xw"], kb),
                start=(kb == 0), stop=(kb == FC - 1)),
                waits=(ev_mix, ld["wA"], ld["wB"], ev_psd_free)
                if kb == 0 else (),
                inc=1 if kb == FC - 1 else 0)
        def th_e(e):
            return e.activation(wlt[0:LORA, :], PSD[0:LORA, 0:TILE], AF.Tanh)
        ev_th = b.emit("S", th_e, waits=(ev_l,), inc=1)
        ev_psd_free = ev_th
        ev_wp = None
        for m in range(FC):
            ps = pslot[m % 2]
            ev_mm = b.emit("T", lambda e, m=m, ps=ps: e.matmul(
                ps[:, 0:TILE], w_lB[0:LORA, 128 * m:128 * m + 128],
                wlt[0:LORA, :], start=True, stop=True),
                waits=(ev_th, ev_psb_free[m % 2]), inc=1)
            def d_e1(e, m=m, ps=ps):
                return e.activation(fm(d_fm, m), ps[:, 0:TILE], AF.Exp,
                                    bias=vecs["w0"][:, m:m + 1])
            ev_wp = b.emit("S", d_e1, waits=(ev_mm,), inc=1)
            ev_psb_free[m % 2] = ev_wp
        def d_e2(e):
            return e.activation(d_fm[:].rearrange("p c t -> p (c t)"),
                                d_fm[:].rearrange("p c t -> p (c t)"),
                                AF.Exp, scale=-1.0)
        ev_d = b.emit("S", d_e2, waits=(ev_wp, ("S", b.counts["S"])), inc=1)

        d4 = d_fm[:].rearrange("p c (n l) -> p c n l", l=CH)
        a4 = abuf[:].rearrange("p c (n l) -> p c n l", l=CH)
        e4 = e_fm[:].rearrange("p c (n l) -> p c n l", l=CH)
        def afill(e):
            return TT(e, a4[:, :, :, 1:CH], d4[:, :, :, 0:CH - 1],
                      d4[:, :, :, 0:CH - 1], ALU.bypass)
        ev_af = b.emit("V", afill, waits=(ev_d,), inc=1)
        def scanE(e):
            return e.tensor_tensor_scan(
                e_fm[:].rearrange("p c t -> p (c t)"),
                abuf[:].rearrange("p c t -> p (c t)"),
                resetv, 0.0, ALU.mult, ALU.add)
        ev_E = b.emit("V", scanE, waits=(ev_af,), inc=1)
        def ach(e):
            return TT(e, A_ch[:].rearrange("p c (n o) -> p c n o", o=1),
                      e4[:, :, :, CH - 1:CH], d4[:, :, :, CH - 1:CH],
                      ALU.mult)
        ev_A = b.emit("V", ach, inc=1)
        if not warm:
            def rtx(e):
                return TT(e, rt_fm[:].rearrange("p c t -> p (c t)"),
                          r_fm[:].rearrange("p c t -> p (c t)"),
                          e_fm[:].rearrange("p c t -> p (c t)"), ALU.mult)
            ev_rt = b.emit("V", rtx, waits=(ev_r,), inc=1)
        def a2fill(e):
            return TT(e, a4[:, :, :, 1:CH], d4[:, :, :, CH - 1:0:-1],
                      d4[:, :, :, CH - 1:0:-1], ALU.bypass)
        ev_a2 = b.emit("V", a2fill, inc=1)
        def scanS(e):
            return e.tensor_tensor_scan(
                e_fm[:].rearrange("p c t -> p (c t)"),
                abuf[:].rearrange("p c t -> p (c t)"),
                resetv, 0.0, ALU.mult, ALU.add)
        ev_Sx = b.emit("V", scanS, waits=(ev_a2,), inc=1)
        ke4 = kh_e[:].rearrange("p c (n l) -> p c n l", l=2 * CH)
        ko4 = kh_o[:].rearrange("p c (n l) -> p c n l", l=2 * CH)
        k4 = k_fm[:].rearrange("p c (n l) -> p c n l", l=2 * CH)
        ex4 = e_fm[:].rearrange("p c (n l) -> p c n l", l=CH)
        def khe(e):
            return TT(e, ke4[:, :, :, 0:CH], k4[:, :, :, 0:CH],
                      e_fm[:].rearrange("p c (n l) -> p c n l", l=2 * CH)
                      [:, :, :, CH - 1::-1], ALU.mult)
        ev_khe = b.emit("V", khe, waits=(ev_Sx, ev_k), inc=1)
        def kho(e):
            return TT(e, ko4[:, :, :, CH:2 * CH], k4[:, :, :, CH:2 * CH],
                      e_fm[:].rearrange("p c (n l) -> p c n l", l=2 * CH)
                      [:, :, :, 2 * CH - 1:CH - 1:-1], ALU.mult)
        ev_kho = b.emit("V", kho, inc=1)

        ev_tm = None
        for (idx, (src, dst, evd)) in enumerate(
                ((kh_e, khtm[0], ev_khe), (kh_o, khtm[1], ev_kho))):
            evt = None
            for c in range(FC):
                evt = b.emit("T", lambda e, src=src, c=c: e.transpose(
                    PST[:, 128 * c:128 * c + 128], fm(src, c), id_bf),
                    waits=(evd, ev_pst_free) if c == 0 else (),
                    inc=1 if c == FC - 1 else 0)
            def ev_e(e, dst=dst):
                return e.copy(dst[:].rearrange("p c t -> p (c t)"), PST[:])
            ev_tm = b.emit("S", ev_e, waits=(evt,), inc=1)
            ev_pst_free = ev_tm
        evt = None
        for c in range(FC):
            evt = b.emit("T", lambda e, c=c: e.transpose(
                PSA[:, 128 * c:128 * c + 128], fm(v_fm, c), id_f),
                waits=(ev_v, ev_psa_free) if c == 0 else (),
                inc=1 if c == FC - 1 else 0)
        def vtevac(e):
            return e.copy(vtm[:].rearrange("p c t -> p (c t)"), PSA[:])
        ev_tm = b.emit("S", vtevac, waits=(evt,), inc=1)
        ev_psa_free = ev_tm

        for ci in range(NCH):
            row0 = 32 * (ci // 2)
            par = ci % 2
            ktm = khtm[par]
            puv = PSU[:, 512 * par:512 * par + 512]
            ev_u = None
            for h in range(H):
                ev_u = b.emit("T", lambda e, h=h, row0=row0, ktm=ktm,
                              puv=puv: e.matmul(
                                  puv[64 * (h % 2):64 * (h % 2) + 64,
                                      DK * (h // 2):DK * (h // 2) + DK],
                                  ktm[row0:row0 + 32, h // 2,
                                      64 * (h % 2):64 * (h % 2) + 64],
                                  vtm[row0:row0 + 32, h // 2,
                                      64 * (h % 2):64 * (h % 2) + 64],
                                  start=True, stop=True,
                                  tile_position=(row0, 64 * (h % 2))),
                              waits=(ev_tm, ev_chain,
                                     ev_u_last.get(par)) if h == 0 else (),
                              inc=1 if h == H - 1 else 0)
            ev_oi = None
            if not warm:
                for h in range(H):
                    ev_oi = b.emit("T", lambda e, h=h, ci=ci: e.matmul(
                        PSA[64 * (h % 2):64 * (h % 2) + 64,
                            128 * (h // 2) + CH * ci:
                            128 * (h // 2) + CH * ci + CH],
                        S[64 * (h % 2):64 * (h % 2) + 64, h // 2, :],
                        rt_fm[64 * (h % 2):64 * (h % 2) + 64, h // 2,
                              CH * ci:CH * ci + CH],
                        start=True, stop=True,
                        tile_position=(64 * (h % 2), 64 * (h % 2))),
                        waits=(ev_rt, ev_psa_free) if (h == 0 and ci == 0)
                        else (), inc=1 if h == H - 1 else 0)
            def supd1(e, ci=ci):
                return TT(e, S.rearrange("p c v -> p c v"), S,
                          A_ch[:, :, ci:ci + 1].to_broadcast((128, FC, DK)),
                          ALU.mult)
            b.emit("V", supd1, waits=(ev_u, ev_oi, ev_A))
            def supd2(e, puv=puv):
                return TT(e, S.rearrange("p c v -> p (c v)"),
                          S.rearrange("p c v -> p (c v)"), puv, ALU.add)
            ev_chain = b.emit("V", supd2, inc=1)
            ev_u_last[par] = ev_chain

        if warm:
            ev_psa_free = ev_chain
            continue

        def oevac(e):
            return e.copy(o_fm[:].rearrange("p c t -> p (c t)"), PSA[:])
        ev_oevac = b.emit("S", oevac, waits=(ev_oi,), inc=1)
        ev_psa_free = ev_oevac

        def odz(e):
            return e.memset(od_fm, 0.0)
        ev_Gprev = b.emit("V", odz, inc=1)
        ev_rk = None
        G4 = G_fm[:].rearrange("p c (n l) -> p c n l", l=CH)
        rk4 = rk_bf[:].rearrange("p c (n l) -> p c n l", l=CH)
        v4 = v_fm[:].rearrange("p c (n l) -> p c n l", l=CH)
        od4 = od_fm[:].rearrange("p c (n l) -> p c n l", l=CH)
        tp4 = tmp_fm[:].rearrange("p c (n l) -> p c n l", l=CH)
        for dl in range(CH):
            W = CH - dl
            ncols = NCH * W
            if dl == 0:
                def g0(e):
                    last = None
                    for c in range(FC):
                        last = e.tensor_scalar(fm(G_fm, c), fm(r_fm, c),
                                               vecs["uvec"][:, c:c + 1],
                                               None, ALU.mult)
                    return last
                ev_Gprev = b.emit("V", g0, waits=(ev_r, ev_Gprev), inc=1)
            elif dl == 1:
                def g1(e):
                    return TT(e, G_fm[:].rearrange("p c t -> p (c t)"),
                              r_fm[:].rearrange("p c t -> p (c t)"),
                              r_fm[:].rearrange("p c t -> p (c t)"),
                              ALU.bypass)
                ev_Gprev = b.emit("V", g1, inc=1)
            else:
                def gu(e, dl=dl):
                    return TT(e, G4[:, :, :, dl:CH], G4[:, :, :, dl:CH],
                              d4[:, :, :, 1:CH - dl + 1], ALU.mult)
                ev_Gprev = b.emit("V", gu, inc=1)
            def rkx(e, dl=dl, W=W):
                return e.tensor_tensor(rk4[:, :, :, dl:CH],
                                       G4[:, :, :, dl:CH],
                                       k_fm[:].rearrange(
                                           "p c (n l) -> p c n l", l=CH)
                                       [:, :, :, 0:W], ALU.mult)
            ev_rk = b.emit("V", rkx, inc=1)
            psdot = PSD if dl % 2 == 0 else PSB
            psbc = PSA if dl % 2 == 0 else PSU
            dfree = ev_psd_free if dl % 2 == 0 else ev_psb_free[0]
            bfree = ev_psa_free if dl % 2 == 0 else ev_psu_free
            dotv = dots_bf[0:H, 0:128] if dl % 2 == 0 \
                else dots_bf2[0:H, 0:128]
            ev_sl = None
            for c in range(FC):
                ev_sl = b.emit("T", lambda e, c=c, dl=dl, ncols=ncols,
                               psdot=psdot: e.matmul(
                                   psdot[0:H, 0:ncols], hsel[:, c, :],
                                   rk_bf[:].rearrange(
                                       "p cc (n l) -> p cc n l", l=CH)
                                   [:, c, :, dl:CH],
                                   start=(c == 0), stop=(c == FC - 1)),
                               waits=(ev_rk, dfree) if c == 0 else (),
                               inc=1 if c == FC - 1 else 0)
            def cpd(e, ncols=ncols, psdot=psdot, dotv=dotv):
                return e.copy(dotv[:, 0:ncols], psdot[0:H, 0:ncols])
            ev_cd = b.emit("S", cpd, waits=(ev_sl,), inc=1)
            if dl % 2 == 0:
                ev_psd_free = ev_cd
            else:
                ev_psb_free[0] = ev_cd
            ev_b2 = None
            for c in range(FC):
                ev_b2 = b.emit("T", lambda e, c=c, W=W, ncols=ncols,
                               psbc=psbc, dotv=dotv: e.matmul(
                    psbc[:, c * 128:c * 128 + ncols],
                    hmap_bf[0:H, c, :], dotv[:, 0:ncols],
                    start=True, stop=True),
                    waits=(ev_cd, bfree) if c == 0 else (),
                    inc=1 if c == FC - 1 else 0)
            def contrib(e, dl=dl, W=W, psbc=psbc):
                return e.scalar_tensor_tensor(
                    tp4[:, :, :, dl:CH],
                    psbc[:].rearrange("p (c x) -> p c x", c=FC)
                    [:, :, 0:W * NCH].rearrange("p c (n l) -> p c n l",
                                                n=NCH),
                    1.0,
                    v4[:, :, :, 0:W], ALU.mult, ALU.mult)
            b.emit("V", contrib, waits=(ev_b2,))
            def oacc(e, dl=dl):
                return TT(e, od4[:, :, :, dl:CH], od4[:, :, :, dl:CH],
                          tp4[:, :, :, dl:CH], ALU.add)
            ev_Gprev = b.emit("V", oacc, inc=1)
            if dl % 2 == 0:
                ev_psa_free = ev_Gprev
            else:
                ev_psu_free = ev_Gprev

        ev_u_last[0] = ev_psu_free
        ev_u_last[1] = ev_psu_free
        def ocomb(e):
            return TT(e, o_fm[:].rearrange("p c t -> p (c t)"),
                      o_fm[:].rearrange("p c t -> p (c t)"),
                      od_fm[:].rearrange("p c t -> p (c t)"), ALU.add)
        ev_ot = b.emit("V", ocomb, inc=1)

        def osq(e):
            return TT(e, onrm[:].rearrange("p c t -> p (c t)"),
                      o_fm[:].rearrange("p c t -> p (c t)"),
                      o_fm[:].rearrange("p c t -> p (c t)"), ALU.mult)
        b.emit("V", osq)
        def ocp(e):
            return TT(e, on_bf[:].rearrange("p c t -> p (c t)"),
                      o_fm[:].rearrange("p c t -> p (c t)"),
                      o_fm[:].rearrange("p c t -> p (c t)"), ALU.bypass)
        ev_ocp = b.emit("V", ocp, inc=1)
        ev_gs = None
        for c in range(FC):
            b.emit("T", lambda e, c=c: e.matmul(
                PSD[0:H, 0:TILE], hsel[:, c, :], fm(on_bf, c),
                start=(c == 0), stop=(c == FC - 1)),
                waits=(ev_ocp, ev_psd_free) if c == 0 else ())
        for c in range(FC):
            ev_gs = b.emit("T", lambda e, c=c: e.matmul(
                PSD[0:H, TILE:2 * TILE], hsel[:, c, :], fm(onrm, c),
                start=(c == 0), stop=(c == FC - 1)),
                inc=1 if c == FC - 1 else 0)
        def gmean(e):
            return e.tensor_scalar(gst[0:H, 0, :], PSD[0:H, 0:TILE],
                                   1.0 / DK, None, ALU.mult)
        b.emit("V", gmean, waits=(ev_gs,))
        def gm2(e):
            return TT(e, gst[0:H, 1, :], gst[0:H, 0, :], gst[0:H, 0, :],
                      ALU.mult)
        b.emit("V", gm2)
        def gvar(e):
            return e.scalar_tensor_tensor(gst[0:H, 2, :],
                                          PSD[0:H, TILE:2 * TILE],
                                          1.0 / DK, gst[0:H, 1, :],
                                          ALU.mult, ALU.subtract)
        ev_gv = b.emit("V", gvar, inc=1)
        ev_psd_free = ev_gv
        def gln(e):
            return e.activation(gst[0:H, 3, :], gst[0:H, 2, :], AF.Ln,
                                bias=epsc[0:H, 1:2])
        ev_gl = b.emit("S", gln, waits=(ev_gv,), inc=1)
        def gex(e):
            return e.activation(gst[0:H, 4, :], gst[0:H, 3, :], AF.Exp,
                                scale=-0.5)
        ev_gr = b.emit("S", gex, waits=(ev_gl,), inc=1)
        ev_gnlast = None
        for c in range(FC):
            ps = pslot[c % 2]
            ev_mb = b.emit("T", lambda e, c=c, ps=ps: e.matmul(
                ps[:, 0:TILE], hmap_f[0:H, c, :], gst[0:H, 0, :],
                start=True, stop=True),
                waits=(ev_gr, ev_psb_free[c % 2]), inc=0)
            ev_rb = b.emit("T", lambda e, c=c, ps=ps: e.matmul(
                ps[:, TILE:2 * TILE], hmap_f[0:H, c, :], gst[0:H, 4, :],
                start=True, stop=True), inc=1)
            def n1(e, c=c, ps=ps):
                return TT(e, fm(tmp_fm, c), fm(o_fm, c), ps[:, 0:TILE],
                          ALU.subtract)
            b.emit("V", n1, waits=(ev_rb,))
            def n2(e, c=c, ps=ps):
                return TT(e, fm(tmp_fm, c), fm(tmp_fm, c), ps[:, TILE:2 * TILE],
                          ALU.mult)
            ev_n2 = b.emit("V", n2, inc=1)
            ev_psb_free[c % 2] = ev_n2
            def n3(e, c=c):
                return e.tensor_scalar(fm(tmp_fm, c), fm(tmp_fm, c),
                                       vecs["gnw"][:, c:c + 1],
                                       vecs["gnb"][:, c:c + 1],
                                       ALU.mult, ALU.add)
            b.emit("V", n3)
            def n4(e, c=c):
                return TT(e, fm(on_bf, c), fm(tmp_fm, c), fm(g_fm, c),
                          ALU.mult)
            ev_gnlast = b.emit("V", n4, waits=(ev_g,), inc=1)

        ev_u1 = None
        for m in range(FC):
            ps = pslot[m % 2]
            evm = None
            for kb in range(FC):
                evm = b.emit("T", lambda e, m=m, kb=kb, ps=ps: e.matmul(
                    ps[:, 0:TILE], wsb["Wo"][:, kb, 128 * m:128 * m + 128],
                    fm(on_bf, kb), start=(kb == 0), stop=(kb == FC - 1)),
                    waits=(ev_gnlast, ld["Wo"], ev_psb_free[m % 2])
                    if kb == 0 else (), inc=1 if kb == FC - 1 else 0)
            def u1e(e, m=m, ps=ps):
                return TT(e, fm(u1_fm, m), fm(x_fm, m), ps[:, 0:TILE], ALU.add)
            ev_u1 = b.emit("V", u1e, waits=(evm,), inc=1)
            ev_psb_free[m % 2] = ev_u1
        def xpu(e):
            return TT(e, tmp_fm[:].rearrange("p c t -> p (c t)"),
                      x_fm[:].rearrange("p c t -> p (c t)"),
                      u1_fm[:].rearrange("p c t -> p (c t)"), ALU.add)
        ev_xpu = b.emit("V", xpu, waits=(ev_u1,), inc=1)
        ev_xfm_free = ev_xpu
        rt0 = (ti - WARM_TILES) * TILE
        ev_su1 = dma_y(u1_scr[:, :, rt0:rt0 + TILE], u1_fm[:],
                       waits=(ev_u1,), chan="Ys")
        if KDEBUG:
            dma_y(dbg_u1[:, :, rt0:rt0 + TILE], u1_fm[:], chan="Ys")
        ev_sxp = dma_y(xpu_scr[:, :, rt0:rt0 + TILE], tmp_fm[:],
                       waits=(ev_xpu,), chan="Ys")
        ev_rowfree = ev_su1
        ev_xfm_free = ev_sxp

    ev_phaseA_T = ("T", b.counts["T"])
    ev_phaseA_V = ("V", b.counts["V"])
    ev_phaseA_done = b.emit("V", lambda e: e.engine_nop()
                            if hasattr(e, "engine_nop") else e.nop(),
                            waits=(ev_su1, ev_sxp), inc=1)

    ar.release()
    ar.mark()
    w_f1 = ar.bf16((FC, FF))
    w_f2 = ar.bf16((FF // 128, D))
    dma_g(w_f1, dr["fw1"].rearrange("(kb p) o -> p kb o", p=128),
          waits=(ev_phaseA_T,), chan="Ga")
    dma_g(w_f2, dr["fw2"].rearrange("(kb p) o -> p kb o", p=128),
          waits=(ev_phaseA_T,), chan="Ga")
    ev_fwAll = ("Ga", b.counts["Ga"])
    ld["fw1"] = ev_fwAll
    ld["fw2"] = ev_fwAll
    uB = ar.f32(FMT)
    xpB = ar.f32(FMT)
    h2B = ar.bf16(FMT)
    hidB = ar.bf16((FF // 128, TILE))
    v2B = ar.f32(FMT)
    y1B = ar.f32(FMT)
    sqB = ar.bf16(FMT)
    rowB = ar.f32((6, TILE))

    ev_uB_free = ev_phaseA_done
    ev_xpB_free = ev_phaseA_done
    ev_y1_w = None
    for tj in range(T_CORE // TILE):
        t0 = tj * TILE
        ev_lu = dma_y(uB[:], u1_scr[:, :, t0:t0 + TILE],
                      waits=(ev_uB_free, ev_su1), chan="Yl")
        ev_lx = dma_y(xpB[:], xpu_scr[:, :, t0:t0 + TILE],
                      waits=(ev_xpB_free, ev_sxp), chan="Yl")
        def sqb(e):
            return TT(e, sqB[:].rearrange("p c t -> p (c t)"),
                      uB[:].rearrange("p c t -> p (c t)"),
                      uB[:].rearrange("p c t -> p (c t)"), ALU.mult)
        ev_sq = b.emit("V", sqb, waits=(ev_lx, ev_phaseA_done), inc=1)
        ev_sum = None
        for c in range(FC):
            ev_sum = b.emit("T", lambda e, c=c: e.matmul(
                PSD[0:1, 0:TILE], ones_c_bf, fm(sqB, c),
                start=(c == 0), stop=(c == FC - 1)),
                waits=(ev_sq, ev_psd_free) if c == 0 else (),
                inc=1 if c == FC - 1 else 0)
        def rln(e):
            return e.activation(rowB[0:1, 0, :], PSD[0:1, 0:TILE], AF.Ln,
                                bias=epsc[0:1, 0:1], scale=1.0 / D)
        ev_rl = b.emit("S", rln, waits=(ev_sum,), inc=1)
        def rex(e):
            return e.activation(rowB[0:1, 1, :], rowB[0:1, 0, :], AF.Exp,
                                scale=-0.5)
        ev_r2 = b.emit("S", rex, waits=(ev_rl,), inc=1)
        ev_psd_free = ev_r2
        ev_bc = b.emit("T", lambda e: e.matmul(
            PSC[:, 0:TILE], ones_r, rowB[0:1, 1, :], start=True, stop=True),
            waits=(ev_r2, ev_psb_free[1]), inc=1)
        ev_h2 = None
        for c in range(FC):
            ev_h2 = b.emit("V", lambda e, c=c: e.scalar_tensor_tensor(
                fm(h2B, c), fm(uB, c), vecs["ln2w"][:, c:c + 1],
                PSC[:, 0:TILE], ALU.mult, ALU.mult),
                waits=(ev_bc,), inc=1 if c == FC - 1 else 0)
        ev_psb_free[1] = ev_h2
        ev_uB_free = ev_h2
        ev_gel = None
        for m in range(FF // 128):
            ps = pslot[m % 2]
            evm = None
            for kb in range(FC):
                evm = b.emit("T", lambda e, m=m, kb=kb, ps=ps: e.matmul(
                    ps[:, 0:TILE], w_f1[:, kb, 128 * m:128 * m + 128],
                    fm(h2B, kb), start=(kb == 0), stop=(kb == FC - 1)),
                    waits=(ev_h2, ld["fw1"], ev_psb_free[m % 2])
                    if kb == 0 else (), inc=1 if kb == FC - 1 else 0)
            def gel(e, m=m, ps=ps):
                return e.activation(hidB[:, m, :], ps[:, 0:TILE], AF.Gelu)
            ev_gel = b.emit("S", gel, waits=(evm,), inc=1)
            ev_psb_free[m % 2] = ev_gel
        ev_v2 = None
        for m in range(FC):
            ps = pslot[m % 2]
            evm = None
            for kb in range(FF // 128):
                evm = b.emit("T", lambda e, m=m, kb=kb, ps=ps: e.matmul(
                    ps[:, 0:TILE], w_f2[:, kb, 128 * m:128 * m + 128],
                    hidB[:, kb, :], start=(kb == 0),
                    stop=(kb == FF // 128 - 1)),
                    waits=(ev_gel, ld["fw2"], ev_psb_free[m % 2])
                    if kb == 0 else (), inc=1 if kb == FF // 128 - 1 else 0)
            def v2e(e, m=m, ps=ps):
                return TT(e, fm(v2B, m), fm(xpB, m), ps[:, 0:TILE], ALU.add)
            ev_v2 = b.emit("V", v2e, waits=(evm, ev_lx), inc=1)
            ev_psb_free[m % 2] = ev_v2
        ev_xpB_free = ev_v2
        def sq2(e):
            return TT(e, sqB[:].rearrange("p c t -> p (c t)"),
                      v2B[:].rearrange("p c t -> p (c t)"),
                      v2B[:].rearrange("p c t -> p (c t)"), ALU.mult)
        b.emit("V", sq2, waits=(ev_v2,))
        def v2c(e):
            return TT(e, h2B[:].rearrange("p c t -> p (c t)"),
                      v2B[:].rearrange("p c t -> p (c t)"),
                      v2B[:].rearrange("p c t -> p (c t)"), ALU.bypass)
        ev_v2b = b.emit("V", v2c, inc=1)
        ev_s2 = None
        for c in range(FC):
            b.emit("T", lambda e, c=c: e.matmul(
                PSD[0:1, 0:TILE], ones_c_bf, fm(h2B, c),
                start=(c == 0), stop=(c == FC - 1)),
                waits=(ev_v2b, ev_psd_free) if c == 0 else ())
        for c in range(FC):
            ev_s2 = b.emit("T", lambda e, c=c: e.matmul(
                PSD[0:1, TILE:2 * TILE], ones_c_bf, fm(sqB, c),
                start=(c == 0), stop=(c == FC - 1)),
                inc=1 if c == FC - 1 else 0)
        def lmean(e):
            return e.tensor_scalar(rowB[0:1, 0, :], PSD[0:1, 0:TILE],
                                   1.0 / D, None, ALU.mult)
        b.emit("V", lmean, waits=(ev_s2,))
        def lm2(e):
            return TT(e, rowB[0:1, 1, :], rowB[0:1, 0, :], rowB[0:1, 0, :],
                      ALU.mult)
        b.emit("V", lm2)
        def lvar(e):
            return e.scalar_tensor_tensor(rowB[0:1, 2, :],
                                          PSD[0:1, TILE:2 * TILE],
                                          1.0 / D, rowB[0:1, 1, :],
                                          ALU.mult, ALU.subtract)
        ev_lv = b.emit("V", lvar, inc=1)
        ev_psd_free = ev_lv
        def lln(e):
            return e.activation(rowB[0:1, 3, :], rowB[0:1, 2, :], AF.Ln,
                                bias=epsc[0:1, 2:3])
        ev_ll = b.emit("S", lln, waits=(ev_lv,), inc=1)
        def lex(e):
            return e.activation(rowB[0:1, 3, :], rowB[0:1, 3, :], AF.Exp,
                                scale=-0.5)
        ev_lr = b.emit("S", lex, waits=(ev_ll,), inc=1)
        b.emit("T", lambda e: e.matmul(PSC[:, 0:TILE], ones_r, rowB[0:1, 0, :],
                                       start=True, stop=True),
               waits=(ev_lr, ev_psb_free[1]))
        ev_bb = b.emit("T", lambda e: e.matmul(
            PSC[:, TILE:2 * TILE], ones_r, rowB[0:1, 3, :],
            start=True, stop=True), inc=1)
        ev_y1t = None
        for c in range(FC):
            def za(e, c=c):
                return TT(e, fm(y1B, c), fm(v2B, c), PSC[:, 0:TILE],
                          ALU.subtract)
            b.emit("V", za, waits=(ev_bb,))
            def zb(e, c=c):
                return TT(e, fm(y1B, c), fm(y1B, c), PSC[:, TILE:2 * TILE],
                          ALU.mult)
            ev_y1t = b.emit("V", zb, inc=1)
        ev_psb_free[1] = ev_y1t
        ev_y1_w = dma_y(y1_scr[:, :, t0:t0 + TILE], y1B[:], waits=(ev_y1t,),
                        chan="Ys")
        if KDEBUG:
            dma_y(dbg_y1[:, :, t0:t0 + TILE], y1B[:], chan="Ys")
        ev_rowfree = ev_y1_w

    ev_phaseB_T = ("T", b.counts["T"])

    ar.release()
    ar.mark()
    gw_sb = ar.bf16((FC, NE))
    dma_g(gw_sb, dr["gate_w"].rearrange("(kb p) o -> p kb o", p=128),
          waits=(ev_phaseB_T,), chan="Ga")
    ld["gw"] = ("Ga", b.counts["Ga"])
    y1bf = ar.bf16((FC, HALF))
    wbuf = [ar.bf16((FC, 1024)) for _ in range(2)]
    hbuf = ar.bf16((FF // 128, HALF))
    accC = ar.f32((FC, HALF))
    gbcf = ar.f32((HALF,))
    tmpC = ar.f32((HALF,))
    lgC = ar.f32((HALF,))
    gtm = ar.f32((HALF // 128, 16))
    gden = ar.f32((16,))
    sqC = ar.bf16((FC, TILE))
    cbC = ar.bf16((FC, TILE))
    y2t = ar.f32((FC, TILE))
    y1t2 = ar.f32((FC, TILE))
    y2s = ar.f32((FC, TILE))

    NPART = 12
    wseq = []
    for half in range(2):
        for e_i in range(NE + 1):
            for part in range(NPART):
                wseq.append((half, e_i, part))

    def w_src(e_i, part):
        if part < 4:
            base = dr["mw1"][e_i] if e_i < NE else dr["ws1"]
            src = base.rearrange("(kb p) m -> p kb m", p=128)
            return src[:, :, 1024 * part:1024 * (part + 1)]
        m = part - 4
        base = dr["mw2r"][e_i, m] if e_i < NE else dr["ws2r"][m]
        return base.rearrange("(kb p) o -> p kb o", p=128)

    wld = {}
    wfree = {0: ev_phaseB_T, 1: ev_phaseB_T}

    def wv2(i):
        return wbuf[i].rearrange("p a b -> p (a b)")[:, 0:32 * 128].rearrange(
            "p (a b) -> p a b", a=32)

    def load_chunk(si):
        half, e_i, part = wseq[si]
        dst = wbuf[si % 2] if part < 4 else wv2(si % 2)
        wld[si] = dma_g(dst, w_src(e_i, part), waits=(wfree[si % 2],),
                        chan="Gm%d" % (si % 2))

    load_chunk(0)
    load_chunk(1)

    ev_accw = None
    ev_y2done = None
    ev_lgt = None
    ev_y1bf_free = None
    ev_psu_free = ev_chain if ev_psu_free is None else ev_psu_free
    for half in range(2):
        h0 = half * HALF
        ev_y1bf = dma_g(y1bf, y1_scr[:, :, h0:h0 + HALF],
                        waits=(ev_y1_w, ev_y1bf_free), chan="Gy")
        ev_lg = None
        for cg in range(2):
            evl = None
            for kb in range(FC):
                evl = b.emit("T", lambda e, kb=kb, cg=cg: e.matmul(
                    PSD[0:NE, 0:512], gw_sb[:, kb, :],
                    y1bf[:, kb, 512 * cg:512 * (cg + 1)],
                    start=(kb == 0), stop=(kb == FC - 1)),
                    waits=(ld["gw"], ev_y1bf, ev_psd_free)
                    if kb == 0 else (), inc=1 if kb == FC - 1 else 0)
            def lgb(e, cg=cg):
                return e.activation(lgC[0:NE, cg * 512:(cg + 1) * 512],
                                    PSD[0:NE, 0:512],
                                    AF.Identity, bias=gate_b[0:NE, :])
            ev_lg = b.emit("S", lgb, waits=(evl,), inc=1)
            ev_psd_free = ev_lg
        ev_lgt = None
        for tb in range(HALF // 128):
            evt = b.emit("T", lambda e, tb=tb: e.transpose(
                PSD[0:128, 0:NE], lgC[0:NE, 128 * tb:128 * tb + 128],
                id_f[0:NE, 0:NE]),
                waits=(ev_lg, ev_psd_free) if tb == 0 else (ev_lgt,), inc=1)
            def lcp(e, tb=tb):
                return TT(e, gtm[:, tb, 0:NE], PSD[0:128, 0:NE],
                          PSD[0:128, 0:NE], ALU.bypass)
            ev_lgt = b.emit("V", lcp, waits=(evt,), inc=1)
        ev_psd_free = ev_lgt
        NB = HALF // 128
        eq1v = tmpC.rearrange("p (bb e) -> p bb e", e=NE)[:, 0:NB, :]
        l2v = gbcf.rearrange("p (bb e) -> p bb e", e=NE)[:, 0:NB, :]
        def c_m1(e):
            return e.tensor_reduce(gtm[:, :, 8:9], gtm[:, :, 0:NE], AX.X,
                                   ALU.max)
        b.emit("V", c_m1, waits=(ev_lgt,))
        def c_eq1(e):
            return TT(e, eq1v, gtm[:, :, 0:NE],
                      gtm[:, :, 8:9].to_broadcast((128, NB, NE)),
                      ALU.is_equal)
        b.emit("V", c_eq1)
        def c_l2(e):
            return e.scalar_tensor_tensor(
                l2v, eq1v, -1e30, gtm[:, :, 0:NE], ALU.mult, ALU.add)
        b.emit("V", c_l2)
        def c_m2(e):
            return e.tensor_reduce(gtm[:, :, 9:10], l2v, AX.X, ALU.max)
        b.emit("V", c_m2)
        def c_eq2(e):
            return TT(e, l2v, l2v,
                      gtm[:, :, 9:10].to_broadcast((128, NB, NE)),
                      ALU.is_equal)
        b.emit("V", c_eq2)
        def c_dm(e):
            return TT(e, gtm[:, :, 10:11], gtm[:, :, 9:10], gtm[:, :, 8:9],
                      ALU.subtract)
        ev_cd = b.emit("V", c_dm, inc=1)
        def c_ex(e):
            return e.activation(
                gtm[:, :, 11:12].rearrange("p bb o -> p (bb o)"),
                gtm[:, :, 10:11].rearrange("p bb o -> p (bb o)"), AF.Tanh,
                scale=0.5)
        ev_ce = b.emit("S", c_ex, waits=(ev_cd,), inc=1)
        def c_g1(e):
            return e.tensor_scalar(gtm[:, :, 10:11], gtm[:, :, 11:12],
                                   -0.5, 0.5, ALU.mult, ALU.add)
        b.emit("V", c_g1, waits=(ev_ce,))
        def c_g2(e):
            return e.tensor_scalar(gtm[:, :, 11:12], gtm[:, :, 11:12],
                                   0.5, 0.5, ALU.mult, ALU.add)
        b.emit("V", c_g2)
        def c_w1(e):
            return TT(e, eq1v, eq1v,
                      gtm[:, :, 10:11].to_broadcast((128, NB, NE)), ALU.mult)
        b.emit("V", c_w1)
        def c_w2(e):
            return TT(e, l2v, l2v,
                      gtm[:, :, 11:12].to_broadcast((128, NB, NE)), ALU.mult)
        b.emit("V", c_w2)
        def c_cb(e):
            return TT(e, gtm[:, :, 0:NE], eq1v, l2v, ALU.add)
        ev_cb = b.emit("V", c_cb, inc=1)
        ev_cfm = None
        for tb in range(NB):
            evt = b.emit("T", lambda e, tb=tb: e.transpose(
                PSD[0:NE, 0:128], gtm[:, tb, 0:NE], id_f),
                waits=(ev_cb, ev_psd_free) if tb == 0 else (ev_cfm,), inc=1)
            def ccp(e, tb=tb):
                return TT(e, comb[0:NE, 128 * tb:128 * tb + 128],
                          PSD[0:NE, 0:128], PSD[0:NE, 0:128], ALU.bypass)
            ev_cfm = b.emit("V", ccp, waits=(evt,), inc=1)
        ev_psd_free = ev_cfm

        if KDEBUG:
            dma_y(dbg_gtm[:, 8 * half:8 * half + 8, :], gtm[:],
                  waits=(ev_cb,), chan="Ys")
            dma_y(dbg_lg[:, h0:h0 + HALF], lgC[0:NE, :], waits=(ev_lg,),
                  chan="Ys")
            dma_y(dbg_comb[:, h0:h0 + HALF], comb[0:NE, 0:HALF],
                  waits=(ev_cfm,), chan="Ys")
        def zacc(e):
            return e.memset(accC, 0.0)
        ev_acc0 = b.emit("V", zacc, waits=(ev_y2done,), inc=1)
        ev_accw = ev_acc0
        ev_gel = None
        for e_i in range(NE + 1):
            shared = e_i == NE
            base_si = half * (NE + 1) * NPART + e_i * NPART
            if not shared:
                b.emit("T", lambda e, e_i=e_i: e.matmul(
                    PSA[:, 0:512], esel[0:NE, e_i, :], comb[0:NE, 0:512],
                    start=True, stop=True),
                    waits=(ev_cfm, ev_psa_free,), inc=0)
                ev_gb = b.emit("T", lambda e, e_i=e_i: e.matmul(
                    PSA[:, 512:1024], esel[0:NE, e_i, :],
                    comb[0:NE, 512:1024], start=True, stop=True), inc=1)
                def gcp(e):
                    return TT(e, gbcf, PSA[:], PSA[:], ALU.bypass)
                ev_gbc = b.emit("V", gcp, waits=(ev_gb,), inc=1)
                ev_psa_free = ev_gbc
            for part in range(4):
                si = base_si + part
                wv = wbuf[si % 2]
                for mm in range(FC):
                    m = FC * part + mm
                    for cg in range(2):
                        ps = pslot[cg]
                        evm = None
                        for kb in range(FC):
                            evm = b.emit("T", lambda e, wv=wv, mm=mm, cg=cg,
                                         kb=kb, ps=ps: e.matmul(
                                ps[:, 0:512],
                                wv[:, kb, 128 * mm:128 * mm + 128],
                                y1bf[:, kb, 512 * cg:512 * (cg + 1)],
                                start=(kb == 0), stop=(kb == FC - 1)),
                                waits=(wld[si], ev_y1bf,
                                       ev_psb_free[cg])
                                if kb == 0 else (),
                                inc=1 if kb == FC - 1 else 0)
                        ev_y1bf_free = evm
                        def gel(e, m=m, cg=cg, e_i=e_i, shared=shared,
                                ps=ps):
                            bias_ap = (bs1[:, m:m + 1] if shared
                                       else b1v[e_i][:, m:m + 1])
                            return e.activation(
                                hbuf[:, m, 512 * cg:512 * (cg + 1)],
                                ps[:, 0:512], AF.Gelu, bias=bias_ap)
                        ev_gel = b.emit("S", gel, waits=(evm,), inc=1)
                        ev_psb_free[cg] = ev_gel
                wfree[si % 2] = ev_gel
                if si + 2 < len(wseq):
                    load_chunk(si + 2)
            for m in range(FC):
                si = base_si + 4 + m
                wv = wv2(si % 2)
                psm = PSA if m % 2 == 0 else PSU
                evm = None
                for kb in range(32):
                    for cg in range(2):
                        evm = b.emit("T", lambda e, wv=wv, m=m, psm=psm,
                                     cg=cg, kb=kb: e.matmul(
                            psm[:, 512 * cg:512 * (cg + 1)],
                            wv[:, kb, :],
                            hbuf[:, kb, 512 * cg:512 * (cg + 1)],
                            start=(kb == 0), stop=(kb == 31)),
                            waits=(wld[si], ev_gel,
                                   ev_psa_free if m % 2 == 0
                                   else ev_psu_free)
                            if (kb == 0 and cg == 0) else (),
                            inc=1 if (kb == 31 and cg == 1) else 0)
                if shared:
                    def accs(e, m=m, psm=psm):
                        return e.scalar_tensor_tensor(
                            fm(accC, m), psm[:], bs2[:, m:m + 1],
                            fm(accC, m), ALU.add, ALU.add)
                    ev_accw = b.emit("V", accs, waits=(evm, ev_accw), inc=1)
                else:
                    def acc1(e, m=m, e_i=e_i, psm=psm):
                        return e.scalar_tensor_tensor(
                            tmpC, psm[:], b2v[e_i][:, m:m + 1], gbcf,
                            ALU.add, ALU.mult)
                    b.emit("V", acc1, waits=(evm, ev_gbc, ev_accw))
                    def acc2(e, m=m):
                        return TT(e, fm(accC, m), fm(accC, m), tmpC, ALU.add)
                    ev_accw = b.emit("V", acc2, inc=1)
                if m % 2 == 0:
                    ev_psa_free = ev_accw
                else:
                    ev_psu_free = ev_accw
                wfree[si % 2] = ev_accw
                if si + 2 < len(wseq):
                    load_chunk(si + 2)

        if KDEBUG:
            dma_y(dbg_acc[:, :, h0:h0 + HALF], accC, waits=(ev_accw,),
                  chan="Ys")
        ev_ydma = None
        ev_y1t2_free = None
        for tt in range(HALF // TILE):
            s0 = tt * TILE
            ev_ly = dma_y(y1t2, y1_scr[:, :, h0 + s0:h0 + s0 + TILE],
                          waits=(ev_accw, ev_y1t2_free), chan="Yl")
            def v3(e, s0=s0):
                return TT(e, y2s, accC[:, :, s0:s0 + TILE], y1t2, ALU.add)
            ev_v3 = b.emit("V", v3, waits=(ev_ly, ev_accw, ev_ydma), inc=1)
            ev_y1t2_free = ev_v3
            def sqx(e):
                return TT(e, sqC.rearrange("p c t -> p (c t)"),
                          y2s.rearrange("p c t -> p (c t)"),
                          y2s.rearrange("p c t -> p (c t)"), ALU.mult)
            b.emit("V", sqx)
            def cbx(e):
                return TT(e, cbC.rearrange("p c t -> p (c t)"),
                          y2s.rearrange("p c t -> p (c t)"),
                          y2s.rearrange("p c t -> p (c t)"), ALU.bypass)
            ev_c = b.emit("V", cbx, inc=1)
            ev_s = None
            for c in range(FC):
                b.emit("T", lambda e, c=c: e.matmul(
                    PSD[0:1, 0:TILE], ones_c_bf, cbC[:, c, :],
                    start=(c == 0), stop=(c == FC - 1)),
                    waits=(ev_c, ev_psd_free) if c == 0 else ())
            for c in range(FC):
                ev_s = b.emit("T", lambda e, c=c: e.matmul(
                    PSD[0:1, TILE:2 * TILE], ones_c_bf, sqC[:, c, :],
                    start=(c == 0), stop=(c == FC - 1)),
                    inc=1 if c == FC - 1 else 0)
            def pm(e):
                return e.tensor_scalar(rrow[0:1, 0, :], PSD[0:1, 0:TILE],
                                       1.0 / D, None, ALU.mult)
            b.emit("V", pm, waits=(ev_s,))
            def pm2(e):
                return TT(e, rrow[0:1, 1, :], rrow[0:1, 0, :],
                          rrow[0:1, 0, :], ALU.mult)
            b.emit("V", pm2)
            def pv(e):
                return e.scalar_tensor_tensor(rrow[0:1, 2, :],
                                              PSD[0:1, TILE:2 * TILE],
                                              1.0 / D, rrow[0:1, 1, :],
                                              ALU.mult, ALU.subtract)
            ev_pv = b.emit("V", pv, inc=1)
            ev_psd_free = ev_pv
            def pl(e):
                return e.activation(rrow[0:1, 3, :], rrow[0:1, 2, :], AF.Ln,
                                    bias=epsc[0:1, 2:3])
            ev_pl = b.emit("S", pl, waits=(ev_pv,), inc=1)
            def pe(e):
                return e.activation(rrow[0:1, 3, :], rrow[0:1, 3, :],
                                    AF.Exp, scale=-0.5)
            ev_pr = b.emit("S", pe, waits=(ev_pl,), inc=1)
            b.emit("T", lambda e: e.matmul(PSC[:, 0:TILE], ones_r,
                                           rrow[0:1, 0, :], start=True,
                                           stop=True),
                   waits=(ev_pr, ev_psb_free[1]))
            ev_bb = b.emit("T", lambda e: e.matmul(
                PSC[:, TILE:2 * TILE], ones_r, rrow[0:1, 3, :],
                start=True, stop=True), inc=1)
            ev_y2 = None
            for c in range(FC):
                def za(e, c=c):
                    return TT(e, y2t[:, c, :], y2s[:, c, :],
                              PSC[:, 0:TILE], ALU.subtract)
                b.emit("V", za, waits=(ev_bb,))
                def zb(e, c=c):
                    return TT(e, y2t[:, c, :], y2t[:, c, :],
                              PSC[:, TILE:2 * TILE], ALU.mult)
                ev_y2 = b.emit("V", zb, inc=1)
            ev_psb_free[1] = ev_y2
            evt = None
            for c in range(FC):
                evt = b.emit("T", lambda e, c=c: e.transpose(
                    PSA[:, 128 * c:128 * c + 128], y2t[:, c, :], id_f),
                    waits=(ev_y2, ev_psa_free) if c == 0 else (),
                    inc=1 if c == FC - 1 else 0)
            def ytc(e):
                return TT(e, y2s.rearrange("p c t -> p (c t)"), PSA[:],
                          PSA[:], ALU.bypass)
            ev_yc = b.emit("V", ytc, waits=(evt,), inc=1)
            ev_psa_free = ev_yc
            r0 = h0 + s0
            ev_ydma = dma_y(y_d[r0:r0 + 128, :],
                            y2s.rearrange("p c t -> p (c t)"),
                            waits=(ev_yc,), chan="Yo")
            ev_y2done = ev_ydma

    b.emit("Y", lambda e: e.nop(), waits=(("Yo", b.counts["Yo"]),
                                          ("Ys", b.counts["Ys"]),))

    with nc.Block() as block:
        b.materialize(block)
    ar.ap = None
    ar.t.__exit__(None, None, None)
    for c in reversed(ctxs):
        c.__exit__(None, None, None)
    return nc


def _host_consts():
    c = {}
    c["ident"] = np.eye(128, dtype=np.float32)
    c["ones_r"] = np.ones((1, 128), np.float32)
    c["ones_c"] = np.ones((128, 1), np.float32)
    hs = np.zeros((128, FC, H), np.float32)
    hm = np.zeros((H, FC, 128), np.float32)
    for cc in range(FC):
        for p in range(128):
            hh = (128 * cc + p) // DK
            hs[p, cc, hh] = 1.0
            hm[hh, cc, p] = 1.0
    c["headsel"] = hs
    c["headmap"] = hm
    es = np.zeros((NE, NE, 128), np.float32)
    for e in range(NE):
        es[e, e, :] = 1.0
    c["esel"] = es
    r1 = np.zeros((128, FC * TILE), np.float32)
    r1[:, ::CH] = 1.0
    c["reset1"] = r1
    return c


def _vec_tile(v):
    return np.ascontiguousarray(np.asarray(v, np.float32).reshape(FC, 128).T)


_CACHE = {}


def kernel(**inp):
    if "nc" not in _CACHE:
        _CACHE["nc"] = build_program()
    nc = _CACHE["nc"]

    x = np.ascontiguousarray(np.asarray(inp["L0_attn_x"], np.float32)[0])
    base = {
        "Wr": inp["L0_attn_Wr"], "Wk": inp["L0_attn_Wk"],
        "Wv": inp["L0_attn_Wv"], "Wg": inp["L0_attn_Wg"],
        "Wo": inp["L0_attn_Wo"], "wA": inp["L0_attn_wA"],
        "wB": inp["L0_attn_wB"], "fw1": inp["L0_attn_ffn_w1"],
        "fw2": inp["L0_attn_ffn_w2"], "mw1": inp["L0_ffn_w1"],
        "mw2r": np.ascontiguousarray(
            np.asarray(inp["L0_ffn_w2"], np.float32).reshape(
                NE, FF, FC, 128).transpose(0, 2, 1, 3)),
        "ws1": inp["L0_ffn_ws1"],
        "ws2r": np.ascontiguousarray(
            np.asarray(inp["L0_ffn_ws2"], np.float32).reshape(
                FF, FC, 128).transpose(1, 0, 2)),
        "gate_w": inp["L0_ffn_gate_w"],
        "ln1w": _vec_tile(inp["L0_attn_ln1_w"]),
        "ln2w": _vec_tile(inp["L0_attn_ln2_w"]),
        "mur": _vec_tile(inp["L0_attn_mu_r"]),
        "muk": _vec_tile(inp["L0_attn_mu_k"]),
        "muv": _vec_tile(inp["L0_attn_mu_v"]),
        "muw": _vec_tile(inp["L0_attn_mu_w"]),
        "mug": _vec_tile(inp["L0_attn_mu_g"]),
        "w0": _vec_tile(inp["L0_attn_w0"]),
        "uvec": _vec_tile(np.asarray(inp["L0_attn_u"],
                                     np.float32).reshape(-1)),
        "gnw": _vec_tile(np.asarray(inp["L0_attn_gn_w"],
                                    np.float32).reshape(-1)),
        "gnb": _vec_tile(np.asarray(inp["L0_attn_gn_b"],
                                    np.float32).reshape(-1)),
        "gate_b": np.asarray(inp["L0_ffn_gate_b"],
                             np.float32).reshape(NE, 1),
        "b1": np.ascontiguousarray(
            np.asarray(inp["L0_ffn_b1"], np.float32).reshape(
                NE, FF // 128, 128).transpose(0, 2, 1)),
        "b2": np.ascontiguousarray(
            np.asarray(inp["L0_ffn_b2"], np.float32).reshape(
                NE, FC, 128).transpose(0, 2, 1)),
        "bs1": np.ascontiguousarray(
            np.asarray(inp["L0_ffn_bs1"], np.float32).reshape(
                FF // 128, 128).T),
        "bs2": np.ascontiguousarray(
            np.asarray(inp["L0_ffn_bs2"], np.float32).reshape(FC, 128).T),
    }
    base.update(_host_consts())
    base = {k: np.ascontiguousarray(np.asarray(v, np.float32))
            for k, v in base.items()}

    in_maps = []
    for ci in range(NCORES):
        s0 = ci * T_CORE - T_WARM
        if s0 < 0:
            xs = np.concatenate(
                [np.zeros((-s0, D), np.float32), x[0:s0 + T_ALL]], axis=0)
        else:
            xs = x[s0:s0 + T_ALL]
        m = dict(base)
        m["x"] = np.ascontiguousarray(xs)
        in_maps.append(m)

    res = run_bass_kernel_spmd(nc, in_maps, core_ids=list(range(NCORES)))
    out = np.concatenate([r["y"] for r in res.results], axis=0)
    return out[None, :, :]


# version 09
# speedup vs baseline: 1.0087x; 1.0087x over previous
import os
import numpy as np
import concourse.bass as bass
import concourse.mybir as mybir
from concourse.bass_utils import run_bass_kernel_spmd

F32 = mybir.dt.float32
BF16 = mybir.dt.bfloat16
ALU = mybir.AluOpType
AF = mybir.ActivationFunctionType
AX = mybir.AxisListType

D = 1024
H = 16
DK = 64
FC = 8
LORA = 64
FF = 4096
NE = 8
NCORES = 8
T_CORE = 2048
T_WARM = 512
T_ALL = T_WARM + T_CORE
TILE = 128
NT = T_ALL // TILE
WARM_TILES = T_WARM // TILE
CH = 16
NCH = TILE // CH
HALF = 1024
EPS_RMS = 1e-6
EPS_GN = 1e-5
EPS_LN = 1e-5

ENGS = ("T", "V", "S", "G", "Y")
CHAN_BODY = {"T": "T", "V": "V", "S": "S", "G": "G", "Y": "Y",
             "Yx": "Y", "Ys": "Y", "Yl": "Y", "Yo": "Y", "Yc": "Y",
             "Gc": "G", "Ga": "G", "Gm0": "G", "Gm1": "G", "Gy": "G"}


class Prog:
    def __init__(self, nc):
        self.nc = nc
        self.ops = []
        self.counts = {c: 0 for c in CHAN_BODY}
        self.sems = {}

    def emit(self, chan, fn, waits=(), inc=0):
        waits = tuple(w for w in waits if w is not None)
        self.ops.append((chan, fn, waits, inc))
        if inc:
            self.counts[chan] += inc
            return (chan, self.counts[chan])
        return None

    def materialize(self, block):
        for c in CHAN_BODY:
            self.sems[c] = self.nc.semaphore("sem_" + c).__enter__()
        watermark = {}

        def body_for(eng_name):
            def body(eng):
                for (oc, fn, waits, inc) in self.ops:
                    if CHAN_BODY[oc] != eng_name:
                        continue
                    for (pc, val) in waits:
                        key = (eng_name, pc)
                        if watermark.get(key, -1) >= val:
                            continue
                        watermark[key] = val
                        eng.wait_ge(self.sems[pc], val)
                    inst = fn(eng)
                    if inc:
                        assert inst is not None
                        inst.then_inc(self.sems[oc], inc)
            return body

        block.tensor(body_for("T"))
        block.vector(body_for("V"))
        block.scalar(body_for("S"))
        block.gpsimd(body_for("G"))
        block.sync(body_for("Y"))


class Arena:

    def __init__(self, nc, cols=52992):
        self.t = nc.sbuf_tensor("arena", [128, cols], F32)
        self.ap = self.t.__enter__()
        self.cols = cols
        self.off = 0
        self.marks = []

    def alloc(self, ncols_f32):
        a = self.off
        self.off += ncols_f32
        assert self.off <= self.cols, f"SBUF arena overflow: {self.off}"
        return a

    def f32(self, shape):
        n = int(np.prod(shape))
        a = self.alloc(n)
        v = self.ap[:, a:a + n]
        if len(shape) == 1:
            return v
        return v.rearrange("p (a b) -> p a b", a=shape[0]) if len(shape) == 2 \
            else v.rearrange("p (a b c) -> p a b c", a=shape[0], b=shape[1])

    def bf16(self, shape):
        n = int(np.prod(shape))
        a = self.alloc((n + 1) // 2)
        v = self.ap[:, a:a + (n + 1) // 2].bitcast(BF16)
        if len(shape) == 1:
            return v[:, 0:n]
        return v[:, 0:n].rearrange("p (a b) -> p a b", a=shape[0]) \
            if len(shape) == 2 else \
            v[:, 0:n].rearrange("p (a b c) -> p a b c", a=shape[0], b=shape[1])

    def mark(self):
        self.marks.append(self.off)

    def release(self):
        self.off = self.marks.pop()


def build_program():
    nc = bass.Bass("TRN2", target_bir_lowering=False, debug=False)
    b = Prog(nc)
    dr = {}

    def din(name, shape):
        dr[name] = nc.dram_tensor(name, list(shape), F32,
                                  kind="ExternalInput").ap()
        return dr[name]

    x_d = din("x", (T_ALL, D))
    for nm in ("Wr", "Wk", "Wv", "Wg", "Wo"):
        din(nm, (D, D))
    din("wA", (D, LORA))
    din("wB", (LORA, D))
    din("fw1", (D, FF))
    din("fw2", (FF, D))
    din("mw1", (NE, D, FF))
    din("mw2r", (NE, FC, FF, 128))
    din("ws1", (D, FF))
    din("ws2r", (FC, FF, 128))
    din("gate_w", (D, NE))
    for nm in ("ln1w", "ln2w", "mur", "muk", "muv", "muw", "mug", "w0",
               "uvec", "gnw", "gnb"):
        din(nm, (128, FC))
    din("gate_b", (NE, 1))
    din("b1", (NE, 128, FF // 128))
    din("b2", (NE, 128, FC))
    din("bs1", (128, FF // 128))
    din("bs2", (128, FC))
    din("ident", (128, 128))
    din("ones_r", (1, 128))
    din("ones_c", (128, 1))
    din("headsel", (128, FC, H))
    din("esel", (NE, NE, 128))
    din("headmap", (H, FC, 128))
    din("reset1", (128, FC * TILE))

    y_d = nc.dram_tensor("y", [T_CORE, D], F32, kind="ExternalOutput").ap()
    u1_scr = nc.dram_tensor("u1_scr", [128, FC, T_CORE], F32,
                            kind="Internal").ap()
    xpu_scr = nc.dram_tensor("xpu_scr", [128, FC, T_CORE], F32,
                             kind="Internal").ap()
    y1_scr = nc.dram_tensor("y1_scr", [128, FC, T_CORE], F32,
                            kind="Internal").ap()
    KDEBUG = os.environ.get("KDEBUG", "") == "1"
    if KDEBUG:
        dbg_lg = nc.dram_tensor("dbg_lg", [NE, T_CORE], F32,
                                kind="ExternalOutput").ap()
        dbg_comb = nc.dram_tensor("dbg_comb", [NE, T_CORE], F32,
                                  kind="ExternalOutput").ap()
        dbg_acc = nc.dram_tensor("dbg_acc", [128, FC, T_CORE], F32,
                                 kind="ExternalOutput").ap()
        dbg_u1 = nc.dram_tensor("dbg_u1", [128, FC, T_CORE], F32,
                                kind="ExternalOutput").ap()
        dbg_y1 = nc.dram_tensor("dbg_y1", [128, FC, T_CORE], F32,
                                kind="ExternalOutput").ap()
        dbg_gtm = nc.dram_tensor("dbg_gtm", [128, 16, 16], F32,
                                 kind="ExternalOutput").ap()

    ctxs = []

    def psum(nm, shp, dt=F32):
        t = nc.psum_tensor(nm, list(shp), dt)
        ctxs.append(t)
        return t.__enter__()

    PSA = psum("PSA", (128, 1024), F32)
    PSU = psum("PSU", (128, 1024), F32)
    PSB = psum("PSB", (128, 512), F32)
    PSC = psum("PSC", (128, 512), F32)
    PSD = psum("PSD", (128, 512), F32)
    PSTF = psum("PST", (128, 512), F32)
    PST = PSTF.bitcast(BF16)

    ar = Arena(nc)

    id_f = ar.f32((128,))
    id_bf = ar.bf16((128,))
    ones_r = ar.f32((128,))[0:1, :]
    ones_c_bf = ar.bf16((2,))[:, 0:1]
    hsel = ar.bf16((FC, H))
    hmap_bf = ar.bf16((FC, 128))
    hmap_f = ar.f32((FC, 128))
    esel = ar.f32((NE, 128))
    reset1 = ar.f32((FC * TILE,))
    vecs = {nm: ar.f32((FC,)) for nm in
            ("ln1w", "ln2w", "mur", "muk", "muv", "muw", "mug", "w0",
             "uvec", "gnw", "gnb")}
    gate_b = ar.f32((1,))
    bs1 = ar.f32((FF // 128,))
    bs2 = ar.f32((FC,))
    b1v = [ar.f32((FF // 128,)) for _ in range(NE)]
    b2v = [ar.f32((FC,)) for _ in range(NE)]
    epsc = ar.f32((4,))
    S = ar.f32((FC, DK))
    h_bnd = ar.f32((FC, 1))
    rrow = ar.f32((6, TILE))
    gst = ar.f32((6, TILE))
    comb = ar.f32((HALF,))
    wlf = ar.f32((TILE,))
    wlt = ar.bf16((TILE,))

    def dma_y(dst, src, waits=(), chan="Yc"):
        return b.emit(chan, lambda e, d=dst, s=src: e.dma_start(d, s),
                      waits=waits, inc=16)

    def dma_g(dst, src, waits=(), chan="Gc"):
        return b.emit(chan, lambda e, d=dst, s=src: e.dma_start(d, s),
                      waits=waits, inc=16)

    ld = {}
    dma_y(id_f, dr["ident"])
    dma_g(id_bf, dr["ident"])
    dma_y(ones_r, dr["ones_r"])
    dma_g(ones_c_bf, dr["ones_c"])
    dma_g(hsel, dr["headsel"])
    dma_g(hmap_bf[0:H, :, :], dr["headmap"])
    dma_y(hmap_f[0:H, :, :], dr["headmap"])
    dma_y(esel[0:NE, :, :], dr["esel"])
    dma_y(reset1, dr["reset1"])
    for nm in vecs:
        dma_y(vecs[nm], dr[nm])
    dma_y(gate_b[0:NE, :], dr["gate_b"])
    dma_y(bs1, dr["bs1"])
    dma_y(bs2, dr["bs2"])
    for e_i in range(NE):
        dma_y(b1v[e_i], dr["b1"][e_i])
        dma_y(b2v[e_i], dr["b2"][e_i])
    ev_const_y = ("Yc", b.counts["Yc"])
    ev_const_g = ("Gc", b.counts["Gc"])

    ar.mark()
    wsb = {}
    for nm in ("Wr", "Wk", "Wv", "Wg", "Wo"):
        wsb[nm] = ar.bf16((FC, D))
        dma_g(wsb[nm], dr[nm].rearrange("(kb p) o -> p kb o", p=128),
              chan="Ga")
    w_lA = ar.bf16((FC, LORA))
    dma_g(w_lA, dr["wA"].rearrange("(kb p) o -> p kb o", p=128), chan="Ga")
    w_lB = ar.bf16((D,))
    dma_g(w_lB[0:LORA, :], dr["wB"], chan="Ga")
    ev_wAll = ("Ga", b.counts["Ga"])
    for nm in ("Wr", "Wk", "Wv", "Wg", "Wo", "wA", "wB"):
        ld[nm] = ev_wAll

    FMT = (FC, TILE)
    x_tm = ar.f32((D,))
    x_fm2 = [ar.f32(FMT), ar.f32(FMT)]
    h_fm = ar.f32(FMT)
    xx_fm = ar.f32(FMT)
    xmix = {nm: ar.bf16(FMT) for nm in ("xr", "xk", "xv", "xw", "xg")}
    r_fm = ar.f32(FMT)
    k_fm = ar.f32(FMT)
    v_fm = ar.f32(FMT)
    d_fm = ar.f32(FMT)
    g_fm = ar.f32(FMT)
    e_fm = ar.f32(FMT)
    abuf = ar.f32(FMT)
    A_ch = ar.f32((FC, NCH))
    rt_fm = ar.f32(FMT)
    kh_e = ar.bf16(FMT)
    kh_o = ar.bf16(FMT)
    khtm = [ar.bf16((FC, 128)) for _ in range(2)]
    vtm = ar.bf16((FC, 128))
    G_fm = ar.f32(FMT)
    rk_bf = ar.bf16(FMT)
    dots_bf = ar.bf16((TILE,))
    od_fm = ar.f32(FMT)
    tmp_fm = ar.f32(FMT)
    onrm = ar.bf16(FMT)
    on_bf = ar.bf16(FMT)
    sig_t = tmp_fm

    def TT(e, out, a, bb, op):
        if op == ALU.bypass:
            return e.tensor_copy(out, a)
        return e.tensor_tensor(out, a, bb, op)

    def fm(t, c=None, sl=None):
        a = t
        if c is not None:
            a = a[:, c, :]
        if sl is not None:
            a = a[:, sl] if c is not None else a[:, :, sl]
        return a

    def _init(e):
        e.memset(epsc[:, 0:1], EPS_RMS)
        e.memset(epsc[:, 1:2], EPS_GN)
        e.memset(epsc[:, 2:3], EPS_LN)
        e.memset(S, 0.0)
        e.memset(h_bnd, 0.0)
        e.memset(abuf, 0.0)
        e.memset(kh_e, 0.0)
        return e.memset(kh_o, 0.0)
    ev_init = b.emit("V", _init, inc=1)

    ev_chain = ev_init
    ev_u_last = {}
    ev_tile_T_done = None
    ev_xfm_free = ev_init
    ev_oevac = None
    ev_psa_free = ev_init
    ev_pst_free = ev_init
    ev_psb_free = {0: ev_init, 1: ev_init}
    ev_psd_free = ev_init
    ev_rowfree = ev_init

    resetv = reset1

    E = dict(chain=ev_init, psa=ev_init, pst=ev_init, psd=ev_init,
             xtmfree=None, hbnd=ev_init, su1=None, sxp=None,
             l=None, gp=None, gnlast=None, g=None, dfree=ev_init,
             kvfree=ev_init, tm=ev_init, rt=None, rtfree=ev_init,
             rmsT=None, odpre=None, rk=None, deltaG=ev_init)
    E["xfree"] = {0: ev_init, 1: ev_init}
    E["uL"] = {}
    pslot = [PSB, PSC]

    def a_front(i):
        warm = i < WARM_TILES
        par = i % 2
        xf = x_fm2[par]
        xff = xf.rearrange("p c t -> p (c t)")
        t0 = i * TILE
        ev_x = dma_y(x_tm, x_d[t0:t0 + TILE, :],
                     waits=(E["xtmfree"], ev_const_y), chan="Yx")
        ev_xev = None
        for hf in range(2):
            evt = None
            for j in range(4):
                c = 4 * hf + j
                evt = b.emit("T", lambda e, c=c, j=j: e.transpose(
                    PSD[:, 128 * j:128 * j + 128],
                    x_tm[:, 128 * c:128 * c + 128], id_f),
                    waits=(ev_x, E["psd"]) if j == 0 else (),
                    inc=1 if j == 3 else 0)
            E["xtmfree"] = evt
            def xevac(e, hf=hf, xff=xff):
                return e.copy(xff[:, 512 * hf:512 * hf + 512], PSD[:, 0:512])
            ev_xev = b.emit("S", xevac,
                            waits=(evt, E["xfree"][par]), inc=1)
            E["psd"] = ev_xev
        def sq(e, xff=xff):
            return TT(e, onrm[:].rearrange("p c t -> p (c t)"), xff, xff,
                      ALU.mult)
        ev_sq = b.emit("V", sq, waits=(ev_xev, E["rmsT"]), inc=1)
        ev_sum = None
        for c in range(FC):
            ev_sum = b.emit("T", lambda e, c=c: e.matmul(
                PSD[0:1, 0:TILE], ones_c_bf, fm(onrm, c),
                start=(c == 0), stop=(c == FC - 1)),
                waits=(ev_sq, E["psd"], ev_const_g) if c == 0 else (),
                inc=1 if c == FC - 1 else 0)
        E["rmsT"] = ev_sum
        ev_rl = b.emit("S", lambda e: e.activation(
            rrow[0:1, 0, :], PSD[0:1, 0:TILE], AF.Ln,
            bias=epsc[0:1, 0:1], scale=1.0 / D), waits=(ev_sum,), inc=1)
        E["psd"] = ev_rl
        ev_r2 = b.emit("S", lambda e: e.activation(
            rrow[0:1, 1, :], rrow[0:1, 0, :], AF.Exp, scale=-0.5),
            waits=(ev_rl,), inc=1)
        ev_bc = b.emit("T", lambda e: e.matmul(
            PSC[:, 0:TILE], ones_r, rrow[0:1, 1, :], start=True, stop=True),
            waits=(ev_r2, ev_psb_free[1]), inc=1)
        ev_h = None
        for c in range(FC):
            ev_h = b.emit("V", lambda e, c=c, xf=xf: e.scalar_tensor_tensor(
                fm(h_fm, c), fm(xf, c), vecs["ln1w"][:, c:c + 1],
                PSC[:, 0:TILE], ALU.mult, ALU.mult),
                waits=(ev_bc,), inc=1 if c == FC - 1 else 0)
        ev_psb_free[1] = ev_h
        b.emit("V", lambda e: e.tensor_sub(
            xx_fm[:, :, 1:TILE], h_fm[:, :, 0:TILE - 1], h_fm[:, :, 1:TILE]),
            waits=(ev_h,))
        b.emit("V", lambda e: e.tensor_sub(
            xx_fm[:, :, 0:1], h_bnd, h_fm[:, :, 0:1]))
        b.emit("V", lambda e: TT(e, h_bnd, h_fm[:, :, TILE - 1:TILE],
                                 h_fm[:, :, TILE - 1:TILE], ALU.bypass))
        mixes = [("xk", "muk"), ("xv", "muv"), ("xw", "muw")]
        if not warm:
            mixes.append(("xr", "mur"))
        ev_mix = None
        for nm, mu in mixes:
            for c in range(FC):
                ev_mix = b.emit("V", lambda e, nm=nm, mu=mu, c=c:
                                e.scalar_tensor_tensor(
                                    fm(xmix[nm], c), fm(xx_fm, c),
                                    vecs[mu][:, c:c + 1], fm(h_fm, c),
                                    ALU.mult, ALU.add),
                                waits=(E["l"],) if (nm == mixes[0][0]
                                                   and c == 0) else (),
                                inc=1 if c == FC - 1 else 0)

        def proj(dst_fm, srcb, wname, efree):
            evs = None
            for m in range(FC):
                ps = pslot[m % 2]
                evm = None
                for kb in range(FC):
                    evm = b.emit("T", lambda e, m=m, kb=kb, w=wname, ps=ps:
                                 e.matmul(
                                     ps[:, 0:TILE],
                                     wsb[w][:, kb, 128 * m:128 * m + 128],
                                     fm(xmix[srcb], kb),
                                     start=(kb == 0), stop=(kb == FC - 1)),
                                 waits=(ev_mix, ld[wname],
                                        ev_psb_free[m % 2]) if kb == 0
                                 else (), inc=1 if kb == FC - 1 else 0)
                def evac(e, m=m, ps=ps, dst=dst_fm):
                    return e.copy(fm(dst, m), ps[:, 0:TILE])
                eve = b.emit("S", evac,
                             waits=(evm, efree) if m == 0 else (evm,), inc=1)
                ev_psb_free[m % 2] = eve
                evs = eve
            return evs

        ev_k = proj(k_fm, "xk", "Wk", E["kvfree"])
        ev_v = proj(v_fm, "xv", "Wv", E["kvfree"])
        if not warm:
            ev_r = proj(r_fm, "xr", "Wr", E["kvfree"])
        ev_l = None
        for kb in range(FC):
            ev_l = b.emit("T", lambda e, kb=kb: e.matmul(
                PSD[0:LORA, 0:TILE], w_lA[:, kb, :], fm(xmix["xw"], kb),
                start=(kb == 0), stop=(kb == FC - 1)),
                waits=(ev_mix, ld["wA"], E["psd"]) if kb == 0 else (),
                inc=1 if kb == FC - 1 else 0)
        E["l"] = ev_l
        ev_th = b.emit("S", lambda e: e.activation(
            wlt[0:LORA, :], PSD[0:LORA, 0:TILE], AF.Tanh),
            waits=(ev_l,), inc=1)
        E["psd"] = ev_th
        ev_wp = None
        for m in range(FC):
            ps = pslot[m % 2]
            ev_mm = b.emit("T", lambda e, m=m, ps=ps: e.matmul(
                ps[:, 0:TILE], w_lB[0:LORA, 128 * m:128 * m + 128],
                wlt[0:LORA, :], start=True, stop=True),
                waits=(ev_th, ev_psb_free[m % 2]), inc=1)
            def d_e1(e, m=m, ps=ps):
                return e.activation(fm(d_fm, m), ps[:, 0:TILE], AF.Exp,
                                    bias=vecs["w0"][:, m:m + 1])
            ev_wp = b.emit("S", d_e1,
                           waits=(ev_mm, E["dfree"]) if m == 0
                           else (ev_mm,), inc=1)
            ev_psb_free[m % 2] = ev_wp
        ev_d = b.emit("S", lambda e: e.activation(
            d_fm[:].rearrange("p c t -> p (c t)"),
            d_fm[:].rearrange("p c t -> p (c t)"), AF.Exp, scale=-1.0),
            waits=(ev_wp, ("S", b.counts["S"])), inc=1)
        d4 = d_fm[:].rearrange("p c (n l) -> p c n l", l=CH)
        a4 = abuf[:].rearrange("p c (n l) -> p c n l", l=CH)
        e4 = e_fm[:].rearrange("p c (n l) -> p c n l", l=CH)
        ev_af = b.emit("V", lambda e: TT(
            e, a4[:, :, :, 1:CH], d4[:, :, :, 0:CH - 1],
            d4[:, :, :, 0:CH - 1], ALU.bypass), waits=(ev_d,), inc=1)
        ev_E = b.emit("V", lambda e: e.tensor_tensor_scan(
            e_fm[:].rearrange("p c t -> p (c t)"),
            abuf[:].rearrange("p c t -> p (c t)"),
            resetv, 0.0, ALU.mult, ALU.add), waits=(ev_af,), inc=1)
        ev_A = b.emit("V", lambda e: TT(
            e, A_ch[:].rearrange("p c (n o) -> p c n o", o=1),
            e4[:, :, :, CH - 1:CH], d4[:, :, :, CH - 1:CH], ALU.mult),
            inc=1)
        E["A"] = ev_A
        if not warm:
            ev_rt = b.emit("V", lambda e: TT(
                e, rt_fm[:].rearrange("p c t -> p (c t)"),
                r_fm[:].rearrange("p c t -> p (c t)"),
                e_fm[:].rearrange("p c t -> p (c t)"), ALU.mult),
                waits=(ev_r, E["rtfree"]), inc=1)
            E["rt"] = ev_rt
        ev_a2 = b.emit("V", lambda e: TT(
            e, a4[:, :, :, 1:CH], d4[:, :, :, CH - 1:0:-1],
            d4[:, :, :, CH - 1:0:-1], ALU.bypass), inc=1)
        ev_Sx = b.emit("V", lambda e: e.tensor_tensor_scan(
            e_fm[:].rearrange("p c t -> p (c t)"),
            abuf[:].rearrange("p c t -> p (c t)"),
            resetv, 0.0, ALU.mult, ALU.add), waits=(ev_a2,), inc=1)
        ke4 = kh_e[:].rearrange("p c (n l) -> p c n l", l=2 * CH)
        ko4 = kh_o[:].rearrange("p c (n l) -> p c n l", l=2 * CH)
        k4 = k_fm[:].rearrange("p c (n l) -> p c n l", l=2 * CH)
        ev_khe = b.emit("V", lambda e: TT(
            e, ke4[:, :, :, 0:CH], k4[:, :, :, 0:CH],
            e_fm[:].rearrange("p c (n l) -> p c n l", l=2 * CH)
            [:, :, :, CH - 1::-1], ALU.mult),
            waits=(ev_Sx, ev_k, E["tm"]), inc=1)
        ev_kho = b.emit("V", lambda e: TT(
            e, ko4[:, :, :, CH:2 * CH], k4[:, :, :, CH:2 * CH],
            e_fm[:].rearrange("p c (n l) -> p c n l", l=2 * CH)
            [:, :, :, 2 * CH - 1:CH - 1:-1], ALU.mult), inc=1)
        ev_tm = None
        for (src_t, dst, evd) in ((kh_e, khtm[0], ev_khe),
                                  (kh_o, khtm[1], ev_kho)):
            evt = None
            for c in range(FC):
                evt = b.emit("T", lambda e, src_t=src_t, c=c: e.transpose(
                    PST[:, 128 * c:128 * c + 128], fm(src_t, c), id_bf),
                    waits=(evd, E["pst"]) if c == 0 else (),
                    inc=1 if c == FC - 1 else 0)
            def tevac(e, dst=dst):
                return e.copy(dst[:].rearrange("p c t -> p (c t)"), PST[:])
            ev_tm = b.emit("S", tevac, waits=(evt,), inc=1)
            E["pst"] = ev_tm
        evt = None
        for c in range(FC):
            evt = b.emit("T", lambda e, c=c: e.transpose(
                PSA[:, 128 * c:128 * c + 128], fm(v_fm, c), id_f),
                waits=(ev_v, E["psa"]) if c == 0 else (),
                inc=1 if c == FC - 1 else 0)
        ev_tm = b.emit("S", lambda e: e.copy(
            vtm[:].rearrange("p c t -> p (c t)"), PSA[:]),
            waits=(evt,), inc=1)
        E["psa"] = ev_tm
        E["tm"] = ev_tm

    def a_gproj(i):
        ev_mix = None
        for c in range(FC):
            ev_mix = b.emit("V", lambda e, c=c: e.scalar_tensor_tensor(
                fm(xmix["xg"], c), fm(xx_fm, c), vecs["mug"][:, c:c + 1],
                fm(h_fm, c), ALU.mult, ALU.add),
                waits=(E["gp"],) if c == 0 else (),
                inc=1 if c == FC - 1 else 0)
        ev_gp = None
        for m in range(FC):
            ps = pslot[m % 2]
            evm = None
            for kb in range(FC):
                evm = b.emit("T", lambda e, m=m, kb=kb, ps=ps: e.matmul(
                    ps[:, 0:TILE], wsb["Wg"][:, kb, 128 * m:128 * m + 128],
                    fm(xmix["xg"], kb), start=(kb == 0),
                    stop=(kb == FC - 1)),
                    waits=(ev_mix, ld["Wg"], ev_psb_free[m % 2])
                    if kb == 0 else (), inc=1 if kb == FC - 1 else 0)
            def evac(e, m=m, ps=ps):
                return e.copy(fm(g_fm, m), ps[:, 0:TILE])
            ev_gp = b.emit("S", evac,
                           waits=(evm, E["gnlast"]) if m == 0 else (evm,),
                           inc=1)
            ev_psb_free[m % 2] = ev_gp
        E["gp"] = ev_gp
        ev_se = b.emit("S", lambda e: e.activation(
            sig_t[:].rearrange("p c t -> p (c t)"),
            g_fm[:].rearrange("p c t -> p (c t)"), AF.Tanh, scale=0.5),
            waits=(ev_gp, E["sxp"]), inc=1)
        b.emit("V", lambda e: e.tensor_scalar(
            sig_t[:].rearrange("p c t -> p (c t)"),
            sig_t[:].rearrange("p c t -> p (c t)"), 0.5, 0.5,
            ALU.mult, ALU.add), waits=(ev_se,))
        E["g"] = b.emit("V", lambda e: TT(
            e, g_fm[:].rearrange("p c t -> p (c t)"),
            g_fm[:].rearrange("p c t -> p (c t)"),
            sig_t[:].rearrange("p c t -> p (c t)"), ALU.mult), inc=1)

    def a_chain(i):
        warm = i < WARM_TILES
        ev_oi = None
        for ci in range(NCH):
            row0 = 32 * (ci // 2)
            par = ci % 2
            ktm = khtm[par]
            puv = PSU[:, 512 * par:512 * par + 512]
            ev_u = None
            for h in range(H):
                ev_u = b.emit("T", lambda e, h=h, row0=row0, ktm=ktm,
                              puv=puv: e.matmul(
                                  puv[64 * (h % 2):64 * (h % 2) + 64,
                                      DK * (h // 2):DK * (h // 2) + DK],
                                  ktm[row0:row0 + 32, h // 2,
                                      64 * (h % 2):64 * (h % 2) + 64],
                                  vtm[row0:row0 + 32, h // 2,
                                      64 * (h % 2):64 * (h % 2) + 64],
                                  start=True, stop=True,
                                  tile_position=(row0, 64 * (h % 2))),
                              waits=(E["tm"], E["chain"],
                                     E["uL"].get(par)) if h == 0 else (),
                              inc=1 if h == H - 1 else 0)
            if not warm:
                for h in range(H):
                    ev_oi = b.emit("T", lambda e, h=h, ci=ci: e.matmul(
                        PSA[64 * (h % 2):64 * (h % 2) + 64,
                            128 * (h // 2) + CH * ci:
                            128 * (h // 2) + CH * ci + CH],
                        S[64 * (h % 2):64 * (h % 2) + 64, h // 2, :],
                        rt_fm[64 * (h % 2):64 * (h % 2) + 64, h // 2,
                              CH * ci:CH * ci + CH],
                        start=True, stop=True,
                        tile_position=(64 * (h % 2), 64 * (h % 2))),
                        waits=(E["rt"], E["psa"]) if (h == 0 and ci == 0)
                        else (), inc=1 if h == H - 1 else 0)
            def supd1(e, ci=ci):
                return TT(e, S, S,
                          A_ch[:, :, ci:ci + 1].to_broadcast((128, FC, DK)),
                          ALU.mult)
            b.emit("V", supd1, waits=(ev_u, ev_oi, E["A"]))
            def supd2(e, puv=puv):
                return TT(e, S.rearrange("p c v -> p (c v)"),
                          S.rearrange("p c v -> p (c v)"), puv, ALU.add)
            E["chain"] = b.emit("V", supd2, inc=1)
            E["uL"][par] = E["chain"]
        E["rtfree"] = ev_oi

    def a_delta(i):
        ev_odp = b.emit("S", lambda e: e.copy(
            od_fm[:].rearrange("p c t -> p (c t)"), PSA[:]),
            waits=(E["rtfree"],), inc=1)
        E["psa"] = ev_odp
        G4 = G_fm[:].rearrange("p c (n l) -> p c n l", l=CH)
        rk4 = rk_bf[:].rearrange("p c (n l) -> p c n l", l=CH)
        v4 = v_fm[:].rearrange("p c (n l) -> p c n l", l=CH)
        od4 = od_fm[:].rearrange("p c (n l) -> p c n l", l=CH)
        tp4 = tmp_fm[:].rearrange("p c (n l) -> p c n l", l=CH)
        d4 = d_fm[:].rearrange("p c (n l) -> p c n l", l=CH)
        ev_G = None
        ev_rk = None
        ev_last = None
        for dl in range(CH):
            W = CH - dl
            ncols = NCH * W
            if dl == 0:
                def g0(e):
                    last = None
                    for c in range(FC):
                        last = e.tensor_scalar(fm(G_fm, c), fm(r_fm, c),
                                               vecs["uvec"][:, c:c + 1],
                                               None, ALU.mult)
                    return last
                ev_G = b.emit("V", g0, inc=1)
            elif dl == 1:
                ev_G = b.emit("V", lambda e: TT(
                    e, G_fm[:].rearrange("p c t -> p (c t)"),
                    r_fm[:].rearrange("p c t -> p (c t)"),
                    r_fm[:].rearrange("p c t -> p (c t)"), ALU.bypass),
                    waits=(ev_rk,), inc=1)
            else:
                def gu(e, dl=dl):
                    return TT(e, G4[:, :, :, dl:CH], G4[:, :, :, dl:CH],
                              d4[:, :, :, 1:CH - dl + 1], ALU.mult)
                ev_G = b.emit("V", gu, waits=(ev_rk,), inc=1)
            def rkx(e, dl=dl, W=W):
                return e.tensor_tensor(
                    rk4[:, :, :, dl:CH], G4[:, :, :, dl:CH],
                    k_fm[:].rearrange("p c (n l) -> p c n l", l=CH)
                    [:, :, :, 0:W], ALU.mult)
            ev_rk = b.emit("G", rkx, waits=(ev_G,), inc=1)
            ev_sl = None
            for c in range(FC):
                ev_sl = b.emit("T", lambda e, c=c, dl=dl, ncols=ncols:
                               e.matmul(
                                   PSD[0:H, 0:ncols], hsel[:, c, :],
                                   rk_bf[:].rearrange(
                                       "p cc (n l) -> p cc n l", l=CH)
                                   [:, c, :, dl:CH],
                                   start=(c == 0), stop=(c == FC - 1)),
                               waits=(ev_rk, E["psd"]) if c == 0 else (),
                               inc=1 if c == FC - 1 else 0)
            ev_cd = b.emit("S", lambda e, ncols=ncols: e.copy(
                dots_bf[0:H, 0:ncols], PSD[0:H, 0:ncols]),
                waits=(ev_sl,), inc=1)
            E["psd"] = ev_cd
            ev_b2 = None
            for c in range(FC):
                ev_b2 = b.emit("T", lambda e, c=c, ncols=ncols: e.matmul(
                    PSA[:, c * 128:c * 128 + ncols],
                    hmap_bf[0:H, c, :], dots_bf[0:H, 0:ncols],
                    start=True, stop=True),
                    waits=(ev_cd, E["psa"]) if c == 0 else (),
                    inc=1 if c == FC - 1 else 0)
            def contrib(e, dl=dl, W=W):
                return e.scalar_tensor_tensor(
                    tp4[:, :, :, dl:CH],
                    PSA[:].rearrange("p (c x) -> p c x", c=FC)
                    [:, :, 0:W * NCH].rearrange("p c (n l) -> p c n l",
                                                n=NCH),
                    1.0, v4[:, :, :, 0:W], ALU.mult, ALU.mult)
            b.emit("V", contrib, waits=(ev_b2, E["sxp"]) if dl == 0
                   else (ev_b2,))
            def oacc(e, dl=dl):
                return TT(e, od4[:, :, :, dl:CH], od4[:, :, :, dl:CH],
                          tp4[:, :, :, dl:CH], ALU.add)
            ev_last = b.emit("V", oacc, waits=(ev_odp,) if dl == 0 else (),
                             inc=1)
            E["psa"] = ev_last
        E["dfree"] = ev_G
        E["kvfree"] = ev_last
        E["deltaG"] = ev_rk
        E["odacc"] = ev_last

    def a_tail(i):
        par = i % 2
        xf = x_fm2[par]
        ev_osq = b.emit("V", lambda e: TT(
            e, onrm[:].rearrange("p c t -> p (c t)"),
            od_fm[:].rearrange("p c t -> p (c t)"),
            od_fm[:].rearrange("p c t -> p (c t)"), ALU.mult),
            waits=(E["odacc"], E["rmsT"]), inc=1)
        ev_ocp = b.emit("V", lambda e: TT(
            e, on_bf[:].rearrange("p c t -> p (c t)"),
            od_fm[:].rearrange("p c t -> p (c t)"),
            od_fm[:].rearrange("p c t -> p (c t)"), ALU.bypass), inc=1)
        ev_gs = None
        for c in range(FC):
            b.emit("T", lambda e, c=c: e.matmul(
                PSD[0:H, 0:TILE], hsel[:, c, :], fm(on_bf, c),
                start=(c == 0), stop=(c == FC - 1)),
                waits=(ev_ocp, E["psd"]) if c == 0 else ())
        for c in range(FC):
            ev_gs = b.emit("T", lambda e, c=c: e.matmul(
                PSD[0:H, TILE:2 * TILE], hsel[:, c, :], fm(onrm, c),
                start=(c == 0), stop=(c == FC - 1)),
                inc=1 if c == FC - 1 else 0)
        b.emit("V", lambda e: e.tensor_scalar(
            gst[0:H, 0, :], PSD[0:H, 0:TILE], 1.0 / DK, None, ALU.mult),
            waits=(ev_gs,))
        b.emit("V", lambda e: TT(e, gst[0:H, 1, :], gst[0:H, 0, :],
                                 gst[0:H, 0, :], ALU.mult))
        ev_gv = b.emit("V", lambda e: e.scalar_tensor_tensor(
            gst[0:H, 2, :], PSD[0:H, TILE:2 * TILE], 1.0 / DK,
            gst[0:H, 1, :], ALU.mult, ALU.subtract), inc=1)
        E["psd"] = ev_gv
        ev_gl = b.emit("S", lambda e: e.activation(
            gst[0:H, 3, :], gst[0:H, 2, :], AF.Ln, bias=epsc[0:H, 1:2]),
            waits=(ev_gv,), inc=1)
        ev_gr = b.emit("S", lambda e: e.activation(
            gst[0:H, 4, :], gst[0:H, 3, :], AF.Exp, scale=-0.5),
            waits=(ev_gl,), inc=1)
        ev_gnlast = None
        for c in range(FC):
            ps = pslot[c % 2]
            b.emit("T", lambda e, c=c, ps=ps: e.matmul(
                ps[:, 0:TILE], hmap_f[0:H, c, :], gst[0:H, 0, :],
                start=True, stop=True),
                waits=(ev_gr, ev_psb_free[c % 2]))
            ev_rb = b.emit("T", lambda e, c=c, ps=ps: e.matmul(
                ps[:, TILE:2 * TILE], hmap_f[0:H, c, :], gst[0:H, 4, :],
                start=True, stop=True), inc=1)
            def n1(e, c=c, ps=ps):
                return TT(e, fm(tmp_fm, c), fm(od_fm, c), ps[:, 0:TILE],
                          ALU.subtract)
            b.emit("V", n1, waits=(ev_rb,))
            def n2(e, c=c, ps=ps):
                return TT(e, fm(tmp_fm, c), fm(tmp_fm, c),
                          ps[:, TILE:2 * TILE], ALU.mult)
            ev_n2 = b.emit("V", n2, inc=1)
            ev_psb_free[c % 2] = ev_n2
            def n3(e, c=c):
                return e.tensor_scalar(fm(tmp_fm, c), fm(tmp_fm, c),
                                       vecs["gnw"][:, c:c + 1],
                                       vecs["gnb"][:, c:c + 1],
                                       ALU.mult, ALU.add)
            b.emit("V", n3)
            def n4(e, c=c):
                return TT(e, fm(on_bf, c), fm(tmp_fm, c), fm(g_fm, c),
                          ALU.mult)
            ev_gnlast = b.emit("V", n4, waits=(E["g"],), inc=1)
        E["gnlast"] = ev_gnlast
        ev_u1 = None
        ev_xpu = None
        for m in range(FC):
            ps = pslot[m % 2]
            evm = None
            for kb in range(FC):
                evm = b.emit("T", lambda e, m=m, kb=kb, ps=ps: e.matmul(
                    ps[:, 0:TILE], wsb["Wo"][:, kb, 128 * m:128 * m + 128],
                    fm(on_bf, kb), start=(kb == 0), stop=(kb == FC - 1)),
                    waits=(ev_gnlast, ld["Wo"], ev_psb_free[m % 2])
                    if kb == 0 else (), inc=1 if kb == FC - 1 else 0)
            def xpue(e, m=m, ps=ps, xf=xf):
                return e.scalar_tensor_tensor(
                    fm(tmp_fm, m), fm(xf, m), 2.0, ps[:, 0:TILE],
                    ALU.mult, ALU.add)
            ev_xpu = b.emit("V", xpue, waits=(evm,), inc=1)
            def u1e(e, m=m, ps=ps, xf=xf):
                return TT(e, fm(xf, m), fm(xf, m), ps[:, 0:TILE], ALU.add)
            ev_u1 = b.emit("V", u1e, inc=1)
            ev_psb_free[m % 2] = ev_u1
        rt0 = (i - WARM_TILES) * TILE
        E["su1"] = dma_y(u1_scr[:, :, rt0:rt0 + TILE], xf[:],
                         waits=(ev_u1,), chan="Ys")
        E["sxp"] = dma_y(xpu_scr[:, :, rt0:rt0 + TILE], tmp_fm[:],
                         waits=(ev_xpu, ev_u1), chan="Ys")
        E["xfree"][par] = E["su1"]

    a_front(0)
    for ti in range(NT):
        if ti >= WARM_TILES:
            a_gproj(ti)
        a_chain(ti)
        if ti >= WARM_TILES:
            a_delta(ti)
        if ti + 1 < NT:
            a_front(ti + 1)
        if ti >= WARM_TILES:
            a_tail(ti)
    ev_su1 = E["su1"]
    ev_sxp = E["sxp"]
    ev_psa_free = E["psa"]
    ev_psd_free = E["psd"]
    ev_chain = E["chain"]

    ev_phaseA_T = ("T", b.counts["T"])
    ev_phaseA_V = ("V", b.counts["V"])
    ev_phaseA_done = b.emit("V", lambda e: e.engine_nop()
                            if hasattr(e, "engine_nop") else e.nop(),
                            waits=(ev_su1, ev_sxp), inc=1)

    ar.release()
    ar.mark()
    w_f1 = ar.bf16((FC, FF))
    w_f2 = ar.bf16((FF // 128, D))
    dma_g(w_f1, dr["fw1"].rearrange("(kb p) o -> p kb o", p=128),
          waits=(ev_phaseA_T,), chan="Ga")
    dma_g(w_f2, dr["fw2"].rearrange("(kb p) o -> p kb o", p=128),
          waits=(ev_phaseA_T,), chan="Ga")
    ev_fwAll = ("Ga", b.counts["Ga"])
    ld["fw1"] = ev_fwAll
    ld["fw2"] = ev_fwAll
    uB = ar.f32(FMT)
    xpB = ar.f32(FMT)
    h2B = ar.bf16(FMT)
    hidB = ar.bf16((FF // 128, TILE))
    v2B = ar.f32(FMT)
    y1B = ar.f32(FMT)
    sqB = ar.bf16(FMT)
    rowB = ar.f32((6, TILE))

    ev_uB_free = ev_phaseA_done
    ev_xpB_free = ev_phaseA_done
    ev_y1_w = None
    for tj in range(T_CORE // TILE):
        t0 = tj * TILE
        ev_lu = dma_y(uB[:], u1_scr[:, :, t0:t0 + TILE],
                      waits=(ev_uB_free, ev_su1), chan="Yl")
        ev_lx = dma_y(xpB[:], xpu_scr[:, :, t0:t0 + TILE],
                      waits=(ev_xpB_free, ev_sxp), chan="Yl")
        def sqb(e):
            return TT(e, sqB[:].rearrange("p c t -> p (c t)"),
                      uB[:].rearrange("p c t -> p (c t)"),
                      uB[:].rearrange("p c t -> p (c t)"), ALU.mult)
        ev_sq = b.emit("V", sqb, waits=(ev_lx, ev_phaseA_done), inc=1)
        ev_sum = None
        for c in range(FC):
            ev_sum = b.emit("T", lambda e, c=c: e.matmul(
                PSD[0:1, 0:TILE], ones_c_bf, fm(sqB, c),
                start=(c == 0), stop=(c == FC - 1)),
                waits=(ev_sq, ev_psd_free) if c == 0 else (),
                inc=1 if c == FC - 1 else 0)
        def rln(e):
            return e.activation(rowB[0:1, 0, :], PSD[0:1, 0:TILE], AF.Ln,
                                bias=epsc[0:1, 0:1], scale=1.0 / D)
        ev_rl = b.emit("S", rln, waits=(ev_sum,), inc=1)
        def rex(e):
            return e.activation(rowB[0:1, 1, :], rowB[0:1, 0, :], AF.Exp,
                                scale=-0.5)
        ev_r2 = b.emit("S", rex, waits=(ev_rl,), inc=1)
        ev_psd_free = ev_r2
        ev_bc = b.emit("T", lambda e: e.matmul(
            PSC[:, 0:TILE], ones_r, rowB[0:1, 1, :], start=True, stop=True),
            waits=(ev_r2, ev_psb_free[1]), inc=1)
        ev_h2 = None
        for c in range(FC):
            ev_h2 = b.emit("V", lambda e, c=c: e.scalar_tensor_tensor(
                fm(h2B, c), fm(uB, c), vecs["ln2w"][:, c:c + 1],
                PSC[:, 0:TILE], ALU.mult, ALU.mult),
                waits=(ev_bc,), inc=1 if c == FC - 1 else 0)
        ev_psb_free[1] = ev_h2
        ev_uB_free = ev_h2
        ev_gel = None
        for m in range(FF // 128):
            ps = pslot[m % 2]
            evm = None
            for kb in range(FC):
                evm = b.emit("T", lambda e, m=m, kb=kb, ps=ps: e.matmul(
                    ps[:, 0:TILE], w_f1[:, kb, 128 * m:128 * m + 128],
                    fm(h2B, kb), start=(kb == 0), stop=(kb == FC - 1)),
                    waits=(ev_h2, ld["fw1"], ev_psb_free[m % 2])
                    if kb == 0 else (), inc=1 if kb == FC - 1 else 0)
            def gel(e, m=m, ps=ps):
                return e.activation(hidB[:, m, :], ps[:, 0:TILE], AF.Gelu)
            ev_gel = b.emit("S", gel, waits=(evm,), inc=1)
            ev_psb_free[m % 2] = ev_gel
        ev_v2 = None
        for m in range(FC):
            ps = pslot[m % 2]
            evm = None
            for kb in range(FF // 128):
                evm = b.emit("T", lambda e, m=m, kb=kb, ps=ps: e.matmul(
                    ps[:, 0:TILE], w_f2[:, kb, 128 * m:128 * m + 128],
                    hidB[:, kb, :], start=(kb == 0),
                    stop=(kb == FF // 128 - 1)),
                    waits=(ev_gel, ld["fw2"], ev_psb_free[m % 2])
                    if kb == 0 else (), inc=1 if kb == FF // 128 - 1 else 0)
            def v2e(e, m=m, ps=ps):
                return TT(e, fm(v2B, m), fm(xpB, m), ps[:, 0:TILE], ALU.add)
            ev_v2 = b.emit("V", v2e, waits=(evm, ev_lx), inc=1)
            ev_psb_free[m % 2] = ev_v2
        ev_xpB_free = ev_v2
        def sq2(e):
            return TT(e, sqB[:].rearrange("p c t -> p (c t)"),
                      v2B[:].rearrange("p c t -> p (c t)"),
                      v2B[:].rearrange("p c t -> p (c t)"), ALU.mult)
        b.emit("V", sq2, waits=(ev_v2,))
        def v2c(e):
            return TT(e, h2B[:].rearrange("p c t -> p (c t)"),
                      v2B[:].rearrange("p c t -> p (c t)"),
                      v2B[:].rearrange("p c t -> p (c t)"), ALU.bypass)
        ev_v2b = b.emit("V", v2c, inc=1)
        ev_s2 = None
        for c in range(FC):
            b.emit("T", lambda e, c=c: e.matmul(
                PSD[0:1, 0:TILE], ones_c_bf, fm(h2B, c),
                start=(c == 0), stop=(c == FC - 1)),
                waits=(ev_v2b, ev_psd_free) if c == 0 else ())
        for c in range(FC):
            ev_s2 = b.emit("T", lambda e, c=c: e.matmul(
                PSD[0:1, TILE:2 * TILE], ones_c_bf, fm(sqB, c),
                start=(c == 0), stop=(c == FC - 1)),
                inc=1 if c == FC - 1 else 0)
        def lmean(e):
            return e.tensor_scalar(rowB[0:1, 0, :], PSD[0:1, 0:TILE],
                                   1.0 / D, None, ALU.mult)
        b.emit("V", lmean, waits=(ev_s2,))
        def lm2(e):
            return TT(e, rowB[0:1, 1, :], rowB[0:1, 0, :], rowB[0:1, 0, :],
                      ALU.mult)
        b.emit("V", lm2)
        def lvar(e):
            return e.scalar_tensor_tensor(rowB[0:1, 2, :],
                                          PSD[0:1, TILE:2 * TILE],
                                          1.0 / D, rowB[0:1, 1, :],
                                          ALU.mult, ALU.subtract)
        ev_lv = b.emit("V", lvar, inc=1)
        ev_psd_free = ev_lv
        def lln(e):
            return e.activation(rowB[0:1, 3, :], rowB[0:1, 2, :], AF.Ln,
                                bias=epsc[0:1, 2:3])
        ev_ll = b.emit("S", lln, waits=(ev_lv,), inc=1)
        def lex(e):
            return e.activation(rowB[0:1, 3, :], rowB[0:1, 3, :], AF.Exp,
                                scale=-0.5)
        ev_lr = b.emit("S", lex, waits=(ev_ll,), inc=1)
        b.emit("T", lambda e: e.matmul(PSC[:, 0:TILE], ones_r, rowB[0:1, 0, :],
                                       start=True, stop=True),
               waits=(ev_lr, ev_psb_free[1]))
        ev_bb = b.emit("T", lambda e: e.matmul(
            PSC[:, TILE:2 * TILE], ones_r, rowB[0:1, 3, :],
            start=True, stop=True), inc=1)
        ev_y1t = None
        for c in range(FC):
            def za(e, c=c):
                return TT(e, fm(y1B, c), fm(v2B, c), PSC[:, 0:TILE],
                          ALU.subtract)
            b.emit("V", za, waits=(ev_bb,))
            def zb(e, c=c):
                return TT(e, fm(y1B, c), fm(y1B, c), PSC[:, TILE:2 * TILE],
                          ALU.mult)
            ev_y1t = b.emit("V", zb, inc=1)
        ev_psb_free[1] = ev_y1t
        ev_y1_w = dma_y(y1_scr[:, :, t0:t0 + TILE], y1B[:], waits=(ev_y1t,),
                        chan="Ys")
        if KDEBUG:
            dma_y(dbg_y1[:, :, t0:t0 + TILE], y1B[:], chan="Ys")
        ev_rowfree = ev_y1_w

    ev_phaseB_T = ("T", b.counts["T"])

    ar.release()
    ar.mark()
    gw_sb = ar.bf16((FC, NE))
    dma_g(gw_sb, dr["gate_w"].rearrange("(kb p) o -> p kb o", p=128),
          waits=(ev_phaseB_T,), chan="Ga")
    ld["gw"] = ("Ga", b.counts["Ga"])
    y1bf = ar.bf16((FC, HALF))
    wbuf = [ar.bf16((FC, 1024)) for _ in range(2)]
    hbuf = ar.bf16((FF // 128, HALF))
    accC = ar.f32((FC, HALF))
    gbcf = ar.f32((HALF,))
    tmpC = ar.f32((HALF,))
    lgC = ar.f32((HALF,))
    gtm = ar.f32((HALF // 128, 16))
    gden = ar.f32((16,))
    sqC = ar.bf16((FC, TILE))
    cbC = ar.bf16((FC, TILE))
    y2t = ar.f32((FC, TILE))
    y1t2 = ar.f32((FC, TILE))
    y2s = ar.f32((FC, TILE))

    NPART = 12
    wseq = []
    for half in range(2):
        for e_i in range(NE + 1):
            for part in range(NPART):
                wseq.append((half, e_i, part))

    def w_src(e_i, part):
        if part < 4:
            base = dr["mw1"][e_i] if e_i < NE else dr["ws1"]
            src = base.rearrange("(kb p) m -> p kb m", p=128)
            return src[:, :, 1024 * part:1024 * (part + 1)]
        m = part - 4
        base = dr["mw2r"][e_i, m] if e_i < NE else dr["ws2r"][m]
        return base.rearrange("(kb p) o -> p kb o", p=128)

    wld = {}
    wfree = {0: ev_phaseB_T, 1: ev_phaseB_T}

    def wv2(i):
        return wbuf[i].rearrange("p a b -> p (a b)")[:, 0:32 * 128].rearrange(
            "p (a b) -> p a b", a=32)

    def load_chunk(si):
        half, e_i, part = wseq[si]
        dst = wbuf[si % 2] if part < 4 else wv2(si % 2)
        wld[si] = dma_g(dst, w_src(e_i, part), waits=(wfree[si % 2],),
                        chan="Gm%d" % (si % 2))

    load_chunk(0)
    load_chunk(1)

    ev_accw = None
    ev_y2done = None
    ev_lgt = None
    ev_y1bf_free = None
    ev_psu_free = ev_chain
    wpair = [(pslot[0], pslot[1]), (PSTF, PSD)]
    wpair_free = [[[ev_psb_free[0]], [ev_psb_free[1]]],
                  [[E["pst"]], [ev_psd_free]]]
    for half in range(2):
        h0 = half * HALF
        ev_y1bf = dma_g(y1bf, y1_scr[:, :, h0:h0 + HALF],
                        waits=(ev_y1_w, ev_y1bf_free), chan="Gy")
        ev_lg = None
        for cg in range(2):
            evl = None
            for kb in range(FC):
                evl = b.emit("T", lambda e, kb=kb, cg=cg: e.matmul(
                    PSD[0:NE, 0:512], gw_sb[:, kb, :],
                    y1bf[:, kb, 512 * cg:512 * (cg + 1)],
                    start=(kb == 0), stop=(kb == FC - 1)),
                    waits=(ld["gw"], ev_y1bf, ev_psd_free)
                    if kb == 0 else (), inc=1 if kb == FC - 1 else 0)
            def lgb(e, cg=cg):
                return e.activation(lgC[0:NE, cg * 512:(cg + 1) * 512],
                                    PSD[0:NE, 0:512],
                                    AF.Identity, bias=gate_b[0:NE, :])
            ev_lg = b.emit("S", lgb, waits=(evl,), inc=1)
            ev_psd_free = ev_lg
        ev_lgt = None
        for tb in range(HALF // 128):
            evt = b.emit("T", lambda e, tb=tb: e.transpose(
                PSD[0:128, 0:NE], lgC[0:NE, 128 * tb:128 * tb + 128],
                id_f[0:NE, 0:NE]),
                waits=(ev_lg, ev_psd_free) if tb == 0 else (ev_lgt,), inc=1)
            def lcp(e, tb=tb):
                return TT(e, gtm[:, tb, 0:NE], PSD[0:128, 0:NE],
                          PSD[0:128, 0:NE], ALU.bypass)
            ev_lgt = b.emit("V", lcp, waits=(evt,), inc=1)
        ev_psd_free = ev_lgt
        NB = HALF // 128
        eq1v = tmpC.rearrange("p (bb e) -> p bb e", e=NE)[:, 0:NB, :]
        l2v = gbcf.rearrange("p (bb e) -> p bb e", e=NE)[:, 0:NB, :]
        def c_m1(e):
            return e.tensor_reduce(gtm[:, :, 8:9], gtm[:, :, 0:NE], AX.X,
                                   ALU.max)
        b.emit("V", c_m1, waits=(ev_lgt,))
        def c_eq1(e):
            return TT(e, eq1v, gtm[:, :, 0:NE],
                      gtm[:, :, 8:9].to_broadcast((128, NB, NE)),
                      ALU.is_equal)
        b.emit("V", c_eq1)
        def c_l2(e):
            return e.scalar_tensor_tensor(
                l2v, eq1v, -1e30, gtm[:, :, 0:NE], ALU.mult, ALU.add)
        b.emit("V", c_l2)
        def c_m2(e):
            return e.tensor_reduce(gtm[:, :, 9:10], l2v, AX.X, ALU.max)
        b.emit("V", c_m2)
        def c_eq2(e):
            return TT(e, l2v, l2v,
                      gtm[:, :, 9:10].to_broadcast((128, NB, NE)),
                      ALU.is_equal)
        b.emit("V", c_eq2)
        def c_dm(e):
            return TT(e, gtm[:, :, 10:11], gtm[:, :, 9:10], gtm[:, :, 8:9],
                      ALU.subtract)
        ev_cd = b.emit("V", c_dm, inc=1)
        def c_ex(e):
            return e.activation(
                gtm[:, :, 11:12].rearrange("p bb o -> p (bb o)"),
                gtm[:, :, 10:11].rearrange("p bb o -> p (bb o)"), AF.Tanh,
                scale=0.5)
        ev_ce = b.emit("S", c_ex, waits=(ev_cd,), inc=1)
        def c_g1(e):
            return e.tensor_scalar(gtm[:, :, 10:11], gtm[:, :, 11:12],
                                   -0.5, 0.5, ALU.mult, ALU.add)
        b.emit("V", c_g1, waits=(ev_ce,))
        def c_g2(e):
            return e.tensor_scalar(gtm[:, :, 11:12], gtm[:, :, 11:12],
                                   0.5, 0.5, ALU.mult, ALU.add)
        b.emit("V", c_g2)
        def c_w1(e):
            return TT(e, eq1v, eq1v,
                      gtm[:, :, 10:11].to_broadcast((128, NB, NE)), ALU.mult)
        b.emit("V", c_w1)
        def c_w2(e):
            return TT(e, l2v, l2v,
                      gtm[:, :, 11:12].to_broadcast((128, NB, NE)), ALU.mult)
        b.emit("V", c_w2)
        def c_cb(e):
            return TT(e, gtm[:, :, 0:NE], eq1v, l2v, ALU.add)
        ev_cb = b.emit("V", c_cb, inc=1)
        ev_cfm = None
        for tb in range(NB):
            evt = b.emit("T", lambda e, tb=tb: e.transpose(
                PSD[0:NE, 0:128], gtm[:, tb, 0:NE], id_f),
                waits=(ev_cb, ev_psd_free) if tb == 0 else (ev_cfm,), inc=1)
            def ccp(e, tb=tb):
                return TT(e, comb[0:NE, 128 * tb:128 * tb + 128],
                          PSD[0:NE, 0:128], PSD[0:NE, 0:128], ALU.bypass)
            ev_cfm = b.emit("V", ccp, waits=(evt,), inc=1)
        ev_psd_free = ev_cfm

        if KDEBUG:
            dma_y(dbg_gtm[:, 8 * half:8 * half + 8, :], gtm[:],
                  waits=(ev_cb,), chan="Ys")
            dma_y(dbg_lg[:, h0:h0 + HALF], lgC[0:NE, :], waits=(ev_lg,),
                  chan="Ys")
            dma_y(dbg_comb[:, h0:h0 + HALF], comb[0:NE, 0:HALF],
                  waits=(ev_cfm,), chan="Ys")
        def zacc(e):
            return e.memset(accC, 0.0)
        ev_acc0 = b.emit("V", zacc, waits=(ev_y2done,), inc=1)
        ev_accw = ev_acc0
        wpair_free[0][0].append(ev_psb_free[0])
        wpair_free[0][1].append(ev_psb_free[1])
        wpair_free[1][1] = [ev_psd_free]
        ev_gel = None
        for e_i in range(NE + 1):
            shared = e_i == NE
            base_si = half * (NE + 1) * NPART + e_i * NPART
            if not shared:
                b.emit("T", lambda e, e_i=e_i: e.matmul(
                    PSA[:, 0:512], esel[0:NE, e_i, :], comb[0:NE, 0:512],
                    start=True, stop=True),
                    waits=(ev_cfm, ev_psa_free,), inc=0)
                ev_gb = b.emit("T", lambda e, e_i=e_i: e.matmul(
                    PSA[:, 512:1024], esel[0:NE, e_i, :],
                    comb[0:NE, 512:1024], start=True, stop=True), inc=1)
                def gcp(e):
                    return TT(e, gbcf, PSA[:], PSA[:], ALU.bypass)
                ev_gbc = b.emit("V", gcp, waits=(ev_gb,), inc=1)
                ev_psa_free = ev_gbc
            for part in range(4):
                si = base_si + part
                wv = wbuf[si % 2]
                for mm in range(FC):
                    m = FC * part + mm
                    pr = mm % 2
                    pp = wpair[pr]
                    evm = {}
                    for kb in range(FC):
                        for cg in range(2):
                            evm[cg] = b.emit("T", lambda e, wv=wv, mm=mm,
                                             cg=cg, kb=kb, pp=pp: e.matmul(
                                pp[cg][:, 0:512],
                                wv[:, kb, 128 * mm:128 * mm + 128],
                                y1bf[:, kb, 512 * cg:512 * (cg + 1)],
                                start=(kb == 0), stop=(kb == FC - 1)),
                                waits=(wld[si], ev_y1bf, *wpair_free[pr][0],
                                       *wpair_free[pr][1])
                                if (kb == 0 and cg == 0) else (),
                                inc=1 if kb == FC - 1 else 0)
                    ev_y1bf_free = evm[1]
                    for cg in range(2):
                        def gel(e, m=m, cg=cg, e_i=e_i, shared=shared,
                                pp=pp):
                            bias_ap = (bs1[:, m:m + 1] if shared
                                       else b1v[e_i][:, m:m + 1])
                            return e.activation(
                                hbuf[:, m, 512 * cg:512 * (cg + 1)],
                                pp[cg][:, 0:512], AF.Gelu, bias=bias_ap)
                        ev_gel = b.emit("S", gel, waits=(evm[cg],), inc=1)
                        wpair_free[pr][cg] = [ev_gel]
                wfree[si % 2] = ev_gel
                if si + 2 < len(wseq):
                    load_chunk(si + 2)
            for m in range(FC):
                si = base_si + 4 + m
                wv = wv2(si % 2)
                psm = PSA if m % 2 == 0 else PSU
                evm = None
                for kb in range(32):
                    for cg in range(2):
                        evm = b.emit("T", lambda e, wv=wv, m=m, psm=psm,
                                     cg=cg, kb=kb: e.matmul(
                            psm[:, 512 * cg:512 * (cg + 1)],
                            wv[:, kb, :],
                            hbuf[:, kb, 512 * cg:512 * (cg + 1)],
                            start=(kb == 0), stop=(kb == 31)),
                            waits=(wld[si], ev_gel,
                                   ev_psa_free if m % 2 == 0
                                   else ev_psu_free)
                            if (kb == 0 and cg == 0) else (),
                            inc=1 if (kb == 31 and cg == 1) else 0)
                if shared:
                    def accs(e, m=m, psm=psm):
                        return e.scalar_tensor_tensor(
                            fm(accC, m), psm[:], bs2[:, m:m + 1],
                            fm(accC, m), ALU.add, ALU.add)
                    ev_accw = b.emit("V", accs, waits=(evm, ev_accw), inc=1)
                else:
                    def acc1(e, m=m, e_i=e_i, psm=psm):
                        return e.scalar_tensor_tensor(
                            tmpC, psm[:], b2v[e_i][:, m:m + 1], gbcf,
                            ALU.add, ALU.mult)
                    b.emit("V", acc1, waits=(evm, ev_gbc, ev_accw))
                    def acc2(e, m=m):
                        return TT(e, fm(accC, m), fm(accC, m), tmpC, ALU.add)
                    ev_accw = b.emit("V", acc2, inc=1)
                if m % 2 == 0:
                    ev_psa_free = ev_accw
                else:
                    ev_psu_free = ev_accw
                wfree[si % 2] = ev_accw
                if si + 2 < len(wseq):
                    load_chunk(si + 2)

        ev_psd_free = wpair_free[1][1][-1]
        if KDEBUG:
            dma_y(dbg_acc[:, :, h0:h0 + HALF], accC, waits=(ev_accw,),
                  chan="Ys")
        ev_ydma = None
        ev_y1t2_free = None
        for tt in range(HALF // TILE):
            s0 = tt * TILE
            ev_ly = dma_y(y1t2, y1_scr[:, :, h0 + s0:h0 + s0 + TILE],
                          waits=(ev_accw, ev_y1t2_free), chan="Yl")
            def v3(e, s0=s0):
                return TT(e, y2s, accC[:, :, s0:s0 + TILE], y1t2, ALU.add)
            ev_v3 = b.emit("V", v3, waits=(ev_ly, ev_accw, ev_ydma), inc=1)
            ev_y1t2_free = ev_v3
            def sqx(e):
                return TT(e, sqC.rearrange("p c t -> p (c t)"),
                          y2s.rearrange("p c t -> p (c t)"),
                          y2s.rearrange("p c t -> p (c t)"), ALU.mult)
            b.emit("V", sqx)
            def cbx(e):
                return TT(e, cbC.rearrange("p c t -> p (c t)"),
                          y2s.rearrange("p c t -> p (c t)"),
                          y2s.rearrange("p c t -> p (c t)"), ALU.bypass)
            ev_c = b.emit("V", cbx, inc=1)
            ev_s = None
            for c in range(FC):
                b.emit("T", lambda e, c=c: e.matmul(
                    PSD[0:1, 0:TILE], ones_c_bf, cbC[:, c, :],
                    start=(c == 0), stop=(c == FC - 1)),
                    waits=(ev_c, ev_psd_free) if c == 0 else ())
            for c in range(FC):
                ev_s = b.emit("T", lambda e, c=c: e.matmul(
                    PSD[0:1, TILE:2 * TILE], ones_c_bf, sqC[:, c, :],
                    start=(c == 0), stop=(c == FC - 1)),
                    inc=1 if c == FC - 1 else 0)
            def pm(e):
                return e.tensor_scalar(rrow[0:1, 0, :], PSD[0:1, 0:TILE],
                                       1.0 / D, None, ALU.mult)
            b.emit("V", pm, waits=(ev_s,))
            def pm2(e):
                return TT(e, rrow[0:1, 1, :], rrow[0:1, 0, :],
                          rrow[0:1, 0, :], ALU.mult)
            b.emit("V", pm2)
            def pv(e):
                return e.scalar_tensor_tensor(rrow[0:1, 2, :],
                                              PSD[0:1, TILE:2 * TILE],
                                              1.0 / D, rrow[0:1, 1, :],
                                              ALU.mult, ALU.subtract)
            ev_pv = b.emit("V", pv, inc=1)
            ev_psd_free = ev_pv
            def pl(e):
                return e.activation(rrow[0:1, 3, :], rrow[0:1, 2, :], AF.Ln,
                                    bias=epsc[0:1, 2:3])
            ev_pl = b.emit("S", pl, waits=(ev_pv,), inc=1)
            def pe(e):
                return e.activation(rrow[0:1, 3, :], rrow[0:1, 3, :],
                                    AF.Exp, scale=-0.5)
            ev_pr = b.emit("S", pe, waits=(ev_pl,), inc=1)
            b.emit("T", lambda e: e.matmul(PSC[:, 0:TILE], ones_r,
                                           rrow[0:1, 0, :], start=True,
                                           stop=True),
                   waits=(ev_pr, ev_psb_free[1]))
            ev_bb = b.emit("T", lambda e: e.matmul(
                PSC[:, TILE:2 * TILE], ones_r, rrow[0:1, 3, :],
                start=True, stop=True), inc=1)
            ev_y2 = None
            for c in range(FC):
                def za(e, c=c):
                    return TT(e, y2t[:, c, :], y2s[:, c, :],
                              PSC[:, 0:TILE], ALU.subtract)
                b.emit("V", za, waits=(ev_bb,))
                def zb(e, c=c):
                    return TT(e, y2t[:, c, :], y2t[:, c, :],
                              PSC[:, TILE:2 * TILE], ALU.mult)
                ev_y2 = b.emit("V", zb, inc=1)
            ev_psb_free[1] = ev_y2
            evt = None
            for c in range(FC):
                evt = b.emit("T", lambda e, c=c: e.transpose(
                    PSA[:, 128 * c:128 * c + 128], y2t[:, c, :], id_f),
                    waits=(ev_y2, ev_psa_free) if c == 0 else (),
                    inc=1 if c == FC - 1 else 0)
            def ytc(e):
                return TT(e, y2s.rearrange("p c t -> p (c t)"), PSA[:],
                          PSA[:], ALU.bypass)
            ev_yc = b.emit("V", ytc, waits=(evt,), inc=1)
            ev_psa_free = ev_yc
            r0 = h0 + s0
            ev_ydma = dma_y(y_d[r0:r0 + 128, :],
                            y2s.rearrange("p c t -> p (c t)"),
                            waits=(ev_yc,), chan="Yo")
            ev_y2done = ev_ydma

    b.emit("Y", lambda e: e.nop(), waits=(("Yo", b.counts["Yo"]),
                                          ("Ys", b.counts["Ys"]),))

    with nc.Block() as block:
        b.materialize(block)
    ar.ap = None
    ar.t.__exit__(None, None, None)
    for c in reversed(ctxs):
        c.__exit__(None, None, None)
    return nc


def _host_consts():
    c = {}
    c["ident"] = np.eye(128, dtype=np.float32)
    c["ones_r"] = np.ones((1, 128), np.float32)
    c["ones_c"] = np.ones((128, 1), np.float32)
    hs = np.zeros((128, FC, H), np.float32)
    hm = np.zeros((H, FC, 128), np.float32)
    for cc in range(FC):
        for p in range(128):
            hh = (128 * cc + p) // DK
            hs[p, cc, hh] = 1.0
            hm[hh, cc, p] = 1.0
    c["headsel"] = hs
    c["headmap"] = hm
    es = np.zeros((NE, NE, 128), np.float32)
    for e in range(NE):
        es[e, e, :] = 1.0
    c["esel"] = es
    r1 = np.zeros((128, FC * TILE), np.float32)
    r1[:, ::CH] = 1.0
    c["reset1"] = r1
    return c


def _vec_tile(v):
    return np.ascontiguousarray(np.asarray(v, np.float32).reshape(FC, 128).T)


_CACHE = {}


def kernel(**inp):
    if "nc" not in _CACHE:
        _CACHE["nc"] = build_program()
    nc = _CACHE["nc"]

    x = np.ascontiguousarray(np.asarray(inp["L0_attn_x"], np.float32)[0])
    base = {
        "Wr": inp["L0_attn_Wr"], "Wk": inp["L0_attn_Wk"],
        "Wv": inp["L0_attn_Wv"], "Wg": inp["L0_attn_Wg"],
        "Wo": inp["L0_attn_Wo"], "wA": inp["L0_attn_wA"],
        "wB": inp["L0_attn_wB"], "fw1": inp["L0_attn_ffn_w1"],
        "fw2": inp["L0_attn_ffn_w2"], "mw1": inp["L0_ffn_w1"],
        "mw2r": np.ascontiguousarray(
            np.asarray(inp["L0_ffn_w2"], np.float32).reshape(
                NE, FF, FC, 128).transpose(0, 2, 1, 3)),
        "ws1": inp["L0_ffn_ws1"],
        "ws2r": np.ascontiguousarray(
            np.asarray(inp["L0_ffn_ws2"], np.float32).reshape(
                FF, FC, 128).transpose(1, 0, 2)),
        "gate_w": inp["L0_ffn_gate_w"],
        "ln1w": _vec_tile(inp["L0_attn_ln1_w"]),
        "ln2w": _vec_tile(inp["L0_attn_ln2_w"]),
        "mur": _vec_tile(inp["L0_attn_mu_r"]),
        "muk": _vec_tile(inp["L0_attn_mu_k"]),
        "muv": _vec_tile(inp["L0_attn_mu_v"]),
        "muw": _vec_tile(inp["L0_attn_mu_w"]),
        "mug": _vec_tile(inp["L0_attn_mu_g"]),
        "w0": _vec_tile(inp["L0_attn_w0"]),
        "uvec": _vec_tile(np.asarray(inp["L0_attn_u"],
                                     np.float32).reshape(-1)),
        "gnw": _vec_tile(np.asarray(inp["L0_attn_gn_w"],
                                    np.float32).reshape(-1)),
        "gnb": _vec_tile(np.asarray(inp["L0_attn_gn_b"],
                                    np.float32).reshape(-1)),
        "gate_b": np.asarray(inp["L0_ffn_gate_b"],
                             np.float32).reshape(NE, 1),
        "b1": np.ascontiguousarray(
            np.asarray(inp["L0_ffn_b1"], np.float32).reshape(
                NE, FF // 128, 128).transpose(0, 2, 1)),
        "b2": np.ascontiguousarray(
            np.asarray(inp["L0_ffn_b2"], np.float32).reshape(
                NE, FC, 128).transpose(0, 2, 1)),
        "bs1": np.ascontiguousarray(
            np.asarray(inp["L0_ffn_bs1"], np.float32).reshape(
                FF // 128, 128).T),
        "bs2": np.ascontiguousarray(
            np.asarray(inp["L0_ffn_bs2"], np.float32).reshape(FC, 128).T),
    }
    base.update(_host_consts())
    base = {k: np.ascontiguousarray(np.asarray(v, np.float32))
            for k, v in base.items()}

    in_maps = []
    for ci in range(NCORES):
        s0 = ci * T_CORE - T_WARM
        if s0 < 0:
            xs = np.concatenate(
                [np.zeros((-s0, D), np.float32), x[0:s0 + T_ALL]], axis=0)
        else:
            xs = x[s0:s0 + T_ALL]
        m = dict(base)
        m["x"] = np.ascontiguousarray(xs)
        in_maps.append(m)

    res = run_bass_kernel_spmd(nc, in_maps, core_ids=list(range(NCORES)))
    out = np.concatenate([r["y"] for r in res.results], axis=0)
    return out[None, :, :]


# version 10
# speedup vs baseline: 1.3037x; 1.2924x over previous
import os
import numpy as np
import concourse.bass as bass
import concourse.mybir as mybir
from concourse.bass_utils import run_bass_kernel_spmd

F32 = mybir.dt.float32
BF16 = mybir.dt.bfloat16
ALU = mybir.AluOpType
AF = mybir.ActivationFunctionType
AX = mybir.AxisListType

D = 1024
H = 16
DK = 64
FC = 8
LORA = 64
FF = 4096
NE = 8
NCORES = 8
T_CORE = 2048
T_WARM = 512
T_ALL = T_WARM + T_CORE
TILE = 128
NT = T_ALL // TILE
WARM_TILES = T_WARM // TILE
CH = 16
NCH = TILE // CH
HALF = 1024
EPS_RMS = 1e-6
EPS_GN = 1e-5
EPS_LN = 1e-5

ENGS = ("T", "V", "S", "G", "Y")
CHAN_BODY = {"T": "T", "V": "V", "S": "S", "G": "G", "Y": "Y",
             "Yx": "Y", "Ys": "Y", "Yl": "Y", "Yo": "Y", "Yc": "Y",
             "Gc": "G", "Ga": "G", "Gm0": "G", "Gm1": "G", "Gy": "G"}


class Prog:
    def __init__(self, nc):
        self.nc = nc
        self.ops = []
        self.counts = {c: 0 for c in CHAN_BODY}
        self.sems = {}

    def emit(self, chan, fn, waits=(), inc=0):
        waits = tuple(w for w in waits if w is not None)
        self.ops.append((chan, fn, waits, inc))
        if inc:
            self.counts[chan] += inc
            return (chan, self.counts[chan])
        return None

    def materialize(self, block):
        for c in CHAN_BODY:
            self.sems[c] = self.nc.semaphore("sem_" + c).__enter__()
        watermark = {}

        def body_for(eng_name):
            def body(eng):
                for (oc, fn, waits, inc) in self.ops:
                    if CHAN_BODY[oc] != eng_name:
                        continue
                    for (pc, val) in waits:
                        key = (eng_name, pc)
                        if watermark.get(key, -1) >= val:
                            continue
                        watermark[key] = val
                        eng.wait_ge(self.sems[pc], val)
                    inst = fn(eng)
                    if inc:
                        assert inst is not None
                        inst.then_inc(self.sems[oc], inc)
            return body

        block.tensor(body_for("T"))
        block.vector(body_for("V"))
        block.scalar(body_for("S"))
        block.gpsimd(body_for("G"))
        block.sync(body_for("Y"))


class Arena:

    def __init__(self, nc, cols=52992):
        self.t = nc.sbuf_tensor("arena", [128, cols], F32)
        self.ap = self.t.__enter__()
        self.cols = cols
        self.off = 0
        self.marks = []

    def alloc(self, ncols_f32):
        a = self.off
        self.off += ncols_f32
        assert self.off <= self.cols, f"SBUF arena overflow: {self.off}"
        return a

    def f32(self, shape):
        n = int(np.prod(shape))
        a = self.alloc(n)
        v = self.ap[:, a:a + n]
        if len(shape) == 1:
            return v
        return v.rearrange("p (a b) -> p a b", a=shape[0]) if len(shape) == 2 \
            else v.rearrange("p (a b c) -> p a b c", a=shape[0], b=shape[1])

    def bf16(self, shape):
        n = int(np.prod(shape))
        a = self.alloc((n + 1) // 2)
        v = self.ap[:, a:a + (n + 1) // 2].bitcast(BF16)
        if len(shape) == 1:
            return v[:, 0:n]
        return v[:, 0:n].rearrange("p (a b) -> p a b", a=shape[0]) \
            if len(shape) == 2 else \
            v[:, 0:n].rearrange("p (a b c) -> p a b c", a=shape[0], b=shape[1])

    def mark(self):
        self.marks.append(self.off)

    def release(self):
        self.off = self.marks.pop()


def build_program():
    nc = bass.Bass("TRN2", target_bir_lowering=False, debug=False)
    b = Prog(nc)
    dr = {}

    def din(name, shape):
        dr[name] = nc.dram_tensor(name, list(shape), F32,
                                  kind="ExternalInput").ap()
        return dr[name]

    x_d = din("x", (T_ALL, D))
    for nm in ("Wr", "Wk", "Wv", "Wg", "Wo"):
        din(nm, (D, D))
    din("wA", (D, LORA))
    din("wB", (LORA, D))
    din("fw1", (D, FF))
    din("fw2", (FF, D))
    din("mw1", (NE, D, FF))
    din("mw2r", (NE, FC, FF, 128))
    din("ws1", (D, FF))
    din("ws2r", (FC, FF, 128))
    din("gate_w", (D, NE))
    for nm in ("ln1w", "ln2w", "mur", "muk", "muv", "muw", "mug", "w0",
               "uvec", "gnw", "gnb"):
        din(nm, (128, FC))
    din("gate_b", (NE, 1))
    din("b1", (NE, 128, FF // 128))
    din("b2", (NE, 128, FC))
    din("bs1", (128, FF // 128))
    din("bs2", (128, FC))
    din("ident", (128, 128))
    din("ones_r", (1, 128))
    din("ones_c", (128, 1))
    din("headsel", (128, FC, H))
    din("esel", (NE, NE, 128))
    din("headmap", (H, FC, 128))
    din("jmat", (128, 128))
    din("reset1", (128, FC * TILE))

    y_d = nc.dram_tensor("y", [T_CORE, D], F32, kind="ExternalOutput").ap()
    u1_scr = nc.dram_tensor("u1_scr", [128, FC, T_CORE], F32,
                            kind="Internal").ap()
    xpu_scr = nc.dram_tensor("xpu_scr", [128, FC, T_CORE], F32,
                             kind="Internal").ap()
    y1_scr = nc.dram_tensor("y1_scr", [128, FC, T_CORE], F32,
                            kind="Internal").ap()
    KDEBUG = os.environ.get("KDEBUG", "") == "1"
    if KDEBUG:
        dbg_lg = nc.dram_tensor("dbg_lg", [NE, T_CORE], F32,
                                kind="ExternalOutput").ap()
        dbg_comb = nc.dram_tensor("dbg_comb", [NE, T_CORE], F32,
                                  kind="ExternalOutput").ap()
        dbg_acc = nc.dram_tensor("dbg_acc", [128, FC, T_CORE], F32,
                                 kind="ExternalOutput").ap()
        dbg_u1 = nc.dram_tensor("dbg_u1", [128, FC, T_CORE], F32,
                                kind="ExternalOutput").ap()
        dbg_y1 = nc.dram_tensor("dbg_y1", [128, FC, T_CORE], F32,
                                kind="ExternalOutput").ap()
        dbg_gtm = nc.dram_tensor("dbg_gtm", [128, 16, 16], F32,
                                 kind="ExternalOutput").ap()

    ctxs = []

    def psum(nm, shp, dt=F32):
        t = nc.psum_tensor(nm, list(shp), dt)
        ctxs.append(t)
        return t.__enter__()

    PSA = psum("PSA", (128, 1024), F32)
    PSU = psum("PSU", (128, 1024), F32)
    PSB = psum("PSB", (128, 512), F32)
    PSC = psum("PSC", (128, 512), F32)
    PSD = psum("PSD", (128, 512), F32)
    PSTF = psum("PST", (128, 512), F32)
    PST = PSTF.bitcast(BF16)

    ar = Arena(nc)

    id_f = ar.f32((128,))
    id_bf = ar.bf16((128,))
    ones_r = ar.f32((128,))[0:1, :]
    ones_c_bf = ar.bf16((2,))[:, 0:1]
    hsel = ar.bf16((FC, H))
    jm_sb = ar.bf16((128,))
    hmap_f = ar.f32((FC, 128))
    esel = ar.f32((NE, 128))
    reset1 = ar.f32((FC * TILE,))
    vecs = {nm: ar.f32((FC,)) for nm in
            ("ln1w", "ln2w", "mur", "muk", "muv", "muw", "mug", "w0",
             "uvec", "gnw", "gnb")}
    gate_b = ar.f32((1,))
    bs1 = ar.f32((FF // 128,))
    bs2 = ar.f32((FC,))
    b1v = [ar.f32((FF // 128,)) for _ in range(NE)]
    b2v = [ar.f32((FC,)) for _ in range(NE)]
    epsc = ar.f32((4,))
    S = ar.f32((FC, DK))
    h_bnd = ar.f32((FC, 1))
    rrow = ar.f32((6, TILE))
    gst = ar.f32((6, TILE))
    comb = ar.f32((HALF,))
    wlf = ar.f32((TILE,))
    wlt = ar.bf16((TILE,))

    def dma_y(dst, src, waits=(), chan="Yc"):
        return b.emit(chan, lambda e, d=dst, s=src: e.dma_start(d, s),
                      waits=waits, inc=16)

    def dma_g(dst, src, waits=(), chan="Gc"):
        return b.emit(chan, lambda e, d=dst, s=src: e.dma_start(d, s),
                      waits=waits, inc=16)

    ld = {}
    dma_y(id_f, dr["ident"])
    dma_g(id_bf, dr["ident"])
    dma_y(ones_r, dr["ones_r"])
    dma_g(ones_c_bf, dr["ones_c"])
    dma_g(hsel, dr["headsel"])
    dma_g(jm_sb, dr["jmat"])
    dma_y(hmap_f[0:H, :, :], dr["headmap"])
    dma_y(esel[0:NE, :, :], dr["esel"])
    dma_y(reset1, dr["reset1"])
    for nm in vecs:
        dma_y(vecs[nm], dr[nm])
    dma_y(gate_b[0:NE, :], dr["gate_b"])
    dma_y(bs1, dr["bs1"])
    dma_y(bs2, dr["bs2"])
    for e_i in range(NE):
        dma_y(b1v[e_i], dr["b1"][e_i])
        dma_y(b2v[e_i], dr["b2"][e_i])
    ev_const_y = ("Yc", b.counts["Yc"])
    ev_const_g = ("Gc", b.counts["Gc"])

    ar.mark()
    wsb = {}
    for nm in ("Wr", "Wk", "Wv", "Wg", "Wo"):
        wsb[nm] = ar.bf16((FC, D))
        dma_g(wsb[nm], dr[nm].rearrange("(kb p) o -> p kb o", p=128),
              chan="Ga")
    w_lA = ar.bf16((FC, LORA))
    dma_g(w_lA, dr["wA"].rearrange("(kb p) o -> p kb o", p=128), chan="Ga")
    w_lB = ar.bf16((D,))
    dma_g(w_lB[0:LORA, :], dr["wB"], chan="Ga")
    ev_wAll = ("Ga", b.counts["Ga"])
    for nm in ("Wr", "Wk", "Wv", "Wg", "Wo", "wA", "wB"):
        ld[nm] = ev_wAll

    FMT = (FC, TILE)
    x_tm = ar.f32((D,))
    x_fm2 = [ar.f32(FMT), ar.f32(FMT)]
    h_fm = ar.f32(FMT)
    xx_fm = ar.f32(FMT)
    xmix = {nm: ar.bf16(FMT) for nm in ("xr", "xk", "xv", "xw", "xg")}
    r_fm = ar.f32(FMT)
    k_fm = ar.f32(FMT)
    v_fm = ar.f32(FMT)
    d_fm = ar.f32(FMT)
    g_fm = ar.f32(FMT)
    e_fm = ar.f32(FMT)
    abuf = ar.f32(FMT)
    A_ch = ar.f32((FC, NCH))
    rt_fm = ar.f32(FMT)
    kh_e = ar.bf16(FMT)
    kh_o = ar.bf16(FMT)
    khtm = [ar.bf16((FC, 128)) for _ in range(2)]
    vtm = ar.bf16((FC, 128))
    G_fm = ar.f32(FMT)
    rk_bf = ar.bf16(FMT)
    od_fm = ar.f32(FMT)
    tmp_fm = ar.f32(FMT)
    onrm = ar.bf16(FMT)
    on_bf = ar.bf16(FMT)
    sig_t = tmp_fm

    def TT(e, out, a, bb, op):
        if op == ALU.bypass:
            return e.tensor_copy(out, a)
        return e.tensor_tensor(out, a, bb, op)

    def fm(t, c=None, sl=None):
        a = t
        if c is not None:
            a = a[:, c, :]
        if sl is not None:
            a = a[:, sl] if c is not None else a[:, :, sl]
        return a

    def _init(e):
        e.memset(epsc[:, 0:1], EPS_RMS)
        e.memset(epsc[:, 1:2], EPS_GN)
        e.memset(epsc[:, 2:3], EPS_LN)
        e.memset(S, 0.0)
        e.memset(h_bnd, 0.0)
        e.memset(abuf, 0.0)
        e.memset(kh_e, 0.0)
        return e.memset(kh_o, 0.0)
    ev_init = b.emit("V", _init, inc=1)

    ev_chain = ev_init
    ev_u_last = {}
    ev_tile_T_done = None
    ev_xfm_free = ev_init
    ev_oevac = None
    ev_psa_free = ev_init
    ev_pst_free = ev_init
    ev_psb_free = {0: ev_init, 1: ev_init}
    ev_psd_free = ev_init
    ev_rowfree = ev_init

    resetv = reset1

    E = dict(chain=ev_init, psa=ev_init, pst=ev_init, psd=ev_init,
             xtmfree=None, hbnd=ev_init, su1=None, sxp=None,
             l=None, gp=None, gnlast=None, g=None, dfree=ev_init,
             kvfree=ev_init, tm=ev_init, rt=None, rtfree=ev_init,
             rmsT=None, odpre=None, rk=None, deltaG=ev_init)
    E["xfree"] = {0: ev_init, 1: ev_init}
    E["uL"] = {}
    pslot = [PSB, PSC]

    def a_front(i):
        warm = i < WARM_TILES
        par = i % 2
        xf = x_fm2[par]
        xff = xf.rearrange("p c t -> p (c t)")
        t0 = i * TILE
        ev_x = dma_y(x_tm, x_d[t0:t0 + TILE, :],
                     waits=(E["xtmfree"], ev_const_y), chan="Yx")
        ev_xev = None
        for hf in range(2):
            evt = None
            for j in range(4):
                c = 4 * hf + j
                evt = b.emit("T", lambda e, c=c, j=j: e.transpose(
                    PSD[:, 128 * j:128 * j + 128],
                    x_tm[:, 128 * c:128 * c + 128], id_f),
                    waits=(ev_x, E["psd"]) if j == 0 else (),
                    inc=1 if j == 3 else 0)
            E["xtmfree"] = evt
            def xevac(e, hf=hf, xff=xff):
                return e.copy(xff[:, 512 * hf:512 * hf + 512], PSD[:, 0:512])
            ev_xev = b.emit("S", xevac,
                            waits=(evt, E["xfree"][par]), inc=1)
            E["psd"] = ev_xev
        def sq(e, xff=xff):
            return TT(e, onrm[:].rearrange("p c t -> p (c t)"), xff, xff,
                      ALU.mult)
        ev_sq = b.emit("V", sq, waits=(ev_xev, E["rmsT"]), inc=1)
        ev_sum = None
        for c in range(FC):
            ev_sum = b.emit("T", lambda e, c=c: e.matmul(
                PSD[0:1, 0:TILE], ones_c_bf, fm(onrm, c),
                start=(c == 0), stop=(c == FC - 1)),
                waits=(ev_sq, E["psd"], ev_const_g) if c == 0 else (),
                inc=1 if c == FC - 1 else 0)
        E["rmsT"] = ev_sum
        ev_rl = b.emit("S", lambda e: e.activation(
            rrow[0:1, 0, :], PSD[0:1, 0:TILE], AF.Ln,
            bias=epsc[0:1, 0:1], scale=1.0 / D), waits=(ev_sum,), inc=1)
        E["psd"] = ev_rl
        ev_r2 = b.emit("S", lambda e: e.activation(
            rrow[0:1, 1, :], rrow[0:1, 0, :], AF.Exp, scale=-0.5),
            waits=(ev_rl,), inc=1)
        ev_bc = b.emit("T", lambda e: e.matmul(
            PSC[:, 0:TILE], ones_r, rrow[0:1, 1, :], start=True, stop=True),
            waits=(ev_r2, ev_psb_free[1]), inc=1)
        ev_h = None
        for c in range(FC):
            ev_h = b.emit("V", lambda e, c=c, xf=xf: e.scalar_tensor_tensor(
                fm(h_fm, c), fm(xf, c), vecs["ln1w"][:, c:c + 1],
                PSC[:, 0:TILE], ALU.mult, ALU.mult),
                waits=(ev_bc,), inc=1 if c == FC - 1 else 0)
        ev_psb_free[1] = ev_h
        b.emit("V", lambda e: e.tensor_sub(
            xx_fm[:, :, 1:TILE], h_fm[:, :, 0:TILE - 1], h_fm[:, :, 1:TILE]),
            waits=(ev_h,))
        b.emit("V", lambda e: e.tensor_sub(
            xx_fm[:, :, 0:1], h_bnd, h_fm[:, :, 0:1]))
        b.emit("V", lambda e: TT(e, h_bnd, h_fm[:, :, TILE - 1:TILE],
                                 h_fm[:, :, TILE - 1:TILE], ALU.bypass))
        mixes = [("xk", "muk"), ("xv", "muv"), ("xw", "muw")]
        if not warm:
            mixes.append(("xr", "mur"))
        ev_mix = None
        for nm, mu in mixes:
            for c in range(FC):
                ev_mix = b.emit("V", lambda e, nm=nm, mu=mu, c=c:
                                e.scalar_tensor_tensor(
                                    fm(xmix[nm], c), fm(xx_fm, c),
                                    vecs[mu][:, c:c + 1], fm(h_fm, c),
                                    ALU.mult, ALU.add),
                                waits=(E["l"],) if (nm == mixes[0][0]
                                                   and c == 0) else (),
                                inc=1 if c == FC - 1 else 0)

        def proj(dst_fm, srcb, wname, efree):
            evs = None
            for m in range(FC):
                ps = pslot[m % 2]
                evm = None
                for kb in range(FC):
                    evm = b.emit("T", lambda e, m=m, kb=kb, w=wname, ps=ps:
                                 e.matmul(
                                     ps[:, 0:TILE],
                                     wsb[w][:, kb, 128 * m:128 * m + 128],
                                     fm(xmix[srcb], kb),
                                     start=(kb == 0), stop=(kb == FC - 1)),
                                 waits=(ev_mix, ld[wname],
                                        ev_psb_free[m % 2]) if kb == 0
                                 else (), inc=1 if kb == FC - 1 else 0)
                def evac(e, m=m, ps=ps, dst=dst_fm):
                    return e.copy(fm(dst, m), ps[:, 0:TILE])
                eve = b.emit("S", evac,
                             waits=(evm, efree) if m == 0 else (evm,), inc=1)
                ev_psb_free[m % 2] = eve
                evs = eve
            return evs

        ev_k = proj(k_fm, "xk", "Wk", E["kvfree"])
        ev_v = proj(v_fm, "xv", "Wv", E["kvfree"])
        if not warm:
            ev_r = proj(r_fm, "xr", "Wr", E["kvfree"])
        ev_l = None
        for kb in range(FC):
            ev_l = b.emit("T", lambda e, kb=kb: e.matmul(
                PSD[0:LORA, 0:TILE], w_lA[:, kb, :], fm(xmix["xw"], kb),
                start=(kb == 0), stop=(kb == FC - 1)),
                waits=(ev_mix, ld["wA"], E["psd"]) if kb == 0 else (),
                inc=1 if kb == FC - 1 else 0)
        E["l"] = ev_l
        ev_th = b.emit("S", lambda e: e.activation(
            wlt[0:LORA, :], PSD[0:LORA, 0:TILE], AF.Tanh),
            waits=(ev_l,), inc=1)
        E["psd"] = ev_th
        ev_wp = None
        for m in range(FC):
            ps = pslot[m % 2]
            ev_mm = b.emit("T", lambda e, m=m, ps=ps: e.matmul(
                ps[:, 0:TILE], w_lB[0:LORA, 128 * m:128 * m + 128],
                wlt[0:LORA, :], start=True, stop=True),
                waits=(ev_th, ev_psb_free[m % 2]), inc=1)
            def d_e1(e, m=m, ps=ps):
                return e.activation(fm(d_fm, m), ps[:, 0:TILE], AF.Exp,
                                    bias=vecs["w0"][:, m:m + 1])
            ev_wp = b.emit("S", d_e1,
                           waits=(ev_mm, E["dfree"]) if m == 0
                           else (ev_mm,), inc=1)
            ev_psb_free[m % 2] = ev_wp
        ev_d = b.emit("S", lambda e: e.activation(
            d_fm[:].rearrange("p c t -> p (c t)"),
            d_fm[:].rearrange("p c t -> p (c t)"), AF.Exp, scale=-1.0),
            waits=(ev_wp, ("S", b.counts["S"])), inc=1)
        d4 = d_fm[:].rearrange("p c (n l) -> p c n l", l=CH)
        a4 = abuf[:].rearrange("p c (n l) -> p c n l", l=CH)
        e4 = e_fm[:].rearrange("p c (n l) -> p c n l", l=CH)
        ev_af = b.emit("V", lambda e: TT(
            e, a4[:, :, :, 1:CH], d4[:, :, :, 0:CH - 1],
            d4[:, :, :, 0:CH - 1], ALU.bypass), waits=(ev_d,), inc=1)
        ev_E = b.emit("V", lambda e: e.tensor_tensor_scan(
            e_fm[:].rearrange("p c t -> p (c t)"),
            abuf[:].rearrange("p c t -> p (c t)"),
            resetv, 0.0, ALU.mult, ALU.add), waits=(ev_af,), inc=1)
        ev_A = b.emit("V", lambda e: TT(
            e, A_ch[:].rearrange("p c (n o) -> p c n o", o=1),
            e4[:, :, :, CH - 1:CH], d4[:, :, :, CH - 1:CH], ALU.mult),
            inc=1)
        E["A"] = ev_A
        if not warm:
            ev_rt = b.emit("V", lambda e: TT(
                e, rt_fm[:].rearrange("p c t -> p (c t)"),
                r_fm[:].rearrange("p c t -> p (c t)"),
                e_fm[:].rearrange("p c t -> p (c t)"), ALU.mult),
                waits=(ev_r, E["rtfree"]), inc=1)
            E["rt"] = ev_rt
        ev_a2 = b.emit("V", lambda e: TT(
            e, a4[:, :, :, 1:CH], d4[:, :, :, CH - 1:0:-1],
            d4[:, :, :, CH - 1:0:-1], ALU.bypass), inc=1)
        ev_Sx = b.emit("V", lambda e: e.tensor_tensor_scan(
            e_fm[:].rearrange("p c t -> p (c t)"),
            abuf[:].rearrange("p c t -> p (c t)"),
            resetv, 0.0, ALU.mult, ALU.add), waits=(ev_a2,), inc=1)
        ke4 = kh_e[:].rearrange("p c (n l) -> p c n l", l=2 * CH)
        ko4 = kh_o[:].rearrange("p c (n l) -> p c n l", l=2 * CH)
        k4 = k_fm[:].rearrange("p c (n l) -> p c n l", l=2 * CH)
        ev_khe = b.emit("V", lambda e: TT(
            e, ke4[:, :, :, 0:CH], k4[:, :, :, 0:CH],
            e_fm[:].rearrange("p c (n l) -> p c n l", l=2 * CH)
            [:, :, :, CH - 1::-1], ALU.mult),
            waits=(ev_Sx, ev_k, E["tm"]), inc=1)
        ev_kho = b.emit("V", lambda e: TT(
            e, ko4[:, :, :, CH:2 * CH], k4[:, :, :, CH:2 * CH],
            e_fm[:].rearrange("p c (n l) -> p c n l", l=2 * CH)
            [:, :, :, 2 * CH - 1:CH - 1:-1], ALU.mult), inc=1)
        ev_tm = None
        for (src_t, dst, evd) in ((kh_e, khtm[0], ev_khe),
                                  (kh_o, khtm[1], ev_kho)):
            evt = None
            for c in range(FC):
                evt = b.emit("T", lambda e, src_t=src_t, c=c: e.transpose(
                    PST[:, 128 * c:128 * c + 128], fm(src_t, c), id_bf),
                    waits=(evd, E["pst"]) if c == 0 else (),
                    inc=1 if c == FC - 1 else 0)
            def tevac(e, dst=dst):
                return e.copy(dst[:].rearrange("p c t -> p (c t)"), PST[:])
            ev_tm = b.emit("S", tevac, waits=(evt,), inc=1)
            E["pst"] = ev_tm
        evt = None
        for c in range(FC):
            evt = b.emit("T", lambda e, c=c: e.transpose(
                PSA[:, 128 * c:128 * c + 128], fm(v_fm, c), id_f),
                waits=(ev_v, E["psa"]) if c == 0 else (),
                inc=1 if c == FC - 1 else 0)
        ev_tm = b.emit("S", lambda e: e.copy(
            vtm[:].rearrange("p c t -> p (c t)"), PSA[:]),
            waits=(evt,), inc=1)
        E["psa"] = ev_tm
        E["tm"] = ev_tm

    def a_gproj(i):
        ev_mix = None
        for c in range(FC):
            ev_mix = b.emit("V", lambda e, c=c: e.scalar_tensor_tensor(
                fm(xmix["xg"], c), fm(xx_fm, c), vecs["mug"][:, c:c + 1],
                fm(h_fm, c), ALU.mult, ALU.add),
                waits=(E["gp"],) if c == 0 else (),
                inc=1 if c == FC - 1 else 0)
        ev_gp = None
        for m in range(FC):
            ps = pslot[m % 2]
            evm = None
            for kb in range(FC):
                evm = b.emit("T", lambda e, m=m, kb=kb, ps=ps: e.matmul(
                    ps[:, 0:TILE], wsb["Wg"][:, kb, 128 * m:128 * m + 128],
                    fm(xmix["xg"], kb), start=(kb == 0),
                    stop=(kb == FC - 1)),
                    waits=(ev_mix, ld["Wg"], ev_psb_free[m % 2])
                    if kb == 0 else (), inc=1 if kb == FC - 1 else 0)
            def evac(e, m=m, ps=ps):
                return e.copy(fm(g_fm, m), ps[:, 0:TILE])
            ev_gp = b.emit("S", evac,
                           waits=(evm, E["gnlast"]) if m == 0 else (evm,),
                           inc=1)
            ev_psb_free[m % 2] = ev_gp
        E["gp"] = ev_gp
        ev_se = b.emit("S", lambda e: e.activation(
            sig_t[:].rearrange("p c t -> p (c t)"),
            g_fm[:].rearrange("p c t -> p (c t)"), AF.Tanh, scale=0.5),
            waits=(ev_gp, E["sxp"]), inc=1)
        b.emit("V", lambda e: e.tensor_scalar(
            sig_t[:].rearrange("p c t -> p (c t)"),
            sig_t[:].rearrange("p c t -> p (c t)"), 0.5, 0.5,
            ALU.mult, ALU.add), waits=(ev_se,))
        E["g"] = b.emit("V", lambda e: TT(
            e, g_fm[:].rearrange("p c t -> p (c t)"),
            g_fm[:].rearrange("p c t -> p (c t)"),
            sig_t[:].rearrange("p c t -> p (c t)"), ALU.mult), inc=1)

    def a_chain(i):
        warm = i < WARM_TILES
        ev_oi = None
        for ci in range(NCH):
            row0 = 32 * (ci // 2)
            par = ci % 2
            ktm = khtm[par]
            puv = PSU[:, 512 * par:512 * par + 512]
            ev_u = None
            for h in range(H):
                ev_u = b.emit("T", lambda e, h=h, row0=row0, ktm=ktm,
                              puv=puv: e.matmul(
                                  puv[64 * (h % 2):64 * (h % 2) + 64,
                                      DK * (h // 2):DK * (h // 2) + DK],
                                  ktm[row0:row0 + 32, h // 2,
                                      64 * (h % 2):64 * (h % 2) + 64],
                                  vtm[row0:row0 + 32, h // 2,
                                      64 * (h % 2):64 * (h % 2) + 64],
                                  start=True, stop=True,
                                  tile_position=(row0, 64 * (h % 2))),
                              waits=(E["tm"], E["chain"],
                                     E["uL"].get(par)) if h == 0 else (),
                              inc=1 if h == H - 1 else 0)
            if not warm:
                for h in range(H):
                    ev_oi = b.emit("T", lambda e, h=h, ci=ci: e.matmul(
                        PSA[64 * (h % 2):64 * (h % 2) + 64,
                            128 * (h // 2) + CH * ci:
                            128 * (h // 2) + CH * ci + CH],
                        S[64 * (h % 2):64 * (h % 2) + 64, h // 2, :],
                        rt_fm[64 * (h % 2):64 * (h % 2) + 64, h // 2,
                              CH * ci:CH * ci + CH],
                        start=True, stop=True,
                        tile_position=(64 * (h % 2), 64 * (h % 2))),
                        waits=(E["rt"], E["psa"]) if (h == 0 and ci == 0)
                        else (), inc=1 if h == H - 1 else 0)
            def supd1(e, ci=ci):
                return TT(e, S, S,
                          A_ch[:, :, ci:ci + 1].to_broadcast((128, FC, DK)),
                          ALU.mult)
            b.emit("V", supd1, waits=(ev_u, ev_oi, E["A"]))
            def supd2(e, puv=puv):
                return TT(e, S.rearrange("p c v -> p (c v)"),
                          S.rearrange("p c v -> p (c v)"), puv, ALU.add)
            E["chain"] = b.emit("V", supd2, inc=1)
            E["uL"][par] = E["chain"]
        E["rtfree"] = ev_oi

    def a_delta(i):
        ev_odp = b.emit("S", lambda e: e.copy(
            od_fm[:].rearrange("p c t -> p (c t)"), PSA[:]),
            waits=(E["rtfree"],), inc=1)
        E["psa"] = ev_odp
        G4 = G_fm[:].rearrange("p c (n l) -> p c n l", l=CH)
        rk4 = rk_bf[:].rearrange("p c (n l) -> p c n l", l=CH)
        v4 = v_fm[:].rearrange("p c (n l) -> p c n l", l=CH)
        od4 = od_fm[:].rearrange("p c (n l) -> p c n l", l=CH)
        tp4 = tmp_fm[:].rearrange("p c (n l) -> p c n l", l=CH)
        d4 = d_fm[:].rearrange("p c (n l) -> p c n l", l=CH)
        k4 = k_fm[:].rearrange("p c (n l) -> p c n l", l=CH)
        ev_G = None
        ev_mm = {}
        ev_ctr = {}
        ev_oacc = {}

        def emit_ctr(dl):
            W = CH - dl
            X = PSA if dl % 2 == 0 else PSU
            def contrib(e, dl=dl, W=W, X=X):
                return e.scalar_tensor_tensor(
                    tp4[:, :, :, dl:CH],
                    X[:].rearrange("p (c x) -> p c x", c=FC)
                    [:, :, 0:W * NCH].rearrange("p c (n w) -> p c n w",
                                                n=NCH),
                    1.0, v4[:, :, :, 0:W], ALU.mult, ALU.mult)
            ev_ctr[dl] = b.emit("V", contrib,
                                waits=(ev_mm[dl], E["sxp"]) if dl == 0
                                else (ev_mm[dl],), inc=1)
            def oacc(e, dl=dl):
                return TT(e, od4[:, :, :, dl:CH], od4[:, :, :, dl:CH],
                          tp4[:, :, :, dl:CH], ALU.add)
            ev_oacc[dl] = b.emit("V", oacc, waits=(ev_odp,) if dl == 0
                                 else (), inc=1)

        for dl in range(CH + 2):
            if dl < CH:
                W = CH - dl
                if dl == 0:
                    def g0(e):
                        last = None
                        for c in range(FC):
                            last = e.tensor_scalar(fm(G_fm, c), fm(r_fm, c),
                                                   vecs["uvec"][:, c:c + 1],
                                                   None, ALU.mult)
                        return last
                    ev_G = b.emit("V", g0, inc=1)
                elif dl == 1:
                    ev_G = b.emit("V", lambda e: TT(
                        e, G_fm[:].rearrange("p c t -> p (c t)"),
                        r_fm[:].rearrange("p c t -> p (c t)"),
                        r_fm[:].rearrange("p c t -> p (c t)"), ALU.bypass),
                        inc=1)
                else:
                    def gu(e, dl=dl):
                        return TT(e, G4[:, :, :, dl:CH], G4[:, :, :, dl:CH],
                                  d4[:, :, :, 1:CH - dl + 1], ALU.mult)
                    ev_G = b.emit("V", gu, inc=1)
                def rkx(e, dl=dl, W=W):
                    return e.tensor_tensor(
                        rk4[:, :, :, dl:CH], G4[:, :, :, dl:CH],
                        k4[:, :, :, 0:W], ALU.mult)
                ev_rk = b.emit("V", rkx, inc=1)
            if dl - 2 >= 0:
                emit_ctr(dl - 2)
            if dl < CH:
                X = PSA if dl % 2 == 0 else PSU
                if dl == 0:
                    pw = (ev_odp, ev_const_g)
                elif dl == 1:
                    pw = (E["chain"],)
                else:
                    pw = (ev_ctr[dl - 2],)
                evq = None
                for c in range(FC):
                    evq = b.emit("T", lambda e, c=c, dl=dl, W=W, X=X:
                                 e.matmul(
                        X[:, 128 * c:128 * c + NCH * W].rearrange(
                            "p (n w) -> p n w", n=NCH),
                        jm_sb, rk4[:, c, :, dl:CH], start=True, stop=True),
                        waits=(ev_rk,) + pw if c == 0 else (),
                        inc=1 if c == FC - 1 else 0)
                ev_mm[dl] = evq
        E["dfree"] = ev_G
        E["kvfree"] = ev_oacc[CH - 1]
        E["deltaG"] = ev_oacc[CH - 1]
        E["odacc"] = ev_oacc[CH - 1]
        E["psa"] = ev_oacc[CH - 2]
        E["uL"][0] = ev_oacc[CH - 1]
        E["uL"][1] = ev_oacc[CH - 1]

    def a_tail(i):
        par = i % 2
        xf = x_fm2[par]
        ev_osq = b.emit("V", lambda e: TT(
            e, onrm[:].rearrange("p c t -> p (c t)"),
            od_fm[:].rearrange("p c t -> p (c t)"),
            od_fm[:].rearrange("p c t -> p (c t)"), ALU.mult),
            waits=(E["odacc"], E["rmsT"]), inc=1)
        ev_ocp = b.emit("V", lambda e: TT(
            e, on_bf[:].rearrange("p c t -> p (c t)"),
            od_fm[:].rearrange("p c t -> p (c t)"),
            od_fm[:].rearrange("p c t -> p (c t)"), ALU.bypass), inc=1)
        ev_gs = None
        for c in range(FC):
            b.emit("T", lambda e, c=c: e.matmul(
                PSD[0:H, 0:TILE], hsel[:, c, :], fm(on_bf, c),
                start=(c == 0), stop=(c == FC - 1)),
                waits=(ev_ocp, E["psd"]) if c == 0 else ())
        for c in range(FC):
            ev_gs = b.emit("T", lambda e, c=c: e.matmul(
                PSD[0:H, TILE:2 * TILE], hsel[:, c, :], fm(onrm, c),
                start=(c == 0), stop=(c == FC - 1)),
                inc=1 if c == FC - 1 else 0)
        b.emit("V", lambda e: e.tensor_scalar(
            gst[0:H, 0, :], PSD[0:H, 0:TILE], 1.0 / DK, None, ALU.mult),
            waits=(ev_gs,))
        b.emit("V", lambda e: TT(e, gst[0:H, 1, :], gst[0:H, 0, :],
                                 gst[0:H, 0, :], ALU.mult))
        ev_gv = b.emit("V", lambda e: e.scalar_tensor_tensor(
            gst[0:H, 2, :], PSD[0:H, TILE:2 * TILE], 1.0 / DK,
            gst[0:H, 1, :], ALU.mult, ALU.subtract), inc=1)
        E["psd"] = ev_gv
        ev_gl = b.emit("S", lambda e: e.activation(
            gst[0:H, 3, :], gst[0:H, 2, :], AF.Ln, bias=epsc[0:H, 1:2]),
            waits=(ev_gv,), inc=1)
        ev_gr = b.emit("S", lambda e: e.activation(
            gst[0:H, 4, :], gst[0:H, 3, :], AF.Exp, scale=-0.5),
            waits=(ev_gl,), inc=1)
        ev_gnlast = None
        for c in range(FC):
            ps = pslot[c % 2]
            b.emit("T", lambda e, c=c, ps=ps: e.matmul(
                ps[:, 0:TILE], hmap_f[0:H, c, :], gst[0:H, 0, :],
                start=True, stop=True),
                waits=(ev_gr, ev_psb_free[c % 2]))
            ev_rb = b.emit("T", lambda e, c=c, ps=ps: e.matmul(
                ps[:, TILE:2 * TILE], hmap_f[0:H, c, :], gst[0:H, 4, :],
                start=True, stop=True), inc=1)
            def n1(e, c=c, ps=ps):
                return TT(e, fm(tmp_fm, c), fm(od_fm, c), ps[:, 0:TILE],
                          ALU.subtract)
            b.emit("V", n1, waits=(ev_rb,))
            def n2(e, c=c, ps=ps):
                return TT(e, fm(tmp_fm, c), fm(tmp_fm, c),
                          ps[:, TILE:2 * TILE], ALU.mult)
            ev_n2 = b.emit("V", n2, inc=1)
            ev_psb_free[c % 2] = ev_n2
            def n3(e, c=c):
                return e.tensor_scalar(fm(tmp_fm, c), fm(tmp_fm, c),
                                       vecs["gnw"][:, c:c + 1],
                                       vecs["gnb"][:, c:c + 1],
                                       ALU.mult, ALU.add)
            b.emit("V", n3)
            def n4(e, c=c):
                return TT(e, fm(on_bf, c), fm(tmp_fm, c), fm(g_fm, c),
                          ALU.mult)
            ev_gnlast = b.emit("V", n4, waits=(E["g"],), inc=1)
        E["gnlast"] = ev_gnlast
        ev_u1 = None
        ev_xpu = None
        for m in range(FC):
            ps = pslot[m % 2]
            evm = None
            for kb in range(FC):
                evm = b.emit("T", lambda e, m=m, kb=kb, ps=ps: e.matmul(
                    ps[:, 0:TILE], wsb["Wo"][:, kb, 128 * m:128 * m + 128],
                    fm(on_bf, kb), start=(kb == 0), stop=(kb == FC - 1)),
                    waits=(ev_gnlast, ld["Wo"], ev_psb_free[m % 2])
                    if kb == 0 else (), inc=1 if kb == FC - 1 else 0)
            def xpue(e, m=m, ps=ps, xf=xf):
                return e.scalar_tensor_tensor(
                    fm(tmp_fm, m), fm(xf, m), 2.0, ps[:, 0:TILE],
                    ALU.mult, ALU.add)
            ev_xpu = b.emit("V", xpue, waits=(evm,), inc=1)
            def u1e(e, m=m, ps=ps, xf=xf):
                return TT(e, fm(xf, m), fm(xf, m), ps[:, 0:TILE], ALU.add)
            ev_u1 = b.emit("V", u1e, inc=1)
            ev_psb_free[m % 2] = ev_u1
        rt0 = (i - WARM_TILES) * TILE
        E["su1"] = dma_y(u1_scr[:, :, rt0:rt0 + TILE], xf[:],
                         waits=(ev_u1,), chan="Ys")
        E["sxp"] = dma_y(xpu_scr[:, :, rt0:rt0 + TILE], tmp_fm[:],
                         waits=(ev_xpu, ev_u1), chan="Ys")
        E["xfree"][par] = E["su1"]

    a_front(0)
    for ti in range(NT):
        if ti >= WARM_TILES:
            a_gproj(ti)
        a_chain(ti)
        if ti >= WARM_TILES:
            a_delta(ti)
        if ti + 1 < NT:
            a_front(ti + 1)
        if ti >= WARM_TILES:
            a_tail(ti)
    ev_su1 = E["su1"]
    ev_sxp = E["sxp"]
    ev_psa_free = E["psa"]
    ev_psd_free = E["psd"]
    ev_chain = E["chain"]

    ev_phaseA_T = ("T", b.counts["T"])
    ev_phaseA_V = ("V", b.counts["V"])
    ev_phaseA_done = b.emit("V", lambda e: e.engine_nop()
                            if hasattr(e, "engine_nop") else e.nop(),
                            waits=(ev_su1, ev_sxp), inc=1)

    ar.release()
    ar.mark()
    w_f1 = ar.bf16((FC, FF))
    w_f2 = ar.bf16((FF // 128, D))
    dma_g(w_f1, dr["fw1"].rearrange("(kb p) o -> p kb o", p=128),
          waits=(ev_phaseA_T,), chan="Ga")
    dma_g(w_f2, dr["fw2"].rearrange("(kb p) o -> p kb o", p=128),
          waits=(ev_phaseA_T,), chan="Ga")
    ev_fwAll = ("Ga", b.counts["Ga"])
    ld["fw1"] = ev_fwAll
    ld["fw2"] = ev_fwAll
    uB = ar.f32(FMT)
    xpB = ar.f32(FMT)
    h2B = ar.bf16(FMT)
    hidB = ar.bf16((FF // 128, TILE))
    v2B = ar.f32(FMT)
    y1B = ar.f32(FMT)
    sqB = ar.bf16(FMT)
    rowB = ar.f32((6, TILE))

    ev_uB_free = ev_phaseA_done
    ev_xpB_free = ev_phaseA_done
    ev_y1_w = None
    for tj in range(T_CORE // TILE):
        t0 = tj * TILE
        ev_lu = dma_y(uB[:], u1_scr[:, :, t0:t0 + TILE],
                      waits=(ev_uB_free, ev_su1), chan="Yl")
        ev_lx = dma_y(xpB[:], xpu_scr[:, :, t0:t0 + TILE],
                      waits=(ev_xpB_free, ev_sxp), chan="Yl")
        def sqb(e):
            return TT(e, sqB[:].rearrange("p c t -> p (c t)"),
                      uB[:].rearrange("p c t -> p (c t)"),
                      uB[:].rearrange("p c t -> p (c t)"), ALU.mult)
        ev_sq = b.emit("V", sqb, waits=(ev_lx, ev_phaseA_done), inc=1)
        ev_sum = None
        for c in range(FC):
            ev_sum = b.emit("T", lambda e, c=c: e.matmul(
                PSD[0:1, 0:TILE], ones_c_bf, fm(sqB, c),
                start=(c == 0), stop=(c == FC - 1)),
                waits=(ev_sq, ev_psd_free) if c == 0 else (),
                inc=1 if c == FC - 1 else 0)
        def rln(e):
            return e.activation(rowB[0:1, 0, :], PSD[0:1, 0:TILE], AF.Ln,
                                bias=epsc[0:1, 0:1], scale=1.0 / D)
        ev_rl = b.emit("S", rln, waits=(ev_sum,), inc=1)
        def rex(e):
            return e.activation(rowB[0:1, 1, :], rowB[0:1, 0, :], AF.Exp,
                                scale=-0.5)
        ev_r2 = b.emit("S", rex, waits=(ev_rl,), inc=1)
        ev_psd_free = ev_r2
        ev_bc = b.emit("T", lambda e: e.matmul(
            PSC[:, 0:TILE], ones_r, rowB[0:1, 1, :], start=True, stop=True),
            waits=(ev_r2, ev_psb_free[1]), inc=1)
        ev_h2 = None
        for c in range(FC):
            ev_h2 = b.emit("V", lambda e, c=c: e.scalar_tensor_tensor(
                fm(h2B, c), fm(uB, c), vecs["ln2w"][:, c:c + 1],
                PSC[:, 0:TILE], ALU.mult, ALU.mult),
                waits=(ev_bc,), inc=1 if c == FC - 1 else 0)
        ev_psb_free[1] = ev_h2
        ev_uB_free = ev_h2
        ev_gel = None
        for m in range(FF // 128):
            ps = pslot[m % 2]
            evm = None
            for kb in range(FC):
                evm = b.emit("T", lambda e, m=m, kb=kb, ps=ps: e.matmul(
                    ps[:, 0:TILE], w_f1[:, kb, 128 * m:128 * m + 128],
                    fm(h2B, kb), start=(kb == 0), stop=(kb == FC - 1)),
                    waits=(ev_h2, ld["fw1"], ev_psb_free[m % 2])
                    if kb == 0 else (), inc=1 if kb == FC - 1 else 0)
            def gel(e, m=m, ps=ps):
                return e.activation(hidB[:, m, :], ps[:, 0:TILE], AF.Gelu)
            ev_gel = b.emit("S", gel, waits=(evm,), inc=1)
            ev_psb_free[m % 2] = ev_gel
        ev_v2 = None
        for m in range(FC):
            ps = pslot[m % 2]
            evm = None
            for kb in range(FF // 128):
                evm = b.emit("T", lambda e, m=m, kb=kb, ps=ps: e.matmul(
                    ps[:, 0:TILE], w_f2[:, kb, 128 * m:128 * m + 128],
                    hidB[:, kb, :], start=(kb == 0),
                    stop=(kb == FF // 128 - 1)),
                    waits=(ev_gel, ld["fw2"], ev_psb_free[m % 2])
                    if kb == 0 else (), inc=1 if kb == FF // 128 - 1 else 0)
            def v2e(e, m=m, ps=ps):
                return TT(e, fm(v2B, m), fm(xpB, m), ps[:, 0:TILE], ALU.add)
            ev_v2 = b.emit("V", v2e, waits=(evm, ev_lx), inc=1)
            ev_psb_free[m % 2] = ev_v2
        ev_xpB_free = ev_v2
        def sq2(e):
            return TT(e, sqB[:].rearrange("p c t -> p (c t)"),
                      v2B[:].rearrange("p c t -> p (c t)"),
                      v2B[:].rearrange("p c t -> p (c t)"), ALU.mult)
        b.emit("V", sq2, waits=(ev_v2,))
        def v2c(e):
            return TT(e, h2B[:].rearrange("p c t -> p (c t)"),
                      v2B[:].rearrange("p c t -> p (c t)"),
                      v2B[:].rearrange("p c t -> p (c t)"), ALU.bypass)
        ev_v2b = b.emit("V", v2c, inc=1)
        ev_s2 = None
        for c in range(FC):
            b.emit("T", lambda e, c=c: e.matmul(
                PSD[0:1, 0:TILE], ones_c_bf, fm(h2B, c),
                start=(c == 0), stop=(c == FC - 1)),
                waits=(ev_v2b, ev_psd_free) if c == 0 else ())
        for c in range(FC):
            ev_s2 = b.emit("T", lambda e, c=c: e.matmul(
                PSD[0:1, TILE:2 * TILE], ones_c_bf, fm(sqB, c),
                start=(c == 0), stop=(c == FC - 1)),
                inc=1 if c == FC - 1 else 0)
        def lmean(e):
            return e.tensor_scalar(rowB[0:1, 0, :], PSD[0:1, 0:TILE],
                                   1.0 / D, None, ALU.mult)
        b.emit("V", lmean, waits=(ev_s2,))
        def lm2(e):
            return TT(e, rowB[0:1, 1, :], rowB[0:1, 0, :], rowB[0:1, 0, :],
                      ALU.mult)
        b.emit("V", lm2)
        def lvar(e):
            return e.scalar_tensor_tensor(rowB[0:1, 2, :],
                                          PSD[0:1, TILE:2 * TILE],
                                          1.0 / D, rowB[0:1, 1, :],
                                          ALU.mult, ALU.subtract)
        ev_lv = b.emit("V", lvar, inc=1)
        ev_psd_free = ev_lv
        def lln(e):
            return e.activation(rowB[0:1, 3, :], rowB[0:1, 2, :], AF.Ln,
                                bias=epsc[0:1, 2:3])
        ev_ll = b.emit("S", lln, waits=(ev_lv,), inc=1)
        def lex(e):
            return e.activation(rowB[0:1, 3, :], rowB[0:1, 3, :], AF.Exp,
                                scale=-0.5)
        ev_lr = b.emit("S", lex, waits=(ev_ll,), inc=1)
        b.emit("T", lambda e: e.matmul(PSC[:, 0:TILE], ones_r, rowB[0:1, 0, :],
                                       start=True, stop=True),
               waits=(ev_lr, ev_psb_free[1]))
        ev_bb = b.emit("T", lambda e: e.matmul(
            PSC[:, TILE:2 * TILE], ones_r, rowB[0:1, 3, :],
            start=True, stop=True), inc=1)
        ev_y1t = None
        for c in range(FC):
            def za(e, c=c):
                return TT(e, fm(y1B, c), fm(v2B, c), PSC[:, 0:TILE],
                          ALU.subtract)
            b.emit("V", za, waits=(ev_bb,))
            def zb(e, c=c):
                return TT(e, fm(y1B, c), fm(y1B, c), PSC[:, TILE:2 * TILE],
                          ALU.mult)
            ev_y1t = b.emit("V", zb, inc=1)
        ev_psb_free[1] = ev_y1t
        ev_y1_w = dma_y(y1_scr[:, :, t0:t0 + TILE], y1B[:], waits=(ev_y1t,),
                        chan="Ys")
        if KDEBUG:
            dma_y(dbg_y1[:, :, t0:t0 + TILE], y1B[:], chan="Ys")
        ev_rowfree = ev_y1_w

    ev_phaseB_T = ("T", b.counts["T"])

    ar.release()
    ar.mark()
    gw_sb = ar.bf16((FC, NE))
    dma_g(gw_sb, dr["gate_w"].rearrange("(kb p) o -> p kb o", p=128),
          waits=(ev_phaseB_T,), chan="Ga")
    ld["gw"] = ("Ga", b.counts["Ga"])
    y1bf = ar.bf16((FC, HALF))
    wbuf = [ar.bf16((FC, 1024)) for _ in range(2)]
    hbuf = ar.bf16((FF // 128, HALF))
    accC = ar.f32((FC, HALF))
    gbcf = ar.f32((HALF,))
    tmpC = ar.f32((HALF,))
    lgC = ar.f32((HALF,))
    gtm = ar.f32((HALF // 128, 16))
    gden = ar.f32((16,))
    sqC = ar.bf16((FC, TILE))
    cbC = ar.bf16((FC, TILE))
    y2t = ar.f32((FC, TILE))
    y1t2 = ar.f32((FC, TILE))
    y2s = ar.f32((FC, TILE))

    NPART = 12
    wseq = []
    for half in range(2):
        for e_i in range(NE + 1):
            for part in range(NPART):
                wseq.append((half, e_i, part))

    def w_src(e_i, part):
        if part < 4:
            base = dr["mw1"][e_i] if e_i < NE else dr["ws1"]
            src = base.rearrange("(kb p) m -> p kb m", p=128)
            return src[:, :, 1024 * part:1024 * (part + 1)]
        m = part - 4
        base = dr["mw2r"][e_i, m] if e_i < NE else dr["ws2r"][m]
        return base.rearrange("(kb p) o -> p kb o", p=128)

    wld = {}
    wfree = {0: ev_phaseB_T, 1: ev_phaseB_T}

    def wv2(i):
        return wbuf[i].rearrange("p a b -> p (a b)")[:, 0:32 * 128].rearrange(
            "p (a b) -> p a b", a=32)

    def load_chunk(si):
        half, e_i, part = wseq[si]
        dst = wbuf[si % 2] if part < 4 else wv2(si % 2)
        wld[si] = dma_g(dst, w_src(e_i, part), waits=(wfree[si % 2],),
                        chan="Gm%d" % (si % 2))

    load_chunk(0)
    load_chunk(1)

    ev_accw = None
    ev_y2done = None
    ev_lgt = None
    ev_y1bf_free = None
    ev_psu_free = ev_chain
    wpair = [(pslot[0], pslot[1]), (PSTF, PSD)]
    wpair_free = [[[ev_psb_free[0]], [ev_psb_free[1]]],
                  [[E["pst"]], [ev_psd_free]]]
    for half in range(2):
        h0 = half * HALF
        ev_y1bf = dma_g(y1bf, y1_scr[:, :, h0:h0 + HALF],
                        waits=(ev_y1_w, ev_y1bf_free), chan="Gy")
        ev_lg = None
        for cg in range(2):
            evl = None
            for kb in range(FC):
                evl = b.emit("T", lambda e, kb=kb, cg=cg: e.matmul(
                    PSD[0:NE, 0:512], gw_sb[:, kb, :],
                    y1bf[:, kb, 512 * cg:512 * (cg + 1)],
                    start=(kb == 0), stop=(kb == FC - 1)),
                    waits=(ld["gw"], ev_y1bf, ev_psd_free)
                    if kb == 0 else (), inc=1 if kb == FC - 1 else 0)
            def lgb(e, cg=cg):
                return e.activation(lgC[0:NE, cg * 512:(cg + 1) * 512],
                                    PSD[0:NE, 0:512],
                                    AF.Identity, bias=gate_b[0:NE, :])
            ev_lg = b.emit("S", lgb, waits=(evl,), inc=1)
            ev_psd_free = ev_lg
        ev_lgt = None
        for tb in range(HALF // 128):
            evt = b.emit("T", lambda e, tb=tb: e.transpose(
                PSD[0:128, 0:NE], lgC[0:NE, 128 * tb:128 * tb + 128],
                id_f[0:NE, 0:NE]),
                waits=(ev_lg, ev_psd_free) if tb == 0 else (ev_lgt,), inc=1)
            def lcp(e, tb=tb):
                return TT(e, gtm[:, tb, 0:NE], PSD[0:128, 0:NE],
                          PSD[0:128, 0:NE], ALU.bypass)
            ev_lgt = b.emit("V", lcp, waits=(evt,), inc=1)
        ev_psd_free = ev_lgt
        NB = HALF // 128
        eq1v = tmpC.rearrange("p (bb e) -> p bb e", e=NE)[:, 0:NB, :]
        l2v = gbcf.rearrange("p (bb e) -> p bb e", e=NE)[:, 0:NB, :]
        def c_m1(e):
            return e.tensor_reduce(gtm[:, :, 8:9], gtm[:, :, 0:NE], AX.X,
                                   ALU.max)
        b.emit("V", c_m1, waits=(ev_lgt,))
        def c_eq1(e):
            return TT(e, eq1v, gtm[:, :, 0:NE],
                      gtm[:, :, 8:9].to_broadcast((128, NB, NE)),
                      ALU.is_equal)
        b.emit("V", c_eq1)
        def c_l2(e):
            return e.scalar_tensor_tensor(
                l2v, eq1v, -1e30, gtm[:, :, 0:NE], ALU.mult, ALU.add)
        b.emit("V", c_l2)
        def c_m2(e):
            return e.tensor_reduce(gtm[:, :, 9:10], l2v, AX.X, ALU.max)
        b.emit("V", c_m2)
        def c_eq2(e):
            return TT(e, l2v, l2v,
                      gtm[:, :, 9:10].to_broadcast((128, NB, NE)),
                      ALU.is_equal)
        b.emit("V", c_eq2)
        def c_dm(e):
            return TT(e, gtm[:, :, 10:11], gtm[:, :, 9:10], gtm[:, :, 8:9],
                      ALU.subtract)
        ev_cd = b.emit("V", c_dm, inc=1)
        def c_ex(e):
            return e.activation(
                gtm[:, :, 11:12].rearrange("p bb o -> p (bb o)"),
                gtm[:, :, 10:11].rearrange("p bb o -> p (bb o)"), AF.Tanh,
                scale=0.5)
        ev_ce = b.emit("S", c_ex, waits=(ev_cd,), inc=1)
        def c_g1(e):
            return e.tensor_scalar(gtm[:, :, 10:11], gtm[:, :, 11:12],
                                   -0.5, 0.5, ALU.mult, ALU.add)
        b.emit("V", c_g1, waits=(ev_ce,))
        def c_g2(e):
            return e.tensor_scalar(gtm[:, :, 11:12], gtm[:, :, 11:12],
                                   0.5, 0.5, ALU.mult, ALU.add)
        b.emit("V", c_g2)
        def c_w1(e):
            return TT(e, eq1v, eq1v,
                      gtm[:, :, 10:11].to_broadcast((128, NB, NE)), ALU.mult)
        b.emit("V", c_w1)
        def c_w2(e):
            return TT(e, l2v, l2v,
                      gtm[:, :, 11:12].to_broadcast((128, NB, NE)), ALU.mult)
        b.emit("V", c_w2)
        def c_cb(e):
            return TT(e, gtm[:, :, 0:NE], eq1v, l2v, ALU.add)
        ev_cb = b.emit("V", c_cb, inc=1)
        ev_cfm = None
        for tb in range(NB):
            evt = b.emit("T", lambda e, tb=tb: e.transpose(
                PSD[0:NE, 0:128], gtm[:, tb, 0:NE], id_f),
                waits=(ev_cb, ev_psd_free) if tb == 0 else (ev_cfm,), inc=1)
            def ccp(e, tb=tb):
                return TT(e, comb[0:NE, 128 * tb:128 * tb + 128],
                          PSD[0:NE, 0:128], PSD[0:NE, 0:128], ALU.bypass)
            ev_cfm = b.emit("V", ccp, waits=(evt,), inc=1)
        ev_psd_free = ev_cfm

        if KDEBUG:
            dma_y(dbg_gtm[:, 8 * half:8 * half + 8, :], gtm[:],
                  waits=(ev_cb,), chan="Ys")
            dma_y(dbg_lg[:, h0:h0 + HALF], lgC[0:NE, :], waits=(ev_lg,),
                  chan="Ys")
            dma_y(dbg_comb[:, h0:h0 + HALF], comb[0:NE, 0:HALF],
                  waits=(ev_cfm,), chan="Ys")
        def zacc(e):
            return e.memset(accC, 0.0)
        ev_acc0 = b.emit("V", zacc, waits=(ev_y2done,), inc=1)
        ev_accw = ev_acc0
        wpair_free[0][0].append(ev_psb_free[0])
        wpair_free[0][1].append(ev_psb_free[1])
        wpair_free[1][1] = [ev_psd_free]
        ev_gel = None
        for e_i in range(NE + 1):
            shared = e_i == NE
            base_si = half * (NE + 1) * NPART + e_i * NPART
            if not shared:
                b.emit("T", lambda e, e_i=e_i: e.matmul(
                    PSA[:, 0:512], esel[0:NE, e_i, :], comb[0:NE, 0:512],
                    start=True, stop=True),
                    waits=(ev_cfm, ev_psa_free,), inc=0)
                ev_gb = b.emit("T", lambda e, e_i=e_i: e.matmul(
                    PSA[:, 512:1024], esel[0:NE, e_i, :],
                    comb[0:NE, 512:1024], start=True, stop=True), inc=1)
                def gcp(e):
                    return TT(e, gbcf, PSA[:], PSA[:], ALU.bypass)
                ev_gbc = b.emit("V", gcp, waits=(ev_gb,), inc=1)
                ev_psa_free = ev_gbc
            for part in range(4):
                si = base_si + part
                wv = wbuf[si % 2]
                for mm in range(FC):
                    m = FC * part + mm
                    pr = mm % 2
                    pp = wpair[pr]
                    evm = {}
                    for kb in range(FC):
                        for cg in range(2):
                            evm[cg] = b.emit("T", lambda e, wv=wv, mm=mm,
                                             cg=cg, kb=kb, pp=pp: e.matmul(
                                pp[cg][:, 0:512],
                                wv[:, kb, 128 * mm:128 * mm + 128],
                                y1bf[:, kb, 512 * cg:512 * (cg + 1)],
                                start=(kb == 0), stop=(kb == FC - 1)),
                                waits=(wld[si], ev_y1bf, *wpair_free[pr][0],
                                       *wpair_free[pr][1])
                                if (kb == 0 and cg == 0) else (),
                                inc=1 if kb == FC - 1 else 0)
                    ev_y1bf_free = evm[1]
                    for cg in range(2):
                        def gel(e, m=m, cg=cg, e_i=e_i, shared=shared,
                                pp=pp):
                            bias_ap = (bs1[:, m:m + 1] if shared
                                       else b1v[e_i][:, m:m + 1])
                            return e.activation(
                                hbuf[:, m, 512 * cg:512 * (cg + 1)],
                                pp[cg][:, 0:512], AF.Gelu, bias=bias_ap)
                        ev_gel = b.emit("S", gel, waits=(evm[cg],), inc=1)
                        wpair_free[pr][cg] = [ev_gel]
                wfree[si % 2] = ev_gel
                if si + 2 < len(wseq):
                    load_chunk(si + 2)
            for m in range(FC):
                si = base_si + 4 + m
                wv = wv2(si % 2)
                psm = PSA if m % 2 == 0 else PSU
                evm = None
                for kb in range(32):
                    for cg in range(2):
                        evm = b.emit("T", lambda e, wv=wv, m=m, psm=psm,
                                     cg=cg, kb=kb: e.matmul(
                            psm[:, 512 * cg:512 * (cg + 1)],
                            wv[:, kb, :],
                            hbuf[:, kb, 512 * cg:512 * (cg + 1)],
                            start=(kb == 0), stop=(kb == 31)),
                            waits=(wld[si], ev_gel,
                                   ev_psa_free if m % 2 == 0
                                   else ev_psu_free)
                            if (kb == 0 and cg == 0) else (),
                            inc=1 if (kb == 31 and cg == 1) else 0)
                if shared:
                    def accs(e, m=m, psm=psm):
                        return e.scalar_tensor_tensor(
                            fm(accC, m), psm[:], bs2[:, m:m + 1],
                            fm(accC, m), ALU.add, ALU.add)
                    ev_accw = b.emit("V", accs, waits=(evm, ev_accw), inc=1)
                else:
                    def acc1(e, m=m, e_i=e_i, psm=psm):
                        return e.scalar_tensor_tensor(
                            tmpC, psm[:], b2v[e_i][:, m:m + 1], gbcf,
                            ALU.add, ALU.mult)
                    b.emit("V", acc1, waits=(evm, ev_gbc, ev_accw))
                    def acc2(e, m=m):
                        return TT(e, fm(accC, m), fm(accC, m), tmpC, ALU.add)
                    ev_accw = b.emit("V", acc2, inc=1)
                if m % 2 == 0:
                    ev_psa_free = ev_accw
                else:
                    ev_psu_free = ev_accw
                wfree[si % 2] = ev_accw
                if si + 2 < len(wseq):
                    load_chunk(si + 2)

        ev_psd_free = wpair_free[1][1][-1]
        if KDEBUG:
            dma_y(dbg_acc[:, :, h0:h0 + HALF], accC, waits=(ev_accw,),
                  chan="Ys")
        ev_ydma = None
        ev_y1t2_free = None
        for tt in range(HALF // TILE):
            s0 = tt * TILE
            ev_ly = dma_y(y1t2, y1_scr[:, :, h0 + s0:h0 + s0 + TILE],
                          waits=(ev_accw, ev_y1t2_free), chan="Yl")
            def v3(e, s0=s0):
                return TT(e, y2s, accC[:, :, s0:s0 + TILE], y1t2, ALU.add)
            ev_v3 = b.emit("V", v3, waits=(ev_ly, ev_accw, ev_ydma), inc=1)
            ev_y1t2_free = ev_v3
            def sqx(e):
                return TT(e, sqC.rearrange("p c t -> p (c t)"),
                          y2s.rearrange("p c t -> p (c t)"),
                          y2s.rearrange("p c t -> p (c t)"), ALU.mult)
            b.emit("V", sqx)
            def cbx(e):
                return TT(e, cbC.rearrange("p c t -> p (c t)"),
                          y2s.rearrange("p c t -> p (c t)"),
                          y2s.rearrange("p c t -> p (c t)"), ALU.bypass)
            ev_c = b.emit("V", cbx, inc=1)
            ev_s = None
            for c in range(FC):
                b.emit("T", lambda e, c=c: e.matmul(
                    PSD[0:1, 0:TILE], ones_c_bf, cbC[:, c, :],
                    start=(c == 0), stop=(c == FC - 1)),
                    waits=(ev_c, ev_psd_free) if c == 0 else ())
            for c in range(FC):
                ev_s = b.emit("T", lambda e, c=c: e.matmul(
                    PSD[0:1, TILE:2 * TILE], ones_c_bf, sqC[:, c, :],
                    start=(c == 0), stop=(c == FC - 1)),
                    inc=1 if c == FC - 1 else 0)
            def pm(e):
                return e.tensor_scalar(rrow[0:1, 0, :], PSD[0:1, 0:TILE],
                                       1.0 / D, None, ALU.mult)
            b.emit("V", pm, waits=(ev_s,))
            def pm2(e):
                return TT(e, rrow[0:1, 1, :], rrow[0:1, 0, :],
                          rrow[0:1, 0, :], ALU.mult)
            b.emit("V", pm2)
            def pv(e):
                return e.scalar_tensor_tensor(rrow[0:1, 2, :],
                                              PSD[0:1, TILE:2 * TILE],
                                              1.0 / D, rrow[0:1, 1, :],
                                              ALU.mult, ALU.subtract)
            ev_pv = b.emit("V", pv, inc=1)
            ev_psd_free = ev_pv
            def pl(e):
                return e.activation(rrow[0:1, 3, :], rrow[0:1, 2, :], AF.Ln,
                                    bias=epsc[0:1, 2:3])
            ev_pl = b.emit("S", pl, waits=(ev_pv,), inc=1)
            def pe(e):
                return e.activation(rrow[0:1, 3, :], rrow[0:1, 3, :],
                                    AF.Exp, scale=-0.5)
            ev_pr = b.emit("S", pe, waits=(ev_pl,), inc=1)
            b.emit("T", lambda e: e.matmul(PSC[:, 0:TILE], ones_r,
                                           rrow[0:1, 0, :], start=True,
                                           stop=True),
                   waits=(ev_pr, ev_psb_free[1]))
            ev_bb = b.emit("T", lambda e: e.matmul(
                PSC[:, TILE:2 * TILE], ones_r, rrow[0:1, 3, :],
                start=True, stop=True), inc=1)
            ev_y2 = None
            for c in range(FC):
                def za(e, c=c):
                    return TT(e, y2t[:, c, :], y2s[:, c, :],
                              PSC[:, 0:TILE], ALU.subtract)
                b.emit("V", za, waits=(ev_bb,))
                def zb(e, c=c):
                    return TT(e, y2t[:, c, :], y2t[:, c, :],
                              PSC[:, TILE:2 * TILE], ALU.mult)
                ev_y2 = b.emit("V", zb, inc=1)
            ev_psb_free[1] = ev_y2
            evt = None
            for c in range(FC):
                evt = b.emit("T", lambda e, c=c: e.transpose(
                    PSA[:, 128 * c:128 * c + 128], y2t[:, c, :], id_f),
                    waits=(ev_y2, ev_psa_free) if c == 0 else (),
                    inc=1 if c == FC - 1 else 0)
            def ytc(e):
                return TT(e, y2s.rearrange("p c t -> p (c t)"), PSA[:],
                          PSA[:], ALU.bypass)
            ev_yc = b.emit("V", ytc, waits=(evt,), inc=1)
            ev_psa_free = ev_yc
            r0 = h0 + s0
            ev_ydma = dma_y(y_d[r0:r0 + 128, :],
                            y2s.rearrange("p c t -> p (c t)"),
                            waits=(ev_yc,), chan="Yo")
            ev_y2done = ev_ydma

    b.emit("Y", lambda e: e.nop(), waits=(("Yo", b.counts["Yo"]),
                                          ("Ys", b.counts["Ys"]),))

    with nc.Block() as block:
        b.materialize(block)
    ar.ap = None
    ar.t.__exit__(None, None, None)
    for c in reversed(ctxs):
        c.__exit__(None, None, None)
    return nc


def _host_consts():
    c = {}
    c["ident"] = np.eye(128, dtype=np.float32)
    c["ones_r"] = np.ones((1, 128), np.float32)
    c["ones_c"] = np.ones((128, 1), np.float32)
    hs = np.zeros((128, FC, H), np.float32)
    hm = np.zeros((H, FC, 128), np.float32)
    for cc in range(FC):
        for p in range(128):
            hh = (128 * cc + p) // DK
            hs[p, cc, hh] = 1.0
            hm[hh, cc, p] = 1.0
    c["headsel"] = hs
    c["headmap"] = hm
    jm = np.zeros((128, 128), np.float32)
    jm[:64, :64] = 1.0
    jm[64:, 64:] = 1.0
    c["jmat"] = jm
    es = np.zeros((NE, NE, 128), np.float32)
    for e in range(NE):
        es[e, e, :] = 1.0
    c["esel"] = es
    r1 = np.zeros((128, FC * TILE), np.float32)
    r1[:, ::CH] = 1.0
    c["reset1"] = r1
    return c


def _vec_tile(v):
    return np.ascontiguousarray(np.asarray(v, np.float32).reshape(FC, 128).T)


_CACHE = {}


def kernel(**inp):
    if "nc" not in _CACHE:
        _CACHE["nc"] = build_program()
    nc = _CACHE["nc"]

    x = np.ascontiguousarray(np.asarray(inp["L0_attn_x"], np.float32)[0])
    base = {
        "Wr": inp["L0_attn_Wr"], "Wk": inp["L0_attn_Wk"],
        "Wv": inp["L0_attn_Wv"], "Wg": inp["L0_attn_Wg"],
        "Wo": inp["L0_attn_Wo"], "wA": inp["L0_attn_wA"],
        "wB": inp["L0_attn_wB"], "fw1": inp["L0_attn_ffn_w1"],
        "fw2": inp["L0_attn_ffn_w2"], "mw1": inp["L0_ffn_w1"],
        "mw2r": np.ascontiguousarray(
            np.asarray(inp["L0_ffn_w2"], np.float32).reshape(
                NE, FF, FC, 128).transpose(0, 2, 1, 3)),
        "ws1": inp["L0_ffn_ws1"],
        "ws2r": np.ascontiguousarray(
            np.asarray(inp["L0_ffn_ws2"], np.float32).reshape(
                FF, FC, 128).transpose(1, 0, 2)),
        "gate_w": inp["L0_ffn_gate_w"],
        "ln1w": _vec_tile(inp["L0_attn_ln1_w"]),
        "ln2w": _vec_tile(inp["L0_attn_ln2_w"]),
        "mur": _vec_tile(inp["L0_attn_mu_r"]),
        "muk": _vec_tile(inp["L0_attn_mu_k"]),
        "muv": _vec_tile(inp["L0_attn_mu_v"]),
        "muw": _vec_tile(inp["L0_attn_mu_w"]),
        "mug": _vec_tile(inp["L0_attn_mu_g"]),
        "w0": _vec_tile(inp["L0_attn_w0"]),
        "uvec": _vec_tile(np.asarray(inp["L0_attn_u"],
                                     np.float32).reshape(-1)),
        "gnw": _vec_tile(np.asarray(inp["L0_attn_gn_w"],
                                    np.float32).reshape(-1)),
        "gnb": _vec_tile(np.asarray(inp["L0_attn_gn_b"],
                                    np.float32).reshape(-1)),
        "gate_b": np.asarray(inp["L0_ffn_gate_b"],
                             np.float32).reshape(NE, 1),
        "b1": np.ascontiguousarray(
            np.asarray(inp["L0_ffn_b1"], np.float32).reshape(
                NE, FF // 128, 128).transpose(0, 2, 1)),
        "b2": np.ascontiguousarray(
            np.asarray(inp["L0_ffn_b2"], np.float32).reshape(
                NE, FC, 128).transpose(0, 2, 1)),
        "bs1": np.ascontiguousarray(
            np.asarray(inp["L0_ffn_bs1"], np.float32).reshape(
                FF // 128, 128).T),
        "bs2": np.ascontiguousarray(
            np.asarray(inp["L0_ffn_bs2"], np.float32).reshape(FC, 128).T),
    }
    base.update(_host_consts())
    base = {k: np.ascontiguousarray(np.asarray(v, np.float32))
            for k, v in base.items()}

    in_maps = []
    for ci in range(NCORES):
        s0 = ci * T_CORE - T_WARM
        if s0 < 0:
            xs = np.concatenate(
                [np.zeros((-s0, D), np.float32), x[0:s0 + T_ALL]], axis=0)
        else:
            xs = x[s0:s0 + T_ALL]
        m = dict(base)
        m["x"] = np.ascontiguousarray(xs)
        in_maps.append(m)

    res = run_bass_kernel_spmd(nc, in_maps, core_ids=list(range(NCORES)))
    out = np.concatenate([r["y"] for r in res.results], axis=0)
    return out[None, :, :]


# version 14
# speedup vs baseline: 1.3506x; 1.0360x over previous
import os
import numpy as np
import concourse.bass as bass
import concourse.mybir as mybir
from concourse.bass_utils import run_bass_kernel_spmd

F32 = mybir.dt.float32
BF16 = mybir.dt.bfloat16
ALU = mybir.AluOpType
AF = mybir.ActivationFunctionType
AX = mybir.AxisListType

D = 1024
H = 16
DK = 64
FC = 8
LORA = 64
FF = 4096
NE = 8
NCORES = 8
T_CORE = 2048
T_WARM = 512
T_ALL = T_WARM + T_CORE
TILE = 128
NT = T_ALL // TILE
WARM_TILES = T_WARM // TILE
CH = 16
NCH = TILE // CH
HALF = 1024
EPS_RMS = 1e-6
EPS_GN = 1e-5
EPS_LN = 1e-5

ENGS = ("T", "V", "S", "G", "Y")
CHAN_BODY = {"T": "T", "V": "V", "S": "S", "G": "G", "Y": "Y",
             "Yx": "Y", "Ys": "Y", "Yl": "Y", "Yo": "Y", "Yc": "Y",
             "Gc": "G", "Ga": "G", "Gm0": "G", "Gm1": "G", "Gy": "G"}


class Prog:
    def __init__(self, nc):
        self.nc = nc
        self.ops = []
        self.counts = {c: 0 for c in CHAN_BODY}
        self.sems = {}

    def emit(self, chan, fn, waits=(), inc=0):
        waits = tuple(w for w in waits if w is not None)
        self.ops.append((chan, fn, waits, inc))
        if inc:
            self.counts[chan] += inc
            return (chan, self.counts[chan])
        return None

    def materialize(self, block):
        for c in CHAN_BODY:
            self.sems[c] = self.nc.semaphore("sem_" + c).__enter__()
        watermark = {}

        def body_for(eng_name):
            def body(eng):
                for (oc, fn, waits, inc) in self.ops:
                    if CHAN_BODY[oc] != eng_name:
                        continue
                    for (pc, val) in waits:
                        key = (eng_name, pc)
                        if watermark.get(key, -1) >= val:
                            continue
                        watermark[key] = val
                        eng.wait_ge(self.sems[pc], val)
                    inst = fn(eng)
                    if inc:
                        assert inst is not None
                        inst.then_inc(self.sems[oc], inc)
            return body

        block.tensor(body_for("T"))
        block.vector(body_for("V"))
        block.scalar(body_for("S"))
        block.gpsimd(body_for("G"))
        block.sync(body_for("Y"))


class Arena:

    def __init__(self, nc, cols=52992):
        self.t = nc.sbuf_tensor("arena", [128, cols], F32)
        self.ap = self.t.__enter__()
        self.cols = cols
        self.off = 0
        self.marks = []

    def alloc(self, ncols_f32):
        a = self.off
        self.off += ncols_f32
        assert self.off <= self.cols, f"SBUF arena overflow: {self.off}"
        return a

    def f32(self, shape):
        n = int(np.prod(shape))
        a = self.alloc(n)
        v = self.ap[:, a:a + n]
        if len(shape) == 1:
            return v
        return v.rearrange("p (a b) -> p a b", a=shape[0]) if len(shape) == 2 \
            else v.rearrange("p (a b c) -> p a b c", a=shape[0], b=shape[1])

    def bf16(self, shape):
        n = int(np.prod(shape))
        a = self.alloc((n + 1) // 2)
        v = self.ap[:, a:a + (n + 1) // 2].bitcast(BF16)
        if len(shape) == 1:
            return v[:, 0:n]
        return v[:, 0:n].rearrange("p (a b) -> p a b", a=shape[0]) \
            if len(shape) == 2 else \
            v[:, 0:n].rearrange("p (a b c) -> p a b c", a=shape[0], b=shape[1])

    def mark(self):
        self.marks.append(self.off)

    def release(self):
        self.off = self.marks.pop()


def build_program():
    nc = bass.Bass("TRN2", target_bir_lowering=False, debug=False)
    b = Prog(nc)
    dr = {}

    def din(name, shape):
        dr[name] = nc.dram_tensor(name, list(shape), F32,
                                  kind="ExternalInput").ap()
        return dr[name]

    x_d = din("x", (T_ALL, D))
    for nm in ("Wr", "Wk", "Wv", "Wg", "Wo"):
        din(nm, (D, D))
    din("wA", (D, LORA))
    din("wB", (LORA, D))
    din("fw1", (D, FF))
    din("fw2", (FF, D))
    din("mw1", (NE, D, FF))
    din("mw2r", (NE, FC, FF, 128))
    din("ws1", (D, FF))
    din("ws2r", (FC, FF, 128))
    din("gate_w", (D, NE))
    for nm in ("ln1w", "ln2w", "mur", "muk", "muv", "muw", "mug", "w0",
               "uvec", "gnw", "gnb"):
        din(nm, (128, FC))
    din("gate_b", (NE, 1))
    din("b1", (NE, 128, FF // 128))
    din("b2", (NE, 128, FC))
    din("bs1", (128, FF // 128))
    din("bs2", (128, FC))
    din("ident", (128, 128))
    din("ones_r", (1, 128))
    din("ones_c", (128, 1))
    din("headsel", (128, FC, H))
    din("esel", (NE, NE, 128))
    din("headmap", (H, FC, 128))
    din("jmat", (128, 128))
    din("reset1", (128, FC * TILE))

    y_d = nc.dram_tensor("y", [T_CORE, D], F32, kind="ExternalOutput").ap()
    u1_scr = nc.dram_tensor("u1_scr", [128, FC, T_CORE], F32,
                            kind="Internal").ap()
    xpu_scr = nc.dram_tensor("xpu_scr", [128, FC, T_CORE], F32,
                             kind="Internal").ap()
    y1_scr = nc.dram_tensor("y1_scr", [128, FC, T_CORE], F32,
                            kind="Internal").ap()
    KDEBUG = os.environ.get("KDEBUG", "") == "1"
    if KDEBUG:
        dbg_lg = nc.dram_tensor("dbg_lg", [NE, T_CORE], F32,
                                kind="ExternalOutput").ap()
        dbg_comb = nc.dram_tensor("dbg_comb", [NE, T_CORE], F32,
                                  kind="ExternalOutput").ap()
        dbg_acc = nc.dram_tensor("dbg_acc", [128, FC, T_CORE], F32,
                                 kind="ExternalOutput").ap()
        dbg_u1 = nc.dram_tensor("dbg_u1", [128, FC, T_CORE], F32,
                                kind="ExternalOutput").ap()
        dbg_y1 = nc.dram_tensor("dbg_y1", [128, FC, T_CORE], F32,
                                kind="ExternalOutput").ap()
        dbg_gtm = nc.dram_tensor("dbg_gtm", [128, 16, 16], F32,
                                 kind="ExternalOutput").ap()

    ctxs = []

    def psum(nm, shp, dt=F32):
        t = nc.psum_tensor(nm, list(shp), dt)
        ctxs.append(t)
        return t.__enter__()

    PSA = psum("PSA", (128, 1024), F32)
    PSU = psum("PSU", (128, 1024), F32)
    PSB = psum("PSB", (128, 512), F32)
    PSC = psum("PSC", (128, 512), F32)
    PSD = psum("PSD", (128, 512), F32)
    PSTF = psum("PST", (128, 512), F32)
    PST = PSTF.bitcast(BF16)

    ar = Arena(nc)

    id_f = ar.f32((128,))
    id_bf = ar.bf16((128,))
    ones_r = ar.f32((128,))[0:1, :]
    ones_c_bf = ar.bf16((2,))[:, 0:1]
    jm_sb = ar.bf16((128,))
    esel = ar.f32((NE, 128))
    vecs = {nm: ar.f32((FC,)) for nm in
            ("ln1w", "ln2w", "mur", "muk", "muv", "muw", "mug", "w0",
             "uvec", "gnw", "gnb")}
    gate_b = ar.f32((1,))
    bs1 = ar.f32((FF // 128,))
    bs2 = ar.f32((FC,))
    b1v = [ar.f32((FF // 128,)) for _ in range(NE)]
    b2v = [ar.f32((FC,)) for _ in range(NE)]
    epsc = ar.f32((4,))
    rrow = ar.f32((6, TILE))
    comb = ar.f32((HALF,))

    def dma_y(dst, src, waits=(), chan="Yc"):
        return b.emit(chan, lambda e, d=dst, s=src: e.dma_start(d, s),
                      waits=waits, inc=16)

    def dma_g(dst, src, waits=(), chan="Gc"):
        return b.emit(chan, lambda e, d=dst, s=src: e.dma_start(d, s),
                      waits=waits, inc=16)

    ld = {}
    dma_y(id_f, dr["ident"])
    dma_g(id_bf, dr["ident"])
    dma_y(ones_r, dr["ones_r"])
    dma_g(ones_c_bf, dr["ones_c"])
    dma_g(jm_sb, dr["jmat"])
    dma_y(esel[0:NE, :, :], dr["esel"])
    for nm in vecs:
        dma_y(vecs[nm], dr[nm])
    dma_y(gate_b[0:NE, :], dr["gate_b"])
    dma_y(bs1, dr["bs1"])
    dma_y(bs2, dr["bs2"])
    for e_i in range(NE):
        dma_y(b1v[e_i], dr["b1"][e_i])
        dma_y(b2v[e_i], dr["b2"][e_i])

    ar.mark()
    hsel = ar.bf16((FC, H))
    hmap_f = ar.f32((FC, 128))
    reset1 = ar.f32((FC * TILE,))
    S = ar.f32((FC, DK))
    h_bnd = ar.f32((FC, 1))
    gst = ar.f32((6, TILE))
    wlf = ar.f32((TILE,))
    wlt = ar.bf16((TILE,))
    dma_g(hsel, dr["headsel"])
    dma_y(hmap_f[0:H, :, :], dr["headmap"])
    dma_y(reset1, dr["reset1"])
    ev_const_y = ("Yc", b.counts["Yc"])
    ev_const_g = ("Gc", b.counts["Gc"])
    wsb = {}
    for nm in ("Wr", "Wk", "Wv", "Wg", "Wo"):
        wsb[nm] = ar.bf16((FC, D))
        dma_g(wsb[nm], dr[nm].rearrange("(kb p) o -> p kb o", p=128),
              chan="Ga")
    w_lA = ar.bf16((FC, LORA))
    dma_g(w_lA, dr["wA"].rearrange("(kb p) o -> p kb o", p=128), chan="Ga")
    w_lB = ar.bf16((D,))
    dma_g(w_lB[0:LORA, :], dr["wB"], chan="Ga")
    ev_wAll = ("Ga", b.counts["Ga"])
    for nm in ("Wr", "Wk", "Wv", "Wg", "Wo", "wA", "wB"):
        ld[nm] = ev_wAll

    FMT = (FC, TILE)
    x_tm = ar.f32((D,))
    x_fm2 = [ar.f32(FMT), ar.f32(FMT)]
    h_fm = ar.f32(FMT)
    xx_fm = ar.f32(FMT)
    xmix = {nm: ar.bf16(FMT) for nm in ("xr", "xk", "xv", "xw", "xg")}
    r_fm = ar.f32(FMT)
    k_fm = ar.f32(FMT)
    v_fm = ar.f32(FMT)
    d_fm = ar.f32(FMT)
    g_fm = ar.f32(FMT)
    e_fm = ar.f32(FMT)
    abuf = ar.f32(FMT)
    A_ch = ar.f32((FC, NCH))
    rt_fm = ar.f32(FMT)
    kh_e = ar.bf16(FMT)
    kh_o = ar.bf16(FMT)
    khtm = [ar.bf16((FC, 128)) for _ in range(2)]
    vtm = ar.bf16((FC, 128))
    G_fm = ar.f32(FMT)
    rk_bf = ar.bf16(FMT)
    od_fm = ar.f32(FMT)
    tmp_fm = ar.f32(FMT)
    onrm = ar.bf16(FMT)
    on_bf = ar.bf16(FMT)
    sig_t = tmp_fm

    def TT(e, out, a, bb, op):
        if op == ALU.bypass:
            return e.tensor_copy(out, a)
        return e.tensor_tensor(out, a, bb, op)

    def fm(t, c=None, sl=None):
        a = t
        if c is not None:
            a = a[:, c, :]
        if sl is not None:
            a = a[:, sl] if c is not None else a[:, :, sl]
        return a

    def _init(e):
        e.memset(epsc[:, 0:1], EPS_RMS)
        e.memset(epsc[:, 1:2], EPS_GN)
        e.memset(epsc[:, 2:3], EPS_LN)
        e.memset(S, 0.0)
        e.memset(h_bnd, 0.0)
        e.memset(abuf, 0.0)
        e.memset(kh_e, 0.0)
        return e.memset(kh_o, 0.0)
    ev_init = b.emit("V", _init, inc=1)

    ev_chain = ev_init
    ev_u_last = {}
    ev_tile_T_done = None
    ev_xfm_free = ev_init
    ev_oevac = None
    ev_psa_free = ev_init
    ev_pst_free = ev_init
    ev_psb_free = {0: ev_init, 1: ev_init}
    ev_psd_free = ev_init
    ev_rowfree = ev_init

    resetv = reset1

    E = dict(chain=ev_init, psa=ev_init, pst=ev_init, psd=ev_init,
             xtmfree=None, hbnd=ev_init, su1=None, sxp=None,
             l=None, gp=None, gnlast=None, g=None, dfree=ev_init,
             kvfree=ev_init, tm=ev_init, rt=None, rtfree=ev_init,
             rmsT=None, odpre=None, rk=None, deltaG=ev_init)
    E["xfree"] = {0: ev_init, 1: ev_init}
    E["uL"] = {}
    pslot = [PSB, PSC]

    def a_front(i):
        warm = i < WARM_TILES
        par = i % 2
        xf = x_fm2[par]
        xff = xf.rearrange("p c t -> p (c t)")
        t0 = i * TILE
        ev_x = dma_y(x_tm, x_d[t0:t0 + TILE, :],
                     waits=(E["xtmfree"], ev_const_y), chan="Yx")
        ev_xev = None
        for hf in range(2):
            evt = None
            for j in range(4):
                c = 4 * hf + j
                evt = b.emit("T", lambda e, c=c, j=j: e.transpose(
                    PSD[:, 128 * j:128 * j + 128],
                    x_tm[:, 128 * c:128 * c + 128], id_f),
                    waits=(ev_x, E["psd"]) if j == 0 else (),
                    inc=1 if j == 3 else 0)
            E["xtmfree"] = evt
            def xevac(e, hf=hf, xff=xff):
                return e.copy(xff[:, 512 * hf:512 * hf + 512], PSD[:, 0:512])
            ev_xev = b.emit("S", xevac,
                            waits=(evt, E["xfree"][par]), inc=1)
            E["psd"] = ev_xev
        def sq(e, xff=xff):
            return TT(e, onrm[:].rearrange("p c t -> p (c t)"), xff, xff,
                      ALU.mult)
        ev_sq = b.emit("V", sq, waits=(ev_xev, E["rmsT"]), inc=1)
        ev_sum = None
        for c in range(FC):
            ev_sum = b.emit("T", lambda e, c=c: e.matmul(
                PSD[0:1, 0:TILE], ones_c_bf, fm(onrm, c),
                start=(c == 0), stop=(c == FC - 1)),
                waits=(ev_sq, E["psd"], ev_const_g) if c == 0 else (),
                inc=1 if c == FC - 1 else 0)
        E["rmsT"] = ev_sum
        ev_rl = b.emit("S", lambda e: e.activation(
            rrow[0:1, 0, :], PSD[0:1, 0:TILE], AF.Ln,
            bias=epsc[0:1, 0:1], scale=1.0 / D), waits=(ev_sum,), inc=1)
        E["psd"] = ev_rl
        ev_r2 = b.emit("S", lambda e: e.activation(
            rrow[0:1, 1, :], rrow[0:1, 0, :], AF.Exp, scale=-0.5),
            waits=(ev_rl,), inc=1)
        ev_bc = b.emit("T", lambda e: e.matmul(
            PSC[:, 0:TILE], ones_r, rrow[0:1, 1, :], start=True, stop=True),
            waits=(ev_r2, ev_psb_free[1]), inc=1)
        ev_h = None
        for c in range(FC):
            ev_h = b.emit("V", lambda e, c=c, xf=xf: e.scalar_tensor_tensor(
                fm(h_fm, c), fm(xf, c), vecs["ln1w"][:, c:c + 1],
                PSC[:, 0:TILE], ALU.mult, ALU.mult),
                waits=(ev_bc,), inc=1 if c == FC - 1 else 0)
        ev_psb_free[1] = ev_h
        b.emit("V", lambda e: e.tensor_sub(
            xx_fm[:, :, 1:TILE], h_fm[:, :, 0:TILE - 1], h_fm[:, :, 1:TILE]),
            waits=(ev_h,))
        b.emit("V", lambda e: e.tensor_sub(
            xx_fm[:, :, 0:1], h_bnd, h_fm[:, :, 0:1]))
        b.emit("V", lambda e: TT(e, h_bnd, h_fm[:, :, TILE - 1:TILE],
                                 h_fm[:, :, TILE - 1:TILE], ALU.bypass))
        mixes = [("xk", "muk"), ("xv", "muv"), ("xw", "muw")]
        if not warm:
            mixes.append(("xr", "mur"))
        ev_mix = None
        for nm, mu in mixes:
            for c in range(FC):
                ev_mix = b.emit("V", lambda e, nm=nm, mu=mu, c=c:
                                e.scalar_tensor_tensor(
                                    fm(xmix[nm], c), fm(xx_fm, c),
                                    vecs[mu][:, c:c + 1], fm(h_fm, c),
                                    ALU.mult, ALU.add),
                                waits=(E["l"],) if (nm == mixes[0][0]
                                                   and c == 0) else (),
                                inc=1 if c == FC - 1 else 0)

        def proj(dst_fm, srcb, wname, efree):
            evs = None
            for m in range(FC):
                ps = pslot[m % 2]
                evm = None
                for kb in range(FC):
                    evm = b.emit("T", lambda e, m=m, kb=kb, w=wname, ps=ps:
                                 e.matmul(
                                     ps[:, 0:TILE],
                                     wsb[w][:, kb, 128 * m:128 * m + 128],
                                     fm(xmix[srcb], kb),
                                     start=(kb == 0), stop=(kb == FC - 1)),
                                 waits=(ev_mix, ld[wname],
                                        ev_psb_free[m % 2]) if kb == 0
                                 else (), inc=1 if kb == FC - 1 else 0)
                def evac(e, m=m, ps=ps, dst=dst_fm):
                    return e.copy(fm(dst, m), ps[:, 0:TILE])
                eve = b.emit("S", evac,
                             waits=(evm, efree) if m == 0 else (evm,), inc=1)
                ev_psb_free[m % 2] = eve
                evs = eve
            return evs

        ev_k = proj(k_fm, "xk", "Wk", E["kvfree"])
        ev_v = proj(v_fm, "xv", "Wv", E["kvfree"])
        if not warm:
            ev_r = proj(r_fm, "xr", "Wr", E["kvfree"])
        ev_l = None
        for kb in range(FC):
            ev_l = b.emit("T", lambda e, kb=kb: e.matmul(
                PSD[0:LORA, 0:TILE], w_lA[:, kb, :], fm(xmix["xw"], kb),
                start=(kb == 0), stop=(kb == FC - 1)),
                waits=(ev_mix, ld["wA"], E["psd"]) if kb == 0 else (),
                inc=1 if kb == FC - 1 else 0)
        E["l"] = ev_l
        ev_th = b.emit("S", lambda e: e.activation(
            wlt[0:LORA, :], PSD[0:LORA, 0:TILE], AF.Tanh),
            waits=(ev_l,), inc=1)
        E["psd"] = ev_th
        ev_wp = None
        for m in range(FC):
            ps = pslot[m % 2]
            ev_mm = b.emit("T", lambda e, m=m, ps=ps: e.matmul(
                ps[:, 0:TILE], w_lB[0:LORA, 128 * m:128 * m + 128],
                wlt[0:LORA, :], start=True, stop=True),
                waits=(ev_th, ev_psb_free[m % 2]), inc=1)
            def d_e1(e, m=m, ps=ps):
                return e.activation(fm(d_fm, m), ps[:, 0:TILE], AF.Exp,
                                    bias=vecs["w0"][:, m:m + 1])
            ev_wp = b.emit("S", d_e1,
                           waits=(ev_mm, E["dfree"]) if m == 0
                           else (ev_mm,), inc=1)
            ev_psb_free[m % 2] = ev_wp
        ev_d = b.emit("S", lambda e: e.activation(
            d_fm[:].rearrange("p c t -> p (c t)"),
            d_fm[:].rearrange("p c t -> p (c t)"), AF.Exp, scale=-1.0),
            waits=(ev_wp, ("S", b.counts["S"])), inc=1)
        d4 = d_fm[:].rearrange("p c (n l) -> p c n l", l=CH)
        a4 = abuf[:].rearrange("p c (n l) -> p c n l", l=CH)
        e4 = e_fm[:].rearrange("p c (n l) -> p c n l", l=CH)
        ev_af = b.emit("V", lambda e: TT(
            e, a4[:, :, :, 1:CH], d4[:, :, :, 0:CH - 1],
            d4[:, :, :, 0:CH - 1], ALU.bypass), waits=(ev_d,), inc=1)
        ev_E = b.emit("V", lambda e: e.tensor_tensor_scan(
            e_fm[:].rearrange("p c t -> p (c t)"),
            abuf[:].rearrange("p c t -> p (c t)"),
            resetv, 0.0, ALU.mult, ALU.add), waits=(ev_af,), inc=1)
        ev_A = b.emit("V", lambda e: TT(
            e, A_ch[:].rearrange("p c (n o) -> p c n o", o=1),
            e4[:, :, :, CH - 1:CH], d4[:, :, :, CH - 1:CH], ALU.mult),
            inc=1)
        E["A"] = ev_A
        if not warm:
            ev_rt = b.emit("V", lambda e: TT(
                e, rt_fm[:].rearrange("p c t -> p (c t)"),
                r_fm[:].rearrange("p c t -> p (c t)"),
                e_fm[:].rearrange("p c t -> p (c t)"), ALU.mult),
                waits=(ev_r, E["rtfree"]), inc=1)
            E["rt"] = ev_rt
        ev_a2 = b.emit("V", lambda e: TT(
            e, a4[:, :, :, 1:CH], d4[:, :, :, CH - 1:0:-1],
            d4[:, :, :, CH - 1:0:-1], ALU.bypass), inc=1)
        ev_Sx = b.emit("V", lambda e: e.tensor_tensor_scan(
            e_fm[:].rearrange("p c t -> p (c t)"),
            abuf[:].rearrange("p c t -> p (c t)"),
            resetv, 0.0, ALU.mult, ALU.add), waits=(ev_a2,), inc=1)
        ke4 = kh_e[:].rearrange("p c (n l) -> p c n l", l=2 * CH)
        ko4 = kh_o[:].rearrange("p c (n l) -> p c n l", l=2 * CH)
        k4 = k_fm[:].rearrange("p c (n l) -> p c n l", l=2 * CH)
        ev_khe = b.emit("V", lambda e: TT(
            e, ke4[:, :, :, 0:CH], k4[:, :, :, 0:CH],
            e_fm[:].rearrange("p c (n l) -> p c n l", l=2 * CH)
            [:, :, :, CH - 1::-1], ALU.mult),
            waits=(ev_Sx, ev_k, E["tm"]), inc=1)
        ev_kho = b.emit("V", lambda e: TT(
            e, ko4[:, :, :, CH:2 * CH], k4[:, :, :, CH:2 * CH],
            e_fm[:].rearrange("p c (n l) -> p c n l", l=2 * CH)
            [:, :, :, 2 * CH - 1:CH - 1:-1], ALU.mult), inc=1)
        ev_tm = None
        for (src_t, dst, evd) in ((kh_e, khtm[0], ev_khe),
                                  (kh_o, khtm[1], ev_kho)):
            evt = None
            for c in range(FC):
                evt = b.emit("T", lambda e, src_t=src_t, c=c: e.transpose(
                    PST[:, 128 * c:128 * c + 128], fm(src_t, c), id_bf),
                    waits=(evd, E["pst"]) if c == 0 else (),
                    inc=1 if c == FC - 1 else 0)
            def tevac(e, dst=dst):
                return e.copy(dst[:].rearrange("p c t -> p (c t)"), PST[:])
            ev_tm = b.emit("S", tevac, waits=(evt,), inc=1)
            E["pst"] = ev_tm
        evt = None
        for c in range(FC):
            evt = b.emit("T", lambda e, c=c: e.transpose(
                PSA[:, 128 * c:128 * c + 128], fm(v_fm, c), id_f),
                waits=(ev_v, E["psa"]) if c == 0 else (),
                inc=1 if c == FC - 1 else 0)
        ev_tm = b.emit("S", lambda e: e.copy(
            vtm[:].rearrange("p c t -> p (c t)"), PSA[:]),
            waits=(evt,), inc=1)
        E["psa"] = ev_tm
        E["tm"] = ev_tm

    def a_gproj(i):
        ev_mix = None
        for c in range(FC):
            ev_mix = b.emit("V", lambda e, c=c: e.scalar_tensor_tensor(
                fm(xmix["xg"], c), fm(xx_fm, c), vecs["mug"][:, c:c + 1],
                fm(h_fm, c), ALU.mult, ALU.add),
                waits=(E["gp"],) if c == 0 else (),
                inc=1 if c == FC - 1 else 0)
        ev_gp = None
        for m in range(FC):
            ps = pslot[m % 2]
            evm = None
            for kb in range(FC):
                evm = b.emit("T", lambda e, m=m, kb=kb, ps=ps: e.matmul(
                    ps[:, 0:TILE], wsb["Wg"][:, kb, 128 * m:128 * m + 128],
                    fm(xmix["xg"], kb), start=(kb == 0),
                    stop=(kb == FC - 1)),
                    waits=(ev_mix, ld["Wg"], ev_psb_free[m % 2])
                    if kb == 0 else (), inc=1 if kb == FC - 1 else 0)
            def evac(e, m=m, ps=ps):
                return e.copy(fm(g_fm, m), ps[:, 0:TILE])
            ev_gp = b.emit("S", evac,
                           waits=(evm, E["gnlast"]) if m == 0 else (evm,),
                           inc=1)
            ev_psb_free[m % 2] = ev_gp
        E["gp"] = ev_gp
        ev_se = b.emit("S", lambda e: e.activation(
            sig_t[:].rearrange("p c t -> p (c t)"),
            g_fm[:].rearrange("p c t -> p (c t)"), AF.Tanh, scale=0.5),
            waits=(ev_gp, E["sxp"]), inc=1)
        b.emit("V", lambda e: e.tensor_scalar(
            sig_t[:].rearrange("p c t -> p (c t)"),
            sig_t[:].rearrange("p c t -> p (c t)"), 0.5, 0.5,
            ALU.mult, ALU.add), waits=(ev_se,))
        E["g"] = b.emit("V", lambda e: TT(
            e, g_fm[:].rearrange("p c t -> p (c t)"),
            g_fm[:].rearrange("p c t -> p (c t)"),
            sig_t[:].rearrange("p c t -> p (c t)"), ALU.mult), inc=1)

    def a_chain(i):
        warm = i < WARM_TILES
        ev_oi = None
        for ci in range(NCH):
            row0 = 32 * (ci // 2)
            par = ci % 2
            ktm = khtm[par]
            puv = PSU[:, 512 * par:512 * par + 512]
            ev_u = None
            for h in range(H):
                ev_u = b.emit("T", lambda e, h=h, row0=row0, ktm=ktm,
                              puv=puv: e.matmul(
                                  puv[64 * (h % 2):64 * (h % 2) + 64,
                                      DK * (h // 2):DK * (h // 2) + DK],
                                  ktm[row0:row0 + 32, h // 2,
                                      64 * (h % 2):64 * (h % 2) + 64],
                                  vtm[row0:row0 + 32, h // 2,
                                      64 * (h % 2):64 * (h % 2) + 64],
                                  start=True, stop=True,
                                  tile_position=(row0, 64 * (h % 2))),
                              waits=(E["tm"], E["chain"],
                                     E["uL"].get(par)) if h == 0 else (),
                              inc=1 if h == H - 1 else 0)
            if not warm:
                for h in range(H):
                    ev_oi = b.emit("T", lambda e, h=h, ci=ci: e.matmul(
                        PSA[64 * (h % 2):64 * (h % 2) + 64,
                            128 * (h // 2) + CH * ci:
                            128 * (h // 2) + CH * ci + CH],
                        S[64 * (h % 2):64 * (h % 2) + 64, h // 2, :],
                        rt_fm[64 * (h % 2):64 * (h % 2) + 64, h // 2,
                              CH * ci:CH * ci + CH],
                        start=True, stop=True,
                        tile_position=(64 * (h % 2), 64 * (h % 2))),
                        waits=(E["rt"], E["psa"]) if (h == 0 and ci == 0)
                        else (), inc=1 if h == H - 1 else 0)
            def supd1(e, ci=ci):
                return TT(e, S, S,
                          A_ch[:, :, ci:ci + 1].to_broadcast((128, FC, DK)),
                          ALU.mult)
            b.emit("V", supd1, waits=(ev_u, ev_oi, E["A"]))
            def supd2(e, puv=puv):
                return TT(e, S.rearrange("p c v -> p (c v)"),
                          S.rearrange("p c v -> p (c v)"), puv, ALU.add)
            E["chain"] = b.emit("V", supd2, inc=1)
            E["uL"][par] = E["chain"]
        E["rtfree"] = ev_oi

    def a_delta(i):
        ev_odp = b.emit("S", lambda e: e.copy(
            od_fm[:].rearrange("p c t -> p (c t)"), PSA[:]),
            waits=(E["rtfree"],), inc=1)
        E["psa"] = ev_odp
        G4 = G_fm[:].rearrange("p c (n l) -> p c n l", l=CH)
        rk4 = rk_bf[:].rearrange("p c (n l) -> p c n l", l=CH)
        v4 = v_fm[:].rearrange("p c (n l) -> p c n l", l=CH)
        od4 = od_fm[:].rearrange("p c (n l) -> p c n l", l=CH)
        tp4 = tmp_fm[:].rearrange("p c (n l) -> p c n l", l=CH)
        d4 = d_fm[:].rearrange("p c (n l) -> p c n l", l=CH)
        k4 = k_fm[:].rearrange("p c (n l) -> p c n l", l=CH)
        ev_G = None
        ev_mm = {}
        ev_ctr = {}
        ev_oacc = {}

        def emit_ctr(dl):
            W = CH - dl
            X = PSA if dl % 2 == 0 else PSU
            def contrib(e, dl=dl, W=W, X=X):
                return e.scalar_tensor_tensor(
                    tp4[:, :, :, dl:CH],
                    X[:].rearrange("p (c x) -> p c x", c=FC)
                    [:, :, 0:W * NCH].rearrange("p c (n w) -> p c n w",
                                                n=NCH),
                    1.0, v4[:, :, :, 0:W], ALU.mult, ALU.mult)
            ev_ctr[dl] = b.emit("V", contrib,
                                waits=(ev_mm[dl], E["sxp"]) if dl == 0
                                else (ev_mm[dl],), inc=1)
            def oacc(e, dl=dl):
                return TT(e, od4[:, :, :, dl:CH], od4[:, :, :, dl:CH],
                          tp4[:, :, :, dl:CH], ALU.add)
            ev_oacc[dl] = b.emit("V", oacc, waits=(ev_odp,) if dl == 0
                                 else (), inc=1)

        for dl in range(CH + 2):
            if dl < CH:
                W = CH - dl
                if dl == 0:
                    def g0(e):
                        last = None
                        for c in range(FC):
                            last = e.tensor_scalar(fm(G_fm, c), fm(r_fm, c),
                                                   vecs["uvec"][:, c:c + 1],
                                                   None, ALU.mult)
                        return last
                    ev_G = b.emit("V", g0, inc=1)
                elif dl == 1:
                    ev_G = b.emit("V", lambda e: TT(
                        e, G_fm[:].rearrange("p c t -> p (c t)"),
                        r_fm[:].rearrange("p c t -> p (c t)"),
                        r_fm[:].rearrange("p c t -> p (c t)"), ALU.bypass),
                        inc=1)
                else:
                    def gu(e, dl=dl):
                        return TT(e, G4[:, :, :, dl:CH], G4[:, :, :, dl:CH],
                                  d4[:, :, :, 1:CH - dl + 1], ALU.mult)
                    ev_G = b.emit("V", gu, inc=1)
                def rkx(e, dl=dl, W=W):
                    return e.tensor_tensor(
                        rk4[:, :, :, dl:CH], G4[:, :, :, dl:CH],
                        k4[:, :, :, 0:W], ALU.mult)
                ev_rk = b.emit("V", rkx, inc=1)
            if dl - 2 >= 0:
                emit_ctr(dl - 2)
            if dl < CH:
                X = PSA if dl % 2 == 0 else PSU
                if dl == 0:
                    pw = (ev_odp, ev_const_g)
                elif dl == 1:
                    pw = (E["chain"],)
                else:
                    pw = (ev_ctr[dl - 2],)
                evq = None
                for c in range(FC):
                    evq = b.emit("T", lambda e, c=c, dl=dl, W=W, X=X:
                                 e.matmul(
                        X[:, 128 * c:128 * c + NCH * W].rearrange(
                            "p (n w) -> p n w", n=NCH),
                        jm_sb, rk4[:, c, :, dl:CH], start=True, stop=True),
                        waits=(ev_rk,) + pw if c == 0 else (),
                        inc=1 if c == FC - 1 else 0)
                ev_mm[dl] = evq
        E["dfree"] = ev_G
        E["kvfree"] = ev_oacc[CH - 1]
        E["deltaG"] = ev_oacc[CH - 1]
        E["odacc"] = ev_oacc[CH - 1]
        E["psa"] = ev_oacc[CH - 2]
        E["uL"][0] = ev_oacc[CH - 1]
        E["uL"][1] = ev_oacc[CH - 1]

    def a_tail(i):
        par = i % 2
        xf = x_fm2[par]
        ev_osq = b.emit("V", lambda e: TT(
            e, onrm[:].rearrange("p c t -> p (c t)"),
            od_fm[:].rearrange("p c t -> p (c t)"),
            od_fm[:].rearrange("p c t -> p (c t)"), ALU.mult),
            waits=(E["odacc"], E["rmsT"]), inc=1)
        ev_ocp = b.emit("V", lambda e: TT(
            e, on_bf[:].rearrange("p c t -> p (c t)"),
            od_fm[:].rearrange("p c t -> p (c t)"),
            od_fm[:].rearrange("p c t -> p (c t)"), ALU.bypass), inc=1)
        ev_gs = None
        for c in range(FC):
            b.emit("T", lambda e, c=c: e.matmul(
                PSD[0:H, 0:TILE], hsel[:, c, :], fm(on_bf, c),
                start=(c == 0), stop=(c == FC - 1)),
                waits=(ev_ocp, E["psd"]) if c == 0 else ())
        for c in range(FC):
            ev_gs = b.emit("T", lambda e, c=c: e.matmul(
                PSD[0:H, TILE:2 * TILE], hsel[:, c, :], fm(onrm, c),
                start=(c == 0), stop=(c == FC - 1)),
                inc=1 if c == FC - 1 else 0)
        b.emit("V", lambda e: e.tensor_scalar(
            gst[0:H, 0, :], PSD[0:H, 0:TILE], 1.0 / DK, None, ALU.mult),
            waits=(ev_gs,))
        b.emit("V", lambda e: TT(e, gst[0:H, 1, :], gst[0:H, 0, :],
                                 gst[0:H, 0, :], ALU.mult))
        ev_gv = b.emit("V", lambda e: e.scalar_tensor_tensor(
            gst[0:H, 2, :], PSD[0:H, TILE:2 * TILE], 1.0 / DK,
            gst[0:H, 1, :], ALU.mult, ALU.subtract), inc=1)
        E["psd"] = ev_gv
        ev_gl = b.emit("S", lambda e: e.activation(
            gst[0:H, 3, :], gst[0:H, 2, :], AF.Ln, bias=epsc[0:H, 1:2]),
            waits=(ev_gv,), inc=1)
        ev_gr = b.emit("S", lambda e: e.activation(
            gst[0:H, 4, :], gst[0:H, 3, :], AF.Exp, scale=-0.5),
            waits=(ev_gl,), inc=1)
        ev_gnlast = None
        for c in range(FC):
            ps = pslot[c % 2]
            b.emit("T", lambda e, c=c, ps=ps: e.matmul(
                ps[:, 0:TILE], hmap_f[0:H, c, :], gst[0:H, 0, :],
                start=True, stop=True),
                waits=(ev_gr, ev_psb_free[c % 2]))
            ev_rb = b.emit("T", lambda e, c=c, ps=ps: e.matmul(
                ps[:, TILE:2 * TILE], hmap_f[0:H, c, :], gst[0:H, 4, :],
                start=True, stop=True), inc=1)
            def n1(e, c=c, ps=ps):
                return TT(e, fm(tmp_fm, c), fm(od_fm, c), ps[:, 0:TILE],
                          ALU.subtract)
            b.emit("V", n1, waits=(ev_rb,))
            def n2(e, c=c, ps=ps):
                return TT(e, fm(tmp_fm, c), fm(tmp_fm, c),
                          ps[:, TILE:2 * TILE], ALU.mult)
            ev_n2 = b.emit("V", n2, inc=1)
            ev_psb_free[c % 2] = ev_n2
            def n3(e, c=c):
                return e.tensor_scalar(fm(tmp_fm, c), fm(tmp_fm, c),
                                       vecs["gnw"][:, c:c + 1],
                                       vecs["gnb"][:, c:c + 1],
                                       ALU.mult, ALU.add)
            b.emit("V", n3)
            def n4(e, c=c):
                return TT(e, fm(on_bf, c), fm(tmp_fm, c), fm(g_fm, c),
                          ALU.mult)
            ev_gnlast = b.emit("V", n4, waits=(E["g"],), inc=1)
        E["gnlast"] = ev_gnlast
        ev_u1 = None
        ev_xpu = None
        for m in range(FC):
            ps = pslot[m % 2]
            evm = None
            for kb in range(FC):
                evm = b.emit("T", lambda e, m=m, kb=kb, ps=ps: e.matmul(
                    ps[:, 0:TILE], wsb["Wo"][:, kb, 128 * m:128 * m + 128],
                    fm(on_bf, kb), start=(kb == 0), stop=(kb == FC - 1)),
                    waits=(ev_gnlast, ld["Wo"], ev_psb_free[m % 2])
                    if kb == 0 else (), inc=1 if kb == FC - 1 else 0)
            def xpue(e, m=m, ps=ps, xf=xf):
                return e.scalar_tensor_tensor(
                    fm(tmp_fm, m), fm(xf, m), 2.0, ps[:, 0:TILE],
                    ALU.mult, ALU.add)
            ev_xpu = b.emit("V", xpue, waits=(evm,), inc=1)
            def u1e(e, m=m, ps=ps, xf=xf):
                return TT(e, fm(xf, m), fm(xf, m), ps[:, 0:TILE], ALU.add)
            ev_u1 = b.emit("V", u1e, inc=1)
            ev_psb_free[m % 2] = ev_u1
        rt0 = (i - WARM_TILES) * TILE
        E["su1"] = dma_y(u1_scr[:, :, rt0:rt0 + TILE], xf[:],
                         waits=(ev_u1,), chan="Ys")
        E["sxp"] = dma_y(xpu_scr[:, :, rt0:rt0 + TILE], tmp_fm[:],
                         waits=(ev_xpu, ev_u1), chan="Ys")
        E["xfree"][par] = E["su1"]

    a_front(0)
    for ti in range(NT):
        if ti >= WARM_TILES:
            a_gproj(ti)
        a_chain(ti)
        if ti >= WARM_TILES:
            a_delta(ti)
        if ti + 1 < NT:
            a_front(ti + 1)
        if ti >= WARM_TILES:
            a_tail(ti)
    ev_su1 = E["su1"]
    ev_sxp = E["sxp"]
    ev_psa_free = E["psa"]
    ev_psd_free = E["psd"]
    ev_chain = E["chain"]
    ev_psu_free = E["uL"][0]

    ev_phaseA_T = ("T", b.counts["T"])
    ev_phaseA_V = ("V", b.counts["V"])
    ev_phaseA_done = b.emit("V", lambda e: e.engine_nop()
                            if hasattr(e, "engine_nop") else e.nop(),
                            waits=(ev_su1, ev_sxp, ev_phaseA_T,
                                   ("S", b.counts["S"])), inc=1)

    ar.release()
    ar.mark()
    slots = [ar.f32((8192,)) for _ in range(2)]
    y1bf = ar.bf16((FC, HALF))
    ar.mark()
    uB = ar.f32((FC, 512))
    xpB = ar.f32((FC, 1024))
    h2B = ar.bf16((FC, 1024))
    hidT = ar.bf16((16, 1024))
    y1B = uB
    rowB = ar.f32((4, 512))
    sqR = ar.bf16((FC, 512))

    def w1v(i):
        return slots[i].bitcast(BF16).rearrange("p (a b) -> p a b", a=FC)

    def w2v(i):
        return slots[i].bitcast(BF16).rearrange("p (a b) -> p a b", a=16)

    NBB = T_CORE // 1024
    bseq = []
    for bb in range(NBB):
        for ffh in range(2):
            bseq.append((bb, ffh, 0))
            bseq.append((bb, ffh, 1))
    bld = {}
    bfree = {0: ev_phaseA_done, 1: ev_phaseA_done}

    def b_load(si):
        bb, ffh, wi = bseq[si]
        if wi == 0:
            src = dr["fw1"].rearrange("(kb p) o -> p kb o", p=128)[
                :, :, 2048 * ffh:2048 * (ffh + 1)]
            dst = w1v(si % 2)
        else:
            src = dr["fw2"].rearrange("(kb p) o -> p kb o", p=128)[
                :, 16 * ffh:16 * (ffh + 1), :]
            dst = w2v(si % 2)
        bld[si] = dma_g(dst, src, waits=(bfree[si % 2],),
                        chan="Gm%d" % (si % 2))

    b_load(0)
    b_load(1)

    wpairB = [(pslot[0], pslot[1]), (PSTF, PSD)]
    wpB_free = [[[ev_psb_free[0]], [ev_psb_free[1]]],
                [[E["pst"]], [ev_psd_free]]]
    ev_uB_free = ev_phaseA_done
    ev_y1_w_b = {}
    ev_y1B_free = ev_phaseA_done
    ev_h2_b = None
    for bb in range(NBB):
        t0 = bb * 1024
        ev_lx = dma_y(xpB[:], xpu_scr[:, :, t0:t0 + 1024],
                      waits=(ev_sxp, ("V", b.counts["V"])), chan="Yl")
        for g in range(2):
            ev_lu = dma_y(uB[:], u1_scr[:, :, t0 + 512 * g:t0 + 512 * g + 512],
                          waits=(ev_uB_free, ev_su1, ev_y1B_free),
                          chan="Yl")
            def sqb(e):
                return TT(e, sqR[:].rearrange("p c t -> p (c t)"),
                          uB[:].rearrange("p c t -> p (c t)"),
                          uB[:].rearrange("p c t -> p (c t)"), ALU.mult)
            ev_sq = b.emit("V", sqb, waits=(ev_lu, ev_phaseA_done,
                                            ev_h2_b), inc=1)
            ev_sum = None
            for c in range(FC):
                ev_sum = b.emit("T", lambda e, c=c: e.matmul(
                    PSD[0:1, 0:512], ones_c_bf, sqR[:, c, :],
                    start=(c == 0), stop=(c == FC - 1)),
                    waits=(ev_sq, *wpB_free[1][1]) if c == 0 else (),
                    inc=1 if c == FC - 1 else 0)
            def rln(e):
                return e.activation(rowB[0:1, 0, :], PSD[0:1, 0:512], AF.Ln,
                                    bias=epsc[0:1, 0:1], scale=1.0 / D)
            ev_rl = b.emit("S", rln, waits=(ev_sum,), inc=1)
            def rex(e):
                return e.activation(rowB[0:1, 1, :], rowB[0:1, 0, :], AF.Exp,
                                    scale=-0.5)
            ev_r2 = b.emit("S", rex, waits=(ev_rl,), inc=1)
            wpB_free[1][1] = [ev_r2]
            ev_bc = b.emit("T", lambda e: e.matmul(
                PSC[:, 0:512], ones_r, rowB[0:1, 1, :],
                start=True, stop=True),
                waits=(ev_r2, *wpB_free[0][1]), inc=1)
            ev_h2_b = None
            for c in range(FC):
                ev_h2_b = b.emit("V", lambda e, c=c, g=g:
                                 e.scalar_tensor_tensor(
                    h2B[:, c, 512 * g:512 * (g + 1)], fm(uB, c),
                    vecs["ln2w"][:, c:c + 1], PSC[:, 0:512],
                    ALU.mult, ALU.mult),
                    waits=(ev_bc,), inc=1 if c == FC - 1 else 0)
            wpB_free[0][1] = [ev_h2_b]
            ev_uB_free = ev_h2_b
        ev_xp_w = None
        for ffh in range(2):
            si1 = 4 * bb + 2 * ffh
            si2 = si1 + 1
            wv = w1v(si1 % 2)
            ev_gel = None
            for mm in range(16):
                pr = mm % 2
                pp = wpairB[pr]
                evm = {}
                for kb in range(FC):
                    for cg in range(2):
                        evm[cg] = b.emit("T", lambda e, wv=wv, mm=mm, cg=cg,
                                         kb=kb, pp=pp, ffh=ffh: e.matmul(
                            pp[cg][:, 0:512],
                            wv[:, kb, 128 * mm:128 * mm + 128],
                            h2B[:, kb, 512 * cg:512 * (cg + 1)],
                            start=(kb == 0), stop=(kb == FC - 1)),
                            waits=(bld[si1], ev_h2_b, *wpB_free[pr][0],
                                   *wpB_free[pr][1])
                            if (kb == 0 and cg == 0) else (),
                            inc=1 if kb == FC - 1 else 0)
                for cg in range(2):
                    def gel(e, mm=mm, cg=cg, pp=pp):
                        return e.activation(
                            hidT[:, mm, 512 * cg:512 * (cg + 1)],
                            pp[cg][:, 0:512], AF.Gelu)
                    ev_gel = b.emit("S", gel, waits=(evm[cg],), inc=1)
                    wpB_free[pr][cg] = [ev_gel]
            bfree[si1 % 2] = ev_gel
            if si1 + 2 < len(bseq):
                b_load(si1 + 2)
            wv2b = w2v(si2 % 2)
            ev_v2 = None
            for m in range(FC):
                psm = PSA if m % 2 == 0 else PSU
                evm = None
                for kb in range(16):
                    for cg in range(2):
                        evm = b.emit("T", lambda e, wv2b=wv2b, m=m, psm=psm,
                                     cg=cg, kb=kb: e.matmul(
                            psm[:, 512 * cg:512 * (cg + 1)],
                            wv2b[:, kb, 128 * m:128 * m + 128],
                            hidT[:, kb, 512 * cg:512 * (cg + 1)],
                            start=(kb == 0), stop=(kb == 15)),
                            waits=(bld[si2], ev_gel,
                                   ev_psa_free if m % 2 == 0
                                   else ev_psu_free)
                            if (kb == 0 and cg == 0) else (),
                            inc=1 if (kb == 15 and cg == 1) else 0)
                def v2e(e, m=m, psm=psm):
                    return TT(e, fm(xpB, m), fm(xpB, m), psm[:], ALU.add)
                ev_v2 = b.emit("V", v2e, waits=(evm, ev_lx), inc=1)
                if m % 2 == 0:
                    ev_psa_free = ev_v2
                else:
                    ev_psu_free = ev_v2
            bfree[si2 % 2] = ev_v2
            if si2 + 2 < len(bseq):
                b_load(si2 + 2)
            ev_xp_w = ev_v2
        sqv = h2B[:, :, 0:512]
        cbv = h2B[:, :, 512:1024]
        for g in range(2):
            xslice = xpB[:, :, 512 * g:512 * (g + 1)]
            def sq2(e, xslice=xslice):
                return TT(e, sqv, xslice, xslice, ALU.mult)
            b.emit("V", sq2, waits=(ev_xp_w,) if g == 0 else ())
            def v2c(e, xslice=xslice):
                return TT(e, cbv, xslice, xslice, ALU.bypass)
            ev_v2b = b.emit("V", v2c, inc=1)
            ev_s2 = None
            for c in range(FC):
                b.emit("T", lambda e, c=c: e.matmul(
                    PSD[0:1, 0:512], ones_c_bf, cbv[:, c, :],
                    start=(c == 0), stop=(c == FC - 1)),
                    waits=(ev_v2b, *wpB_free[1][1]) if c == 0 else ())
            for c in range(FC):
                ev_s2 = b.emit("T", lambda e, c=c: e.matmul(
                    PSTF[0:1, 0:512], ones_c_bf, sqv[:, c, :],
                    start=(c == 0), stop=(c == FC - 1)),
                    waits=(*wpB_free[1][0],) if c == 0 else (),
                    inc=1 if c == FC - 1 else 0)
            def lmean(e):
                return e.tensor_scalar(rowB[0:1, 0, :], PSD[0:1, 0:512],
                                       1.0 / D, None, ALU.mult)
            b.emit("V", lmean, waits=(ev_s2,))
            def lm2(e):
                return TT(e, rowB[0:1, 1, :], rowB[0:1, 0, :],
                          rowB[0:1, 0, :], ALU.mult)
            b.emit("V", lm2)
            def lvar(e):
                return e.scalar_tensor_tensor(rowB[0:1, 2, :],
                                              PSTF[0:1, 0:512],
                                              1.0 / D, rowB[0:1, 1, :],
                                              ALU.mult, ALU.subtract)
            ev_lv = b.emit("V", lvar, inc=1)
            wpB_free[1][1] = [ev_lv]
            wpB_free[1][0] = [ev_lv]
            def lln(e):
                return e.activation(rowB[0:1, 3, :], rowB[0:1, 2, :], AF.Ln,
                                    bias=epsc[0:1, 2:3])
            ev_ll = b.emit("S", lln, waits=(ev_lv,), inc=1)
            def lex(e):
                return e.activation(rowB[0:1, 3, :], rowB[0:1, 3, :],
                                    AF.Exp, scale=-0.5)
            ev_lr = b.emit("S", lex, waits=(ev_ll,), inc=1)
            b.emit("T", lambda e: e.matmul(
                PSC[:, 0:512], ones_r, rowB[0:1, 0, :],
                start=True, stop=True),
                waits=(ev_lr, *wpB_free[0][1]))
            ev_bb2 = b.emit("T", lambda e: e.matmul(
                PSB[:, 0:512], ones_r, rowB[0:1, 3, :],
                start=True, stop=True),
                waits=(*wpB_free[0][0],), inc=1)
            ev_y1t = None
            for c in range(FC):
                def za(e, c=c, g=g):
                    return TT(e, fm(y1B, c), xpB[:, c, 512 * g:512 * (g + 1)],
                              PSC[:, 0:512], ALU.subtract)
                b.emit("V", za, waits=(ev_bb2, ev_y1B_free) if c == 0
                       else ())
                def zb(e, c=c):
                    return TT(e, fm(y1B, c), fm(y1B, c), PSB[:, 0:512],
                              ALU.mult)
                ev_y1t = b.emit("V", zb, inc=1)
            wpB_free[0][1] = [ev_y1t]
            wpB_free[0][0] = [ev_y1t]
            ev_y1B_free = dma_y(y1_scr[:, :, t0 + 512 * g:t0 + 512 * g + 512],
                                y1B[:], waits=(ev_y1t,), chan="Ys")
        ev_y1_w_b[bb] = ev_y1B_free

    ev_psb_free[0] = wpB_free[0][0][-1]
    ev_psb_free[1] = wpB_free[0][1][-1]
    ev_pstB = wpB_free[1][0][-1]
    ev_psd_free = wpB_free[1][1][-1]
    ev_phaseB_T = ("T", b.counts["T"])

    ar.release()
    ar.mark()
    gw_sb = ar.bf16((FC, NE))
    dma_g(gw_sb, dr["gate_w"].rearrange("(kb p) o -> p kb o", p=128),
          waits=(ev_phaseB_T, ev_y1_w_b[NBB - 1]), chan="Ga")
    ld["gw"] = ("Ga", b.counts["Ga"])
    wbuf = [slots[i].bitcast(BF16)[:, 0:8192].rearrange(
        "p (a b) -> p a b", a=FC) for i in range(2)]
    hbuf = ar.bf16((FF // 128, HALF))
    accC = ar.f32((FC, HALF))
    gbcf = ar.f32((HALF,))
    tmpC = ar.f32((HALF,))
    lgC = ar.f32((HALF,))
    gtm = ar.f32((HALF // 128, 16))
    gden = ar.f32((16,))
    sqC = ar.bf16((FC, TILE))
    cbC = ar.bf16((FC, TILE))
    y2t = tmpC.rearrange("p (c t) -> p c t", c=FC)
    y1t2 = lgC.rearrange("p (c t) -> p c t", c=FC)
    y2s = gbcf.rearrange("p (c t) -> p c t", c=FC)

    NPART = 12
    wseq = []
    for half in range(2):
        for e_i in range(NE + 1):
            for part in range(NPART):
                wseq.append((half, e_i, part))

    def w_src(e_i, part):
        if part < 4:
            base = dr["mw1"][e_i] if e_i < NE else dr["ws1"]
            src = base.rearrange("(kb p) m -> p kb m", p=128)
            return src[:, :, 1024 * part:1024 * (part + 1)]
        m = part - 4
        base = dr["mw2r"][e_i, m] if e_i < NE else dr["ws2r"][m]
        return base.rearrange("(kb p) o -> p kb o", p=128)

    wld = {}
    wfree = dict(bfree)

    def wv2(i):
        return wbuf[i].rearrange("p a b -> p (a b)")[:, 0:32 * 128].rearrange(
            "p (a b) -> p a b", a=32)

    def load_chunk(si):
        half, e_i, part = wseq[si]
        dst = wbuf[si % 2] if part < 4 else wv2(si % 2)
        wld[si] = dma_g(dst, w_src(e_i, part), waits=(wfree[si % 2],),
                        chan="Gm%d" % (si % 2))

    load_chunk(0)
    load_chunk(1)

    ev_accw = None
    ev_y2done = None
    ev_lgt = None
    ev_y1bf_free = None
    wpair = [(pslot[0], pslot[1]), (PSTF, PSD)]
    wpair_free = [[[ev_psb_free[0]], [ev_psb_free[1]]],
                  [[ev_pstB], [ev_psd_free]]]
    for half in range(2):
        h0 = half * HALF
        ev_y1bf = dma_g(y1bf, y1_scr[:, :, h0:h0 + HALF],
                        waits=(ev_y1_w_b[half], ev_y1bf_free), chan="Gy")
        ev_lg = None
        for cg in range(2):
            evl = None
            for kb in range(FC):
                evl = b.emit("T", lambda e, kb=kb, cg=cg: e.matmul(
                    PSD[0:NE, 0:512], gw_sb[:, kb, :],
                    y1bf[:, kb, 512 * cg:512 * (cg + 1)],
                    start=(kb == 0), stop=(kb == FC - 1)),
                    waits=(ld["gw"], ev_y1bf, ev_psd_free)
                    if kb == 0 else (), inc=1 if kb == FC - 1 else 0)
            def lgb(e, cg=cg):
                return e.activation(lgC[0:NE, cg * 512:(cg + 1) * 512],
                                    PSD[0:NE, 0:512],
                                    AF.Identity, bias=gate_b[0:NE, :])
            ev_lg = b.emit("S", lgb, waits=(evl,), inc=1)
            ev_psd_free = ev_lg
        ev_lgt = None
        for tb in range(HALF // 128):
            evt = b.emit("T", lambda e, tb=tb: e.transpose(
                PSD[0:128, 0:NE], lgC[0:NE, 128 * tb:128 * tb + 128],
                id_f[0:NE, 0:NE]),
                waits=(ev_lg, ev_psd_free) if tb == 0 else (ev_lgt,), inc=1)
            def lcp(e, tb=tb):
                return TT(e, gtm[:, tb, 0:NE], PSD[0:128, 0:NE],
                          PSD[0:128, 0:NE], ALU.bypass)
            ev_lgt = b.emit("V", lcp, waits=(evt,), inc=1)
        ev_psd_free = ev_lgt
        NB = HALF // 128
        eq1v = tmpC.rearrange("p (bb e) -> p bb e", e=NE)[:, 0:NB, :]
        l2v = gbcf.rearrange("p (bb e) -> p bb e", e=NE)[:, 0:NB, :]
        def c_m1(e):
            return e.tensor_reduce(gtm[:, :, 8:9], gtm[:, :, 0:NE], AX.X,
                                   ALU.max)
        b.emit("V", c_m1, waits=(ev_lgt,))
        def c_eq1(e):
            return TT(e, eq1v, gtm[:, :, 0:NE],
                      gtm[:, :, 8:9].to_broadcast((128, NB, NE)),
                      ALU.is_equal)
        b.emit("V", c_eq1)
        def c_l2(e):
            return e.scalar_tensor_tensor(
                l2v, eq1v, -1e30, gtm[:, :, 0:NE], ALU.mult, ALU.add)
        b.emit("V", c_l2)
        def c_m2(e):
            return e.tensor_reduce(gtm[:, :, 9:10], l2v, AX.X, ALU.max)
        b.emit("V", c_m2)
        def c_eq2(e):
            return TT(e, l2v, l2v,
                      gtm[:, :, 9:10].to_broadcast((128, NB, NE)),
                      ALU.is_equal)
        b.emit("V", c_eq2)
        def c_dm(e):
            return TT(e, gtm[:, :, 10:11], gtm[:, :, 9:10], gtm[:, :, 8:9],
                      ALU.subtract)
        ev_cd = b.emit("V", c_dm, inc=1)
        def c_ex(e):
            return e.activation(
                gtm[:, :, 11:12].rearrange("p bb o -> p (bb o)"),
                gtm[:, :, 10:11].rearrange("p bb o -> p (bb o)"), AF.Tanh,
                scale=0.5)
        ev_ce = b.emit("S", c_ex, waits=(ev_cd,), inc=1)
        def c_g1(e):
            return e.tensor_scalar(gtm[:, :, 10:11], gtm[:, :, 11:12],
                                   -0.5, 0.5, ALU.mult, ALU.add)
        b.emit("V", c_g1, waits=(ev_ce,))
        def c_g2(e):
            return e.tensor_scalar(gtm[:, :, 11:12], gtm[:, :, 11:12],
                                   0.5, 0.5, ALU.mult, ALU.add)
        b.emit("V", c_g2)
        def c_w1(e):
            return TT(e, eq1v, eq1v,
                      gtm[:, :, 10:11].to_broadcast((128, NB, NE)), ALU.mult)
        b.emit("V", c_w1)
        def c_w2(e):
            return TT(e, l2v, l2v,
                      gtm[:, :, 11:12].to_broadcast((128, NB, NE)), ALU.mult)
        b.emit("V", c_w2)
        def c_cb(e):
            return TT(e, gtm[:, :, 0:NE], eq1v, l2v, ALU.add)
        ev_cb = b.emit("V", c_cb, inc=1)
        ev_cfm = None
        for tb in range(NB):
            evt = b.emit("T", lambda e, tb=tb: e.transpose(
                PSD[0:NE, 0:128], gtm[:, tb, 0:NE], id_f),
                waits=(ev_cb, ev_psd_free) if tb == 0 else (ev_cfm,), inc=1)
            def ccp(e, tb=tb):
                return TT(e, comb[0:NE, 128 * tb:128 * tb + 128],
                          PSD[0:NE, 0:128], PSD[0:NE, 0:128], ALU.bypass)
            ev_cfm = b.emit("V", ccp, waits=(evt,), inc=1)
        ev_psd_free = ev_cfm

        if KDEBUG:
            dma_y(dbg_gtm[:, 8 * half:8 * half + 8, :], gtm[:],
                  waits=(ev_cb,), chan="Ys")
            dma_y(dbg_lg[:, h0:h0 + HALF], lgC[0:NE, :], waits=(ev_lg,),
                  chan="Ys")
            dma_y(dbg_comb[:, h0:h0 + HALF], comb[0:NE, 0:HALF],
                  waits=(ev_cfm,), chan="Ys")
        def zacc(e):
            return e.memset(accC, 0.0)
        ev_acc0 = b.emit("V", zacc, waits=(ev_y2done,), inc=1)
        ev_accw = ev_acc0
        wpair_free[0][0].append(ev_psb_free[0])
        wpair_free[0][1].append(ev_psb_free[1])
        wpair_free[1][1] = [ev_psd_free]
        ev_gel = None
        for e_i in range(NE + 1):
            shared = e_i == NE
            base_si = half * (NE + 1) * NPART + e_i * NPART
            if not shared:
                b.emit("T", lambda e, e_i=e_i: e.matmul(
                    PSA[:, 0:512], esel[0:NE, e_i, :], comb[0:NE, 0:512],
                    start=True, stop=True),
                    waits=(ev_cfm, ev_psa_free,), inc=0)
                ev_gb = b.emit("T", lambda e, e_i=e_i: e.matmul(
                    PSA[:, 512:1024], esel[0:NE, e_i, :],
                    comb[0:NE, 512:1024], start=True, stop=True), inc=1)
                def gcp(e):
                    return TT(e, gbcf, PSA[:], PSA[:], ALU.bypass)
                ev_gbc = b.emit("V", gcp,
                                waits=(ev_gb, ev_y2done) if e_i == 0
                                else (ev_gb,), inc=1)
                ev_psa_free = ev_gbc
            for part in range(4):
                si = base_si + part
                wv = wbuf[si % 2]
                for mm in range(FC):
                    m = FC * part + mm
                    pr = mm % 2
                    pp = wpair[pr]
                    evm = {}
                    for kb in range(FC):
                        for cg in range(2):
                            evm[cg] = b.emit("T", lambda e, wv=wv, mm=mm,
                                             cg=cg, kb=kb, pp=pp: e.matmul(
                                pp[cg][:, 0:512],
                                wv[:, kb, 128 * mm:128 * mm + 128],
                                y1bf[:, kb, 512 * cg:512 * (cg + 1)],
                                start=(kb == 0), stop=(kb == FC - 1)),
                                waits=(wld[si], ev_y1bf, *wpair_free[pr][0],
                                       *wpair_free[pr][1])
                                if (kb == 0 and cg == 0) else (),
                                inc=1 if kb == FC - 1 else 0)
                    ev_y1bf_free = evm[1]
                    for cg in range(2):
                        def gel(e, m=m, cg=cg, e_i=e_i, shared=shared,
                                pp=pp):
                            bias_ap = (bs1[:, m:m + 1] if shared
                                       else b1v[e_i][:, m:m + 1])
                            return e.activation(
                                hbuf[:, m, 512 * cg:512 * (cg + 1)],
                                pp[cg][:, 0:512], AF.Gelu, bias=bias_ap)
                        ev_gel = b.emit("S", gel, waits=(evm[cg],), inc=1)
                        wpair_free[pr][cg] = [ev_gel]
                wfree[si % 2] = ev_gel
                if si + 2 < len(wseq):
                    load_chunk(si + 2)
            for m in range(FC):
                si = base_si + 4 + m
                wv = wv2(si % 2)
                psm = PSA if m % 2 == 0 else PSU
                evm = None
                for kb in range(32):
                    for cg in range(2):
                        evm = b.emit("T", lambda e, wv=wv, m=m, psm=psm,
                                     cg=cg, kb=kb: e.matmul(
                            psm[:, 512 * cg:512 * (cg + 1)],
                            wv[:, kb, :],
                            hbuf[:, kb, 512 * cg:512 * (cg + 1)],
                            start=(kb == 0), stop=(kb == 31)),
                            waits=(wld[si], ev_gel,
                                   ev_psa_free if m % 2 == 0
                                   else ev_psu_free)
                            if (kb == 0 and cg == 0) else (),
                            inc=1 if (kb == 31 and cg == 1) else 0)
                if shared:
                    def accs(e, m=m, psm=psm):
                        return e.scalar_tensor_tensor(
                            fm(accC, m), psm[:], bs2[:, m:m + 1],
                            fm(accC, m), ALU.add, ALU.add)
                    ev_accw = b.emit("V", accs, waits=(evm, ev_accw), inc=1)
                else:
                    def acc1(e, m=m, e_i=e_i, psm=psm):
                        return e.scalar_tensor_tensor(
                            tmpC, psm[:], b2v[e_i][:, m:m + 1], gbcf,
                            ALU.add, ALU.mult)
                    b.emit("V", acc1, waits=(evm, ev_gbc, ev_accw))
                    def acc2(e, m=m):
                        return TT(e, fm(accC, m), fm(accC, m), tmpC, ALU.add)
                    ev_accw = b.emit("V", acc2, inc=1)
                if m % 2 == 0:
                    ev_psa_free = ev_accw
                else:
                    ev_psu_free = ev_accw
                wfree[si % 2] = ev_accw
                if si + 2 < len(wseq):
                    load_chunk(si + 2)

        ev_psd_free = wpair_free[1][1][-1]
        if KDEBUG:
            dma_y(dbg_acc[:, :, h0:h0 + HALF], accC, waits=(ev_accw,),
                  chan="Ys")
        ev_ydma = None
        ev_y1t2_free = None
        for tt in range(HALF // TILE):
            s0 = tt * TILE
            ev_ly = dma_y(y1t2, y1_scr[:, :, h0 + s0:h0 + s0 + TILE],
                          waits=(ev_accw, ev_y1t2_free), chan="Yl")
            def v3(e, s0=s0):
                return TT(e, y2s, accC[:, :, s0:s0 + TILE], y1t2, ALU.add)
            ev_v3 = b.emit("V", v3, waits=(ev_ly, ev_accw, ev_ydma), inc=1)
            ev_y1t2_free = ev_v3
            def sqx(e):
                return TT(e, sqC.rearrange("p c t -> p (c t)"),
                          y2s.rearrange("p c t -> p (c t)"),
                          y2s.rearrange("p c t -> p (c t)"), ALU.mult)
            b.emit("V", sqx)
            def cbx(e):
                return TT(e, cbC.rearrange("p c t -> p (c t)"),
                          y2s.rearrange("p c t -> p (c t)"),
                          y2s.rearrange("p c t -> p (c t)"), ALU.bypass)
            ev_c = b.emit("V", cbx, inc=1)
            ev_s = None
            for c in range(FC):
                b.emit("T", lambda e, c=c: e.matmul(
                    PSD[0:1, 0:TILE], ones_c_bf, cbC[:, c, :],
                    start=(c == 0), stop=(c == FC - 1)),
                    waits=(ev_c, ev_psd_free) if c == 0 else ())
            for c in range(FC):
                ev_s = b.emit("T", lambda e, c=c: e.matmul(
                    PSD[0:1, TILE:2 * TILE], ones_c_bf, sqC[:, c, :],
                    start=(c == 0), stop=(c == FC - 1)),
                    inc=1 if c == FC - 1 else 0)
            def pm(e):
                return e.tensor_scalar(rrow[0:1, 0, :], PSD[0:1, 0:TILE],
                                       1.0 / D, None, ALU.mult)
            b.emit("V", pm, waits=(ev_s,))
            def pm2(e):
                return TT(e, rrow[0:1, 1, :], rrow[0:1, 0, :],
                          rrow[0:1, 0, :], ALU.mult)
            b.emit("V", pm2)
            def pv(e):
                return e.scalar_tensor_tensor(rrow[0:1, 2, :],
                                              PSD[0:1, TILE:2 * TILE],
                                              1.0 / D, rrow[0:1, 1, :],
                                              ALU.mult, ALU.subtract)
            ev_pv = b.emit("V", pv, inc=1)
            ev_psd_free = ev_pv
            def pl(e):
                return e.activation(rrow[0:1, 3, :], rrow[0:1, 2, :], AF.Ln,
                                    bias=epsc[0:1, 2:3])
            ev_pl = b.emit("S", pl, waits=(ev_pv,), inc=1)
            def pe(e):
                return e.activation(rrow[0:1, 3, :], rrow[0:1, 3, :],
                                    AF.Exp, scale=-0.5)
            ev_pr = b.emit("S", pe, waits=(ev_pl,), inc=1)
            b.emit("T", lambda e: e.matmul(PSC[:, 0:TILE], ones_r,
                                           rrow[0:1, 0, :], start=True,
                                           stop=True),
                   waits=(ev_pr, ev_psb_free[1]))
            ev_bb = b.emit("T", lambda e: e.matmul(
                PSC[:, TILE:2 * TILE], ones_r, rrow[0:1, 3, :],
                start=True, stop=True), inc=1)
            ev_y2 = None
            for c in range(FC):
                def za(e, c=c):
                    return TT(e, y2t[:, c, :], y2s[:, c, :],
                              PSC[:, 0:TILE], ALU.subtract)
                b.emit("V", za, waits=(ev_bb,))
                def zb(e, c=c):
                    return TT(e, y2t[:, c, :], y2t[:, c, :],
                              PSC[:, TILE:2 * TILE], ALU.mult)
                ev_y2 = b.emit("V", zb, inc=1)
            ev_psb_free[1] = ev_y2
            evt = None
            for c in range(FC):
                evt = b.emit("T", lambda e, c=c: e.transpose(
                    PSA[:, 128 * c:128 * c + 128], y2t[:, c, :], id_f),
                    waits=(ev_y2, ev_psa_free) if c == 0 else (),
                    inc=1 if c == FC - 1 else 0)
            def ytc(e):
                return TT(e, y2s.rearrange("p c t -> p (c t)"), PSA[:],
                          PSA[:], ALU.bypass)
            ev_yc = b.emit("V", ytc, waits=(evt,), inc=1)
            ev_psa_free = ev_yc
            r0 = h0 + s0
            ev_ydma = dma_y(y_d[r0:r0 + 128, :],
                            y2s.rearrange("p c t -> p (c t)"),
                            waits=(ev_yc,), chan="Yo")
            ev_y2done = ev_ydma

    b.emit("Y", lambda e: e.nop(), waits=(("Yo", b.counts["Yo"]),
                                          ("Ys", b.counts["Ys"]),))

    with nc.Block() as block:
        b.materialize(block)
    ar.ap = None
    ar.t.__exit__(None, None, None)
    for c in reversed(ctxs):
        c.__exit__(None, None, None)
    return nc


def _host_consts():
    c = {}
    c["ident"] = np.eye(128, dtype=np.float32)
    c["ones_r"] = np.ones((1, 128), np.float32)
    c["ones_c"] = np.ones((128, 1), np.float32)
    hs = np.zeros((128, FC, H), np.float32)
    hm = np.zeros((H, FC, 128), np.float32)
    for cc in range(FC):
        for p in range(128):
            hh = (128 * cc + p) // DK
            hs[p, cc, hh] = 1.0
            hm[hh, cc, p] = 1.0
    c["headsel"] = hs
    c["headmap"] = hm
    jm = np.zeros((128, 128), np.float32)
    jm[:64, :64] = 1.0
    jm[64:, 64:] = 1.0
    c["jmat"] = jm
    es = np.zeros((NE, NE, 128), np.float32)
    for e in range(NE):
        es[e, e, :] = 1.0
    c["esel"] = es
    r1 = np.zeros((128, FC * TILE), np.float32)
    r1[:, ::CH] = 1.0
    c["reset1"] = r1
    return c


def _vec_tile(v):
    return np.ascontiguousarray(np.asarray(v, np.float32).reshape(FC, 128).T)


_CACHE = {}


def kernel(**inp):
    if "nc" not in _CACHE:
        _CACHE["nc"] = build_program()
    nc = _CACHE["nc"]

    x = np.ascontiguousarray(np.asarray(inp["L0_attn_x"], np.float32)[0])
    base = {
        "Wr": inp["L0_attn_Wr"], "Wk": inp["L0_attn_Wk"],
        "Wv": inp["L0_attn_Wv"], "Wg": inp["L0_attn_Wg"],
        "Wo": inp["L0_attn_Wo"], "wA": inp["L0_attn_wA"],
        "wB": inp["L0_attn_wB"], "fw1": inp["L0_attn_ffn_w1"],
        "fw2": inp["L0_attn_ffn_w2"], "mw1": inp["L0_ffn_w1"],
        "mw2r": np.ascontiguousarray(
            np.asarray(inp["L0_ffn_w2"], np.float32).reshape(
                NE, FF, FC, 128).transpose(0, 2, 1, 3)),
        "ws1": inp["L0_ffn_ws1"],
        "ws2r": np.ascontiguousarray(
            np.asarray(inp["L0_ffn_ws2"], np.float32).reshape(
                FF, FC, 128).transpose(1, 0, 2)),
        "gate_w": inp["L0_ffn_gate_w"],
        "ln1w": _vec_tile(inp["L0_attn_ln1_w"]),
        "ln2w": _vec_tile(inp["L0_attn_ln2_w"]),
        "mur": _vec_tile(inp["L0_attn_mu_r"]),
        "muk": _vec_tile(inp["L0_attn_mu_k"]),
        "muv": _vec_tile(inp["L0_attn_mu_v"]),
        "muw": _vec_tile(inp["L0_attn_mu_w"]),
        "mug": _vec_tile(inp["L0_attn_mu_g"]),
        "w0": _vec_tile(inp["L0_attn_w0"]),
        "uvec": _vec_tile(np.asarray(inp["L0_attn_u"],
                                     np.float32).reshape(-1)),
        "gnw": _vec_tile(np.asarray(inp["L0_attn_gn_w"],
                                    np.float32).reshape(-1)),
        "gnb": _vec_tile(np.asarray(inp["L0_attn_gn_b"],
                                    np.float32).reshape(-1)),
        "gate_b": np.asarray(inp["L0_ffn_gate_b"],
                             np.float32).reshape(NE, 1),
        "b1": np.ascontiguousarray(
            np.asarray(inp["L0_ffn_b1"], np.float32).reshape(
                NE, FF // 128, 128).transpose(0, 2, 1)),
        "b2": np.ascontiguousarray(
            np.asarray(inp["L0_ffn_b2"], np.float32).reshape(
                NE, FC, 128).transpose(0, 2, 1)),
        "bs1": np.ascontiguousarray(
            np.asarray(inp["L0_ffn_bs1"], np.float32).reshape(
                FF // 128, 128).T),
        "bs2": np.ascontiguousarray(
            np.asarray(inp["L0_ffn_bs2"], np.float32).reshape(FC, 128).T),
    }
    base.update(_host_consts())
    base = {k: np.ascontiguousarray(np.asarray(v, np.float32))
            for k, v in base.items()}

    in_maps = []
    for ci in range(NCORES):
        s0 = ci * T_CORE - T_WARM
        if s0 < 0:
            xs = np.concatenate(
                [np.zeros((-s0, D), np.float32), x[0:s0 + T_ALL]], axis=0)
        else:
            xs = x[s0:s0 + T_ALL]
        m = dict(base)
        m["x"] = np.ascontiguousarray(xs)
        in_maps.append(m)

    res = run_bass_kernel_spmd(nc, in_maps, core_ids=list(range(NCORES)))
    out = np.concatenate([r["y"] for r in res.results], axis=0)
    return out[None, :, :]


# version 16
# speedup vs baseline: 1.3582x; 1.0056x over previous
import os
import numpy as np
import concourse.bass as bass
import concourse.mybir as mybir
from concourse.bass_utils import run_bass_kernel_spmd

F32 = mybir.dt.float32
BF16 = mybir.dt.bfloat16
ALU = mybir.AluOpType
AF = mybir.ActivationFunctionType
AX = mybir.AxisListType

D = 1024
H = 16
DK = 64
FC = 8
LORA = 64
FF = 4096
NE = 8
NCORES = 8
T_CORE = 2048
T_WARM = 512
T_ALL = T_WARM + T_CORE
TILE = 128
NT = T_ALL // TILE
WARM_TILES = T_WARM // TILE
CH = 16
NCH = TILE // CH
HALF = 1024
EPS_RMS = 1e-6
EPS_GN = 1e-5
EPS_LN = 1e-5

ENGS = ("T", "V", "S", "G", "Y")
CHAN_BODY = {"T": "T", "V": "V", "S": "S", "G": "G", "Y": "Y",
             "Yx": "Y", "Ys": "Y", "Yl": "Y", "Yo": "Y", "Yc": "Y",
             "Gc": "G", "Ga": "G", "Gm0": "G", "Gm1": "G", "Gy": "G"}


class Prog:
    def __init__(self, nc):
        self.nc = nc
        self.ops = []
        self.counts = {c: 0 for c in CHAN_BODY}
        self.sems = {}

    def emit(self, chan, fn, waits=(), inc=0):
        waits = tuple(w for w in waits if w is not None)
        self.ops.append((chan, fn, waits, inc))
        if inc:
            self.counts[chan] += inc
            return (chan, self.counts[chan])
        return None

    def materialize(self, block):
        for c in CHAN_BODY:
            self.sems[c] = self.nc.semaphore("sem_" + c).__enter__()
        watermark = {}

        def body_for(eng_name):
            def body(eng):
                for (oc, fn, waits, inc) in self.ops:
                    if CHAN_BODY[oc] != eng_name:
                        continue
                    for (pc, val) in waits:
                        key = (eng_name, pc)
                        if watermark.get(key, -1) >= val:
                            continue
                        watermark[key] = val
                        eng.wait_ge(self.sems[pc], val)
                    inst = fn(eng)
                    if inc:
                        assert inst is not None
                        inst.then_inc(self.sems[oc], inc)
            return body

        block.tensor(body_for("T"))
        block.vector(body_for("V"))
        block.scalar(body_for("S"))
        block.gpsimd(body_for("G"))
        block.sync(body_for("Y"))


class Arena:

    def __init__(self, nc, cols=52992):
        self.t = nc.sbuf_tensor("arena", [128, cols], F32)
        self.ap = self.t.__enter__()
        self.cols = cols
        self.off = 0
        self.marks = []

    def alloc(self, ncols_f32):
        a = self.off
        self.off += ncols_f32
        assert self.off <= self.cols, f"SBUF arena overflow: {self.off}"
        return a

    def f32(self, shape):
        n = int(np.prod(shape))
        a = self.alloc(n)
        v = self.ap[:, a:a + n]
        if len(shape) == 1:
            return v
        return v.rearrange("p (a b) -> p a b", a=shape[0]) if len(shape) == 2 \
            else v.rearrange("p (a b c) -> p a b c", a=shape[0], b=shape[1])

    def bf16(self, shape):
        n = int(np.prod(shape))
        a = self.alloc((n + 1) // 2)
        v = self.ap[:, a:a + (n + 1) // 2].bitcast(BF16)
        if len(shape) == 1:
            return v[:, 0:n]
        return v[:, 0:n].rearrange("p (a b) -> p a b", a=shape[0]) \
            if len(shape) == 2 else \
            v[:, 0:n].rearrange("p (a b c) -> p a b c", a=shape[0], b=shape[1])

    def mark(self):
        self.marks.append(self.off)

    def release(self):
        self.off = self.marks.pop()


def build_program():
    nc = bass.Bass("TRN2", target_bir_lowering=False, debug=False)
    b = Prog(nc)
    dr = {}

    def din(name, shape):
        dr[name] = nc.dram_tensor(name, list(shape), F32,
                                  kind="ExternalInput").ap()
        return dr[name]

    x_d = din("x", (T_ALL, D))
    for nm in ("Wr", "Wk", "Wv", "Wg", "Wo"):
        din(nm, (D, D))
    din("wA", (D, LORA))
    din("wB", (LORA, D))
    din("fw1", (D, FF))
    din("fw2", (FF, D))
    din("mw1", (NE, D, FF))
    din("mw2r", (NE, FC, FF, 128))
    din("ws1", (D, FF))
    din("ws2r", (FC, FF, 128))
    din("gate_w", (D, NE))
    for nm in ("ln1w", "ln2w", "mur", "muk", "muv", "muw", "mug", "w0",
               "uvec", "gnw", "gnb"):
        din(nm, (128, FC))
    din("gate_b", (NE, 1))
    din("b1", (NE, 128, FF // 128))
    din("b2", (NE, 128, FC))
    din("bs1", (128, FF // 128))
    din("bs2", (128, FC))
    din("ident", (128, 128))
    din("ones_r", (1, 128))
    din("ones_c", (128, 1))
    din("headsel", (128, FC, H))
    din("esel", (NE, NE, 128))
    din("headmap", (H, FC, 128))
    din("jmat", (128, 128))
    din("reset1", (128, FC * TILE))

    y_d = nc.dram_tensor("y", [T_CORE, D], F32, kind="ExternalOutput").ap()
    u1_scr = nc.dram_tensor("u1_scr", [128, FC, T_CORE], F32,
                            kind="Internal").ap()
    xpu_scr = nc.dram_tensor("xpu_scr", [128, FC, T_CORE], F32,
                             kind="Internal").ap()
    y1_scr = nc.dram_tensor("y1_scr", [128, FC, T_CORE], F32,
                            kind="Internal").ap()
    KDEBUG = os.environ.get("KDEBUG", "") == "1"
    if KDEBUG:
        dbg_lg = nc.dram_tensor("dbg_lg", [NE, T_CORE], F32,
                                kind="ExternalOutput").ap()
        dbg_comb = nc.dram_tensor("dbg_comb", [NE, T_CORE], F32,
                                  kind="ExternalOutput").ap()
        dbg_acc = nc.dram_tensor("dbg_acc", [128, FC, T_CORE], F32,
                                 kind="ExternalOutput").ap()
        dbg_u1 = nc.dram_tensor("dbg_u1", [128, FC, T_CORE], F32,
                                kind="ExternalOutput").ap()
        dbg_y1 = nc.dram_tensor("dbg_y1", [128, FC, T_CORE], F32,
                                kind="ExternalOutput").ap()
        dbg_gtm = nc.dram_tensor("dbg_gtm", [128, 16, 16], F32,
                                 kind="ExternalOutput").ap()

    ctxs = []

    def psum(nm, shp, dt=F32):
        t = nc.psum_tensor(nm, list(shp), dt)
        ctxs.append(t)
        return t.__enter__()

    PSA = psum("PSA", (128, 1024), F32)
    PSU = psum("PSU", (128, 1024), F32)
    PSB = psum("PSB", (128, 512), F32)
    PSC = psum("PSC", (128, 512), F32)
    PSD = psum("PSD", (128, 512), F32)
    PSTF = psum("PST", (128, 512), F32)
    PST = PSTF.bitcast(BF16)

    ar = Arena(nc)

    id_f = ar.f32((128,))
    id_bf = ar.bf16((128,))
    ones_r = ar.f32((128,))[0:1, :]
    ones_c_bf = ar.bf16((2,))[:, 0:1]
    jm_sb = ar.bf16((128,))
    esel = ar.f32((NE, 128))
    vecs = {nm: ar.f32((FC,)) for nm in
            ("ln1w", "ln2w", "mur", "muk", "muv", "muw", "mug", "w0",
             "uvec", "gnw", "gnb")}
    gate_b = ar.f32((1,))
    bs1 = ar.f32((FF // 128,))
    bs2 = ar.f32((FC,))
    b1v = [ar.f32((FF // 128,)) for _ in range(NE)]
    b2v = [ar.f32((FC,)) for _ in range(NE)]
    epsc = ar.f32((4,))
    rrow = ar.f32((6, TILE))
    comb = ar.f32((HALF,))

    def dma_y(dst, src, waits=(), chan="Yc"):
        return b.emit(chan, lambda e, d=dst, s=src: e.dma_start(d, s),
                      waits=waits, inc=16)

    def dma_g(dst, src, waits=(), chan="Gc"):
        return b.emit(chan, lambda e, d=dst, s=src: e.dma_start(d, s),
                      waits=waits, inc=16)

    ld = {}
    dma_y(id_f, dr["ident"])
    dma_g(id_bf, dr["ident"])
    dma_y(ones_r, dr["ones_r"])
    dma_g(ones_c_bf, dr["ones_c"])
    dma_g(jm_sb, dr["jmat"])
    dma_y(esel[0:NE, :, :], dr["esel"])
    for nm in vecs:
        dma_y(vecs[nm], dr[nm])
    dma_y(gate_b[0:NE, :], dr["gate_b"])
    dma_y(bs1, dr["bs1"])
    dma_y(bs2, dr["bs2"])
    for e_i in range(NE):
        dma_y(b1v[e_i], dr["b1"][e_i])
        dma_y(b2v[e_i], dr["b2"][e_i])

    ar.mark()
    hsel = ar.bf16((FC, H))
    hmap_f = ar.f32((FC, 128))
    reset1 = ar.f32((FC * TILE,))
    S = ar.f32((FC, DK))
    h_bnd = ar.f32((FC, 1))
    gst = ar.f32((6, TILE))
    wlf = ar.f32((TILE,))
    wlt = ar.bf16((TILE,))
    dma_g(hsel, dr["headsel"])
    dma_y(hmap_f[0:H, :, :], dr["headmap"])
    dma_y(reset1, dr["reset1"])
    ev_const_y = ("Yc", b.counts["Yc"])
    ev_const_g = ("Gc", b.counts["Gc"])
    wsb = {}
    for nm in ("Wr", "Wk", "Wv", "Wg", "Wo"):
        wsb[nm] = ar.bf16((FC, D))
        dma_g(wsb[nm], dr[nm].rearrange("(kb p) o -> p kb o", p=128),
              chan="Ga")
    w_lA = ar.bf16((FC, LORA))
    dma_g(w_lA, dr["wA"].rearrange("(kb p) o -> p kb o", p=128), chan="Ga")
    w_lB = ar.bf16((D,))
    dma_g(w_lB[0:LORA, :], dr["wB"], chan="Ga")
    ev_wAll = ("Ga", b.counts["Ga"])
    for nm in ("Wr", "Wk", "Wv", "Wg", "Wo", "wA", "wB"):
        ld[nm] = ev_wAll

    FMT = (FC, TILE)
    x_tm = ar.f32((D,))
    x_fm2 = [ar.f32(FMT), ar.f32(FMT)]
    h_fm = ar.f32(FMT)
    xx_fm = ar.f32(FMT)
    xmix = {nm: ar.bf16(FMT) for nm in ("xr", "xk", "xv", "xw", "xg")}
    r_fm = ar.f32(FMT)
    k_fm = ar.f32(FMT)
    v_fm = ar.f32(FMT)
    d_fm = ar.f32(FMT)
    g_fm = ar.f32(FMT)
    e_fm = ar.f32(FMT)
    abuf = ar.f32(FMT)
    A_ch = ar.f32((FC, NCH))
    rt_fm = ar.f32(FMT)
    kh_e = ar.bf16(FMT)
    kh_o = ar.bf16(FMT)
    khtm = [ar.bf16((FC, 128)) for _ in range(2)]
    vtm = ar.bf16((FC, 128))
    G_fm = ar.f32(FMT)
    rk_bf = ar.bf16(FMT)
    od_fm = ar.f32(FMT)
    tmp_fm = ar.f32(FMT)
    onrm = ar.bf16(FMT)
    onrm_t = ar.bf16(FMT)
    on_bf = ar.bf16(FMT)
    sig_t = tmp_fm

    def TT(e, out, a, bb, op):
        if op == ALU.bypass:
            return e.tensor_copy(out, a)
        return e.tensor_tensor(out, a, bb, op)

    def fm(t, c=None, sl=None):
        a = t
        if c is not None:
            a = a[:, c, :]
        if sl is not None:
            a = a[:, sl] if c is not None else a[:, :, sl]
        return a

    def _init(e):
        e.memset(epsc[:, 0:1], EPS_RMS)
        e.memset(epsc[:, 1:2], EPS_GN)
        e.memset(epsc[:, 2:3], EPS_LN)
        e.memset(S, 0.0)
        e.memset(h_bnd, 0.0)
        e.memset(abuf, 0.0)
        e.memset(kh_e, 0.0)
        return e.memset(kh_o, 0.0)
    ev_init = b.emit("V", _init, inc=1)

    ev_chain = ev_init
    ev_u_last = {}
    ev_tile_T_done = None
    ev_xfm_free = ev_init
    ev_oevac = None
    ev_psa_free = ev_init
    ev_pst_free = ev_init
    ev_psb_free = {0: ev_init, 1: ev_init}
    ev_psd_free = ev_init
    ev_rowfree = ev_init

    resetv = reset1

    E = dict(chain=ev_init, psa=ev_init, pst=ev_init, psd=ev_init,
             xtmfree=None, hbnd=ev_init, su1=None, sxp=None,
             l=None, gp=None, gnlast=None, g=None, dfree=ev_init,
             kvfree=ev_init, tm=ev_init, rt=None, rtfree=ev_init,
             rmsT=None, odpre=None, rk=None, deltaG=ev_init,
             gstT=ev_init)
    E["xfree"] = {0: ev_init, 1: ev_init}
    E["uL"] = {}
    pslot = [PSB, PSC]

    def a_front(i):
        warm = i < WARM_TILES
        par = i % 2
        xf = x_fm2[par]
        xff = xf.rearrange("p c t -> p (c t)")
        t0 = i * TILE
        ev_x = dma_y(x_tm, x_d[t0:t0 + TILE, :],
                     waits=(E["xtmfree"], ev_const_y), chan="Yx")
        ev_xev = None
        for hf in range(2):
            evt = None
            for j in range(4):
                c = 4 * hf + j
                evt = b.emit("T", lambda e, c=c, j=j: e.transpose(
                    PSD[:, 128 * j:128 * j + 128],
                    x_tm[:, 128 * c:128 * c + 128], id_f),
                    waits=(ev_x, E["psd"]) if j == 0 else (),
                    inc=1 if j == 3 else 0)
            E["xtmfree"] = evt
            def xevac(e, hf=hf, xff=xff):
                return e.copy(xff[:, 512 * hf:512 * hf + 512], PSD[:, 0:512])
            ev_xev = b.emit("S", xevac,
                            waits=(evt, E["xfree"][par]), inc=1)
            E["psd"] = ev_xev
        def sq(e, xff=xff):
            return TT(e, onrm[:].rearrange("p c t -> p (c t)"), xff, xff,
                      ALU.mult)
        ev_sq = b.emit("V", sq, waits=(ev_xev, E["rmsT"]), inc=1)
        ev_sum = None
        for c in range(FC):
            ev_sum = b.emit("T", lambda e, c=c: e.matmul(
                PSD[0:1, 0:TILE], ones_c_bf, fm(onrm, c),
                start=(c == 0), stop=(c == FC - 1)),
                waits=(ev_sq, E["psd"], ev_const_g) if c == 0 else (),
                inc=1 if c == FC - 1 else 0)
        E["rmsT"] = ev_sum
        ev_rl = b.emit("S", lambda e: e.activation(
            rrow[0:1, 0, :], PSD[0:1, 0:TILE], AF.Ln,
            bias=epsc[0:1, 0:1], scale=1.0 / D), waits=(ev_sum,), inc=1)
        E["psd"] = ev_rl
        ev_r2 = b.emit("S", lambda e: e.activation(
            rrow[0:1, 1, :], rrow[0:1, 0, :], AF.Exp, scale=-0.5),
            waits=(ev_rl,), inc=1)
        ev_bc = b.emit("T", lambda e: e.matmul(
            PSC[:, 0:TILE], ones_r, rrow[0:1, 1, :], start=True, stop=True),
            waits=(ev_r2, ev_psb_free[1]), inc=1)
        ev_h = None
        for c in range(FC):
            ev_h = b.emit("V", lambda e, c=c, xf=xf: e.scalar_tensor_tensor(
                fm(h_fm, c), fm(xf, c), vecs["ln1w"][:, c:c + 1],
                PSC[:, 0:TILE], ALU.mult, ALU.mult),
                waits=(ev_bc,), inc=1 if c == FC - 1 else 0)
        ev_psb_free[1] = ev_h
        b.emit("V", lambda e: e.tensor_sub(
            xx_fm[:, :, 1:TILE], h_fm[:, :, 0:TILE - 1], h_fm[:, :, 1:TILE]),
            waits=(ev_h,))
        b.emit("V", lambda e: e.tensor_sub(
            xx_fm[:, :, 0:1], h_bnd, h_fm[:, :, 0:1]))
        b.emit("V", lambda e: TT(e, h_bnd, h_fm[:, :, TILE - 1:TILE],
                                 h_fm[:, :, TILE - 1:TILE], ALU.bypass))
        mixes = [("xk", "muk"), ("xv", "muv"), ("xw", "muw")]
        if not warm:
            mixes.append(("xr", "mur"))
        ev_mix = None
        for nm, mu in mixes:
            for c in range(FC):
                ev_mix = b.emit("V", lambda e, nm=nm, mu=mu, c=c:
                                e.scalar_tensor_tensor(
                                    fm(xmix[nm], c), fm(xx_fm, c),
                                    vecs[mu][:, c:c + 1], fm(h_fm, c),
                                    ALU.mult, ALU.add),
                                waits=(E["l"],) if (nm == mixes[0][0]
                                                   and c == 0) else (),
                                inc=1 if c == FC - 1 else 0)

        def proj(dst_fm, srcb, wname, efree):
            evs = None
            for m in range(FC):
                ps = pslot[m % 2]
                evm = None
                for kb in range(FC):
                    evm = b.emit("T", lambda e, m=m, kb=kb, w=wname, ps=ps:
                                 e.matmul(
                                     ps[:, 0:TILE],
                                     wsb[w][:, kb, 128 * m:128 * m + 128],
                                     fm(xmix[srcb], kb),
                                     start=(kb == 0), stop=(kb == FC - 1)),
                                 waits=(ev_mix, ld[wname],
                                        ev_psb_free[m % 2]) if kb == 0
                                 else (), inc=1 if kb == FC - 1 else 0)
                def evac(e, m=m, ps=ps, dst=dst_fm):
                    return e.copy(fm(dst, m), ps[:, 0:TILE])
                eve = b.emit("S", evac,
                             waits=(evm, efree) if m == 0 else (evm,), inc=1)
                ev_psb_free[m % 2] = eve
                evs = eve
            return evs

        ev_k = proj(k_fm, "xk", "Wk", E["kvfree"])
        ev_v = proj(v_fm, "xv", "Wv", E["kvfree"])
        if not warm:
            ev_r = proj(r_fm, "xr", "Wr", E["kvfree"])
        ev_l = None
        for kb in range(FC):
            ev_l = b.emit("T", lambda e, kb=kb: e.matmul(
                PSD[0:LORA, 0:TILE], w_lA[:, kb, :], fm(xmix["xw"], kb),
                start=(kb == 0), stop=(kb == FC - 1)),
                waits=(ev_mix, ld["wA"], E["psd"]) if kb == 0 else (),
                inc=1 if kb == FC - 1 else 0)
        E["l"] = ev_l
        ev_th = b.emit("S", lambda e: e.activation(
            wlt[0:LORA, :], PSD[0:LORA, 0:TILE], AF.Tanh),
            waits=(ev_l,), inc=1)
        E["psd"] = ev_th
        ev_wp = None
        for m in range(FC):
            ps = pslot[m % 2]
            ev_mm = b.emit("T", lambda e, m=m, ps=ps: e.matmul(
                ps[:, 0:TILE], w_lB[0:LORA, 128 * m:128 * m + 128],
                wlt[0:LORA, :], start=True, stop=True),
                waits=(ev_th, ev_psb_free[m % 2]), inc=1)
            def d_e1(e, m=m, ps=ps):
                return e.activation(fm(d_fm, m), ps[:, 0:TILE], AF.Exp,
                                    bias=vecs["w0"][:, m:m + 1])
            ev_wp = b.emit("S", d_e1,
                           waits=(ev_mm, E["dfree"]) if m == 0
                           else (ev_mm,), inc=1)
            ev_psb_free[m % 2] = ev_wp
        ev_d = b.emit("S", lambda e: e.activation(
            d_fm[:].rearrange("p c t -> p (c t)"),
            d_fm[:].rearrange("p c t -> p (c t)"), AF.Exp, scale=-1.0),
            waits=(ev_wp, ("S", b.counts["S"])), inc=1)
        d4 = d_fm[:].rearrange("p c (n l) -> p c n l", l=CH)
        a4 = abuf[:].rearrange("p c (n l) -> p c n l", l=CH)
        e4 = e_fm[:].rearrange("p c (n l) -> p c n l", l=CH)
        ev_af = b.emit("V", lambda e: TT(
            e, a4[:, :, :, 1:CH], d4[:, :, :, 0:CH - 1],
            d4[:, :, :, 0:CH - 1], ALU.bypass), waits=(ev_d,), inc=1)
        ev_E = b.emit("V", lambda e: e.tensor_tensor_scan(
            e_fm[:].rearrange("p c t -> p (c t)"),
            abuf[:].rearrange("p c t -> p (c t)"),
            resetv, 0.0, ALU.mult, ALU.add), waits=(ev_af,), inc=1)
        ev_A = b.emit("V", lambda e: TT(
            e, A_ch[:].rearrange("p c (n o) -> p c n o", o=1),
            e4[:, :, :, CH - 1:CH], d4[:, :, :, CH - 1:CH], ALU.mult),
            inc=1)
        E["A"] = ev_A
        if not warm:
            ev_rt = b.emit("V", lambda e: TT(
                e, rt_fm[:].rearrange("p c t -> p (c t)"),
                r_fm[:].rearrange("p c t -> p (c t)"),
                e_fm[:].rearrange("p c t -> p (c t)"), ALU.mult),
                waits=(ev_r, E["rtfree"]), inc=1)
            E["rt"] = ev_rt
        ev_a2 = b.emit("V", lambda e: TT(
            e, a4[:, :, :, 1:CH], d4[:, :, :, CH - 1:0:-1],
            d4[:, :, :, CH - 1:0:-1], ALU.bypass), inc=1)
        ev_Sx = b.emit("V", lambda e: e.tensor_tensor_scan(
            e_fm[:].rearrange("p c t -> p (c t)"),
            abuf[:].rearrange("p c t -> p (c t)"),
            resetv, 0.0, ALU.mult, ALU.add), waits=(ev_a2,), inc=1)
        ke4 = kh_e[:].rearrange("p c (n l) -> p c n l", l=2 * CH)
        ko4 = kh_o[:].rearrange("p c (n l) -> p c n l", l=2 * CH)
        k4 = k_fm[:].rearrange("p c (n l) -> p c n l", l=2 * CH)
        ev_khe = b.emit("V", lambda e: TT(
            e, ke4[:, :, :, 0:CH], k4[:, :, :, 0:CH],
            e_fm[:].rearrange("p c (n l) -> p c n l", l=2 * CH)
            [:, :, :, CH - 1::-1], ALU.mult),
            waits=(ev_Sx, ev_k, E["tm"]), inc=1)
        ev_kho = b.emit("V", lambda e: TT(
            e, ko4[:, :, :, CH:2 * CH], k4[:, :, :, CH:2 * CH],
            e_fm[:].rearrange("p c (n l) -> p c n l", l=2 * CH)
            [:, :, :, 2 * CH - 1:CH - 1:-1], ALU.mult), inc=1)
        ev_tm = None
        for (src_t, dst, evd) in ((kh_e, khtm[0], ev_khe),
                                  (kh_o, khtm[1], ev_kho)):
            evt = None
            for c in range(FC):
                evt = b.emit("T", lambda e, src_t=src_t, c=c: e.transpose(
                    PST[:, 128 * c:128 * c + 128], fm(src_t, c), id_bf),
                    waits=(evd, E["pst"]) if c == 0 else (),
                    inc=1 if c == FC - 1 else 0)
            def tevac(e, dst=dst):
                return e.copy(dst[:].rearrange("p c t -> p (c t)"), PST[:])
            ev_tm = b.emit("S", tevac, waits=(evt,), inc=1)
            E["pst"] = ev_tm
        evt = None
        for c in range(FC):
            evt = b.emit("T", lambda e, c=c: e.transpose(
                PSA[:, 128 * c:128 * c + 128], fm(v_fm, c), id_f),
                waits=(ev_v, E["psa"]) if c == 0 else (),
                inc=1 if c == FC - 1 else 0)
        ev_tm = b.emit("S", lambda e: e.copy(
            vtm[:].rearrange("p c t -> p (c t)"), PSA[:]),
            waits=(evt,), inc=1)
        E["psa"] = ev_tm
        E["tm"] = ev_tm

    def a_gproj(i):
        ev_mix = None
        for c in range(FC):
            ev_mix = b.emit("V", lambda e, c=c: e.scalar_tensor_tensor(
                fm(xmix["xg"], c), fm(xx_fm, c), vecs["mug"][:, c:c + 1],
                fm(h_fm, c), ALU.mult, ALU.add),
                waits=(E["gp"],) if c == 0 else (),
                inc=1 if c == FC - 1 else 0)
        ev_gp = None
        for m in range(FC):
            ps = pslot[m % 2]
            evm = None
            for kb in range(FC):
                evm = b.emit("T", lambda e, m=m, kb=kb, ps=ps: e.matmul(
                    ps[:, 0:TILE], wsb["Wg"][:, kb, 128 * m:128 * m + 128],
                    fm(xmix["xg"], kb), start=(kb == 0),
                    stop=(kb == FC - 1)),
                    waits=(ev_mix, ld["Wg"], ev_psb_free[m % 2])
                    if kb == 0 else (), inc=1 if kb == FC - 1 else 0)
            def evac(e, m=m, ps=ps):
                return e.copy(fm(g_fm, m), ps[:, 0:TILE])
            ev_gp = b.emit("S", evac,
                           waits=(evm, E["gnlast"]) if m == 0 else (evm,),
                           inc=1)
            ev_psb_free[m % 2] = ev_gp
        E["gp"] = ev_gp
        ev_se = b.emit("S", lambda e: e.activation(
            sig_t[:].rearrange("p c t -> p (c t)"),
            g_fm[:].rearrange("p c t -> p (c t)"), AF.Tanh, scale=0.5),
            waits=(ev_gp, E["sxp"]), inc=1)
        b.emit("V", lambda e: e.tensor_scalar(
            sig_t[:].rearrange("p c t -> p (c t)"),
            sig_t[:].rearrange("p c t -> p (c t)"), 0.5, 0.5,
            ALU.mult, ALU.add), waits=(ev_se,))
        E["g"] = b.emit("V", lambda e: TT(
            e, g_fm[:].rearrange("p c t -> p (c t)"),
            g_fm[:].rearrange("p c t -> p (c t)"),
            sig_t[:].rearrange("p c t -> p (c t)"), ALU.mult), inc=1)

    def a_chain(i):
        warm = i < WARM_TILES
        ev_oi = None
        for ci in range(NCH):
            row0 = 32 * (ci // 2)
            par = ci % 2
            ktm = khtm[par]
            puv = PSU[:, 512 * par:512 * par + 512]
            ev_u = None
            for h in range(H):
                ev_u = b.emit("T", lambda e, h=h, row0=row0, ktm=ktm,
                              puv=puv: e.matmul(
                                  puv[64 * (h % 2):64 * (h % 2) + 64,
                                      DK * (h // 2):DK * (h // 2) + DK],
                                  ktm[row0:row0 + 32, h // 2,
                                      64 * (h % 2):64 * (h % 2) + 64],
                                  vtm[row0:row0 + 32, h // 2,
                                      64 * (h % 2):64 * (h % 2) + 64],
                                  start=True, stop=True,
                                  tile_position=(row0, 64 * (h % 2))),
                              waits=(E["tm"], E["chain"],
                                     E["uL"].get(par)) if h == 0 else (),
                              inc=1 if h == H - 1 else 0)
            if not warm:
                for h in range(H):
                    ev_oi = b.emit("T", lambda e, h=h, ci=ci: e.matmul(
                        PSA[64 * (h % 2):64 * (h % 2) + 64,
                            128 * (h // 2) + CH * ci:
                            128 * (h // 2) + CH * ci + CH],
                        S[64 * (h % 2):64 * (h % 2) + 64, h // 2, :],
                        rt_fm[64 * (h % 2):64 * (h % 2) + 64, h // 2,
                              CH * ci:CH * ci + CH],
                        start=True, stop=True,
                        tile_position=(64 * (h % 2), 64 * (h % 2))),
                        waits=(E["rt"], E["psa"]) if (h == 0 and ci == 0)
                        else (), inc=1 if h == H - 1 else 0)
            def supd1(e, ci=ci):
                return TT(e, S, S,
                          A_ch[:, :, ci:ci + 1].to_broadcast((128, FC, DK)),
                          ALU.mult)
            b.emit("V", supd1, waits=(ev_u, ev_oi, E["A"]))
            def supd2(e, puv=puv):
                return TT(e, S.rearrange("p c v -> p (c v)"),
                          S.rearrange("p c v -> p (c v)"), puv, ALU.add)
            E["chain"] = b.emit("V", supd2, inc=1)
            E["uL"][par] = E["chain"]
        E["rtfree"] = ev_oi

    def a_delta(i):
        ev_odp = b.emit("S", lambda e: e.copy(
            od_fm[:].rearrange("p c t -> p (c t)"), PSA[:]),
            waits=(E["rtfree"],), inc=1)
        E["psa"] = ev_odp
        G4 = G_fm[:].rearrange("p c (n l) -> p c n l", l=CH)
        rk4 = rk_bf[:].rearrange("p c (n l) -> p c n l", l=CH)
        v4 = v_fm[:].rearrange("p c (n l) -> p c n l", l=CH)
        od4 = od_fm[:].rearrange("p c (n l) -> p c n l", l=CH)
        tp4 = tmp_fm[:].rearrange("p c (n l) -> p c n l", l=CH)
        d4 = d_fm[:].rearrange("p c (n l) -> p c n l", l=CH)
        k4 = k_fm[:].rearrange("p c (n l) -> p c n l", l=CH)
        ev_G = None
        ev_mm = {}
        ev_ctr = {}
        ev_oacc = {}

        def emit_ctr(dl):
            W = CH - dl
            X = PSA if dl % 2 == 0 else PSU
            def contrib(e, dl=dl, W=W, X=X):
                return e.scalar_tensor_tensor(
                    tp4[:, :, :, dl:CH],
                    X[:].rearrange("p (c x) -> p c x", c=FC)
                    [:, :, 0:W * NCH].rearrange("p c (n w) -> p c n w",
                                                n=NCH),
                    1.0, v4[:, :, :, 0:W], ALU.mult, ALU.mult)
            ev_ctr[dl] = b.emit("V", contrib,
                                waits=(ev_mm[dl], E["sxp"]) if dl == 0
                                else (ev_mm[dl],), inc=1)
            def oacc(e, dl=dl):
                return TT(e, od4[:, :, :, dl:CH], od4[:, :, :, dl:CH],
                          tp4[:, :, :, dl:CH], ALU.add)
            ev_oacc[dl] = b.emit("V", oacc, waits=(ev_odp,) if dl == 0
                                 else (), inc=1)

        for dl in range(CH + 2):
            if dl < CH:
                W = CH - dl
                if dl == 0:
                    def g0(e):
                        last = None
                        for c in range(FC):
                            last = e.tensor_scalar(fm(G_fm, c), fm(r_fm, c),
                                                   vecs["uvec"][:, c:c + 1],
                                                   None, ALU.mult)
                        return last
                    ev_G = b.emit("V", g0, inc=1)
                elif dl == 1:
                    ev_G = b.emit("V", lambda e: TT(
                        e, G_fm[:].rearrange("p c t -> p (c t)"),
                        r_fm[:].rearrange("p c t -> p (c t)"),
                        r_fm[:].rearrange("p c t -> p (c t)"), ALU.bypass),
                        inc=1)
                else:
                    def gu(e, dl=dl):
                        return TT(e, G4[:, :, :, dl:CH], G4[:, :, :, dl:CH],
                                  d4[:, :, :, 1:CH - dl + 1], ALU.mult)
                    ev_G = b.emit("V", gu, inc=1)
                def rkx(e, dl=dl, W=W):
                    return e.tensor_tensor(
                        rk4[:, :, :, dl:CH], G4[:, :, :, dl:CH],
                        k4[:, :, :, 0:W], ALU.mult)
                ev_rk = b.emit("V", rkx, inc=1)
            if dl - 2 >= 0:
                emit_ctr(dl - 2)
            if dl < CH:
                X = PSA if dl % 2 == 0 else PSU
                if dl == 0:
                    pw = (ev_odp, ev_const_g)
                elif dl == 1:
                    pw = (E["chain"],)
                else:
                    pw = (ev_ctr[dl - 2],)
                evq = None
                for c in range(FC):
                    evq = b.emit("T", lambda e, c=c, dl=dl, W=W, X=X:
                                 e.matmul(
                        X[:, 128 * c:128 * c + NCH * W].rearrange(
                            "p (n w) -> p n w", n=NCH),
                        jm_sb, rk4[:, c, :, dl:CH], start=True, stop=True),
                        waits=(ev_rk,) + pw if c == 0 else (),
                        inc=1 if c == FC - 1 else 0)
                ev_mm[dl] = evq
        E["dfree"] = ev_G
        E["kvfree"] = ev_oacc[CH - 1]
        E["deltaG"] = ev_oacc[CH - 1]
        E["odacc"] = ev_oacc[CH - 1]
        E["psa"] = ev_oacc[CH - 2]
        E["uL"][0] = ev_oacc[CH - 1]
        E["uL"][1] = ev_oacc[CH - 1]

    def a_tail(i):
        par = i % 2
        xf = x_fm2[par]
        ev_osq = b.emit("V", lambda e: TT(
            e, onrm_t[:].rearrange("p c t -> p (c t)"),
            od_fm[:].rearrange("p c t -> p (c t)"),
            od_fm[:].rearrange("p c t -> p (c t)"), ALU.mult),
            waits=(E["odacc"], E["gstT"]), inc=1)
        ev_ocp = b.emit("V", lambda e: TT(
            e, on_bf[:].rearrange("p c t -> p (c t)"),
            od_fm[:].rearrange("p c t -> p (c t)"),
            od_fm[:].rearrange("p c t -> p (c t)"), ALU.bypass), inc=1)
        ev_gs = None
        for c in range(FC):
            b.emit("T", lambda e, c=c: e.matmul(
                PSD[0:H, 0:TILE], hsel[:, c, :], fm(on_bf, c),
                start=(c == 0), stop=(c == FC - 1)),
                waits=(ev_ocp, E["psd"]) if c == 0 else ())
        for c in range(FC):
            ev_gs = b.emit("T", lambda e, c=c: e.matmul(
                PSD[0:H, TILE:2 * TILE], hsel[:, c, :], fm(onrm_t, c),
                start=(c == 0), stop=(c == FC - 1)),
                inc=1 if c == FC - 1 else 0)
        E["gstT"] = ev_gs
        b.emit("V", lambda e: e.tensor_scalar(
            gst[0:H, 0, :], PSD[0:H, 0:TILE], 1.0 / DK, None, ALU.mult),
            waits=(ev_gs,))
        b.emit("V", lambda e: TT(e, gst[0:H, 1, :], gst[0:H, 0, :],
                                 gst[0:H, 0, :], ALU.mult))
        ev_gv = b.emit("V", lambda e: e.scalar_tensor_tensor(
            gst[0:H, 2, :], PSD[0:H, TILE:2 * TILE], 1.0 / DK,
            gst[0:H, 1, :], ALU.mult, ALU.subtract), inc=1)
        E["psd"] = ev_gv
        ev_gl = b.emit("S", lambda e: e.activation(
            gst[0:H, 3, :], gst[0:H, 2, :], AF.Ln, bias=epsc[0:H, 1:2]),
            waits=(ev_gv,), inc=1)
        ev_gr = b.emit("S", lambda e: e.activation(
            gst[0:H, 4, :], gst[0:H, 3, :], AF.Exp, scale=-0.5),
            waits=(ev_gl,), inc=1)
        ev_gnlast = None
        for c in range(FC):
            ps = pslot[c % 2]
            b.emit("T", lambda e, c=c, ps=ps: e.matmul(
                ps[:, 0:TILE], hmap_f[0:H, c, :], gst[0:H, 0, :],
                start=True, stop=True),
                waits=(ev_gr, ev_psb_free[c % 2]))
            ev_rb = b.emit("T", lambda e, c=c, ps=ps: e.matmul(
                ps[:, TILE:2 * TILE], hmap_f[0:H, c, :], gst[0:H, 4, :],
                start=True, stop=True), inc=1)
            def n1(e, c=c, ps=ps):
                return TT(e, fm(tmp_fm, c), fm(od_fm, c), ps[:, 0:TILE],
                          ALU.subtract)
            b.emit("V", n1, waits=(ev_rb,))
            def n2(e, c=c, ps=ps):
                return TT(e, fm(tmp_fm, c), fm(tmp_fm, c),
                          ps[:, TILE:2 * TILE], ALU.mult)
            ev_n2 = b.emit("V", n2, inc=1)
            ev_psb_free[c % 2] = ev_n2
            def n3(e, c=c):
                return e.tensor_scalar(fm(tmp_fm, c), fm(tmp_fm, c),
                                       vecs["gnw"][:, c:c + 1],
                                       vecs["gnb"][:, c:c + 1],
                                       ALU.mult, ALU.add)
            b.emit("V", n3)
            def n4(e, c=c):
                return TT(e, fm(on_bf, c), fm(tmp_fm, c), fm(g_fm, c),
                          ALU.mult)
            ev_gnlast = b.emit("V", n4, waits=(E["g"],), inc=1)
        E["gnlast"] = ev_gnlast
        ev_u1 = None
        ev_xpu = None
        for m in range(FC):
            ps = pslot[m % 2]
            evm = None
            for kb in range(FC):
                evm = b.emit("T", lambda e, m=m, kb=kb, ps=ps: e.matmul(
                    ps[:, 0:TILE], wsb["Wo"][:, kb, 128 * m:128 * m + 128],
                    fm(on_bf, kb), start=(kb == 0), stop=(kb == FC - 1)),
                    waits=(ev_gnlast, ld["Wo"], ev_psb_free[m % 2])
                    if kb == 0 else (), inc=1 if kb == FC - 1 else 0)
            def xpue(e, m=m, ps=ps, xf=xf):
                return e.scalar_tensor_tensor(
                    fm(tmp_fm, m), fm(xf, m), 2.0, ps[:, 0:TILE],
                    ALU.mult, ALU.add)
            ev_xpu = b.emit("V", xpue, waits=(evm,), inc=1)
            def u1e(e, m=m, ps=ps, xf=xf):
                return TT(e, fm(xf, m), fm(xf, m), ps[:, 0:TILE], ALU.add)
            ev_u1 = b.emit("V", u1e, inc=1)
            ev_psb_free[m % 2] = ev_u1
        rt0 = (i - WARM_TILES) * TILE
        E["su1"] = dma_y(u1_scr[:, :, rt0:rt0 + TILE], xf[:],
                         waits=(ev_u1,), chan="Ys")
        E["sxp"] = dma_y(xpu_scr[:, :, rt0:rt0 + TILE], tmp_fm[:],
                         waits=(ev_xpu, ev_u1), chan="Ys")
        E["xfree"][par] = E["su1"]

    a_front(0)
    for ti in range(NT):
        if ti >= WARM_TILES:
            a_gproj(ti)
        a_chain(ti)
        if ti >= WARM_TILES:
            a_delta(ti)
        if ti + 1 < NT:
            a_front(ti + 1)
        if ti >= WARM_TILES:
            a_tail(ti)
    ev_su1 = E["su1"]
    ev_sxp = E["sxp"]
    ev_psa_free = E["psa"]
    ev_psd_free = E["psd"]
    ev_chain = E["chain"]
    ev_psu_free = E["uL"][0]

    ev_phaseA_T = ("T", b.counts["T"])
    ev_phaseA_V = ("V", b.counts["V"])
    ev_phaseA_done = b.emit("V", lambda e: e.engine_nop()
                            if hasattr(e, "engine_nop") else e.nop(),
                            waits=(ev_su1, ev_sxp, ev_phaseA_T,
                                   ("S", b.counts["S"])), inc=1)

    ar.release()
    ar.mark()
    slots = [ar.f32((8192,)) for _ in range(2)]
    y1bf = ar.bf16((FC, HALF))
    ar.mark()
    uB = ar.f32((FC, 512))
    xpB = ar.f32((FC, 1024))
    h2B = ar.bf16((FC, 1024))
    hidT = ar.bf16((16, 1024))
    y1B = uB
    rowB = ar.f32((4, 512))
    sqR = ar.bf16((FC, 512))

    def w1v(i):
        return slots[i].bitcast(BF16).rearrange("p (a b) -> p a b", a=FC)

    def w2v(i):
        return slots[i].bitcast(BF16).rearrange("p (a b) -> p a b", a=16)

    NBB = T_CORE // 1024
    bseq = []
    for bb in range(NBB):
        for ffh in range(2):
            bseq.append((bb, ffh, 0))
            bseq.append((bb, ffh, 1))
    bld = {}
    bfree = {0: ev_phaseA_done, 1: ev_phaseA_done}

    def b_load(si):
        bb, ffh, wi = bseq[si]
        if wi == 0:
            src = dr["fw1"].rearrange("(kb p) o -> p kb o", p=128)[
                :, :, 2048 * ffh:2048 * (ffh + 1)]
            dst = w1v(si % 2)
        else:
            src = dr["fw2"].rearrange("(kb p) o -> p kb o", p=128)[
                :, 16 * ffh:16 * (ffh + 1), :]
            dst = w2v(si % 2)
        bld[si] = dma_g(dst, src, waits=(bfree[si % 2],),
                        chan="Gm%d" % (si % 2))

    b_load(0)
    b_load(1)

    wpairB = [(pslot[0], pslot[1]), (PSTF, PSD)]
    wpB_free = [[[ev_psb_free[0]], [ev_psb_free[1]]],
                [[E["pst"]], [ev_psd_free]]]
    ev_uB_free = ev_phaseA_done
    ev_y1_w_b = {}
    ev_y1B_free = ev_phaseA_done
    ev_h2_b = None
    for bb in range(NBB):
        t0 = bb * 1024
        ev_lx = dma_y(xpB[:], xpu_scr[:, :, t0:t0 + 1024],
                      waits=(ev_sxp, ("V", b.counts["V"])), chan="Yl")
        for g in range(2):
            ev_lu = dma_y(uB[:], u1_scr[:, :, t0 + 512 * g:t0 + 512 * g + 512],
                          waits=(ev_uB_free, ev_su1, ev_y1B_free),
                          chan="Yl")
            def sqb(e):
                return TT(e, sqR[:].rearrange("p c t -> p (c t)"),
                          uB[:].rearrange("p c t -> p (c t)"),
                          uB[:].rearrange("p c t -> p (c t)"), ALU.mult)
            ev_sq = b.emit("V", sqb, waits=(ev_lu, ev_phaseA_done,
                                            ev_h2_b), inc=1)
            ev_sum = None
            for c in range(FC):
                ev_sum = b.emit("T", lambda e, c=c: e.matmul(
                    PSD[0:1, 0:512], ones_c_bf, sqR[:, c, :],
                    start=(c == 0), stop=(c == FC - 1)),
                    waits=(ev_sq, *wpB_free[1][1]) if c == 0 else (),
                    inc=1 if c == FC - 1 else 0)
            def rln(e):
                return e.activation(rowB[0:1, 0, :], PSD[0:1, 0:512], AF.Ln,
                                    bias=epsc[0:1, 0:1], scale=1.0 / D)
            ev_rl = b.emit("S", rln, waits=(ev_sum,), inc=1)
            def rex(e):
                return e.activation(rowB[0:1, 1, :], rowB[0:1, 0, :], AF.Exp,
                                    scale=-0.5)
            ev_r2 = b.emit("S", rex, waits=(ev_rl,), inc=1)
            wpB_free[1][1] = [ev_r2]
            ev_bc = b.emit("T", lambda e: e.matmul(
                PSC[:, 0:512], ones_r, rowB[0:1, 1, :],
                start=True, stop=True),
                waits=(ev_r2, *wpB_free[0][1]), inc=1)
            ev_h2_b = None
            for c in range(FC):
                ev_h2_b = b.emit("V", lambda e, c=c, g=g:
                                 e.scalar_tensor_tensor(
                    h2B[:, c, 512 * g:512 * (g + 1)], fm(uB, c),
                    vecs["ln2w"][:, c:c + 1], PSC[:, 0:512],
                    ALU.mult, ALU.mult),
                    waits=(ev_bc,), inc=1 if c == FC - 1 else 0)
            wpB_free[0][1] = [ev_h2_b]
            ev_uB_free = ev_h2_b
        ev_xp_w = None
        for ffh in range(2):
            si1 = 4 * bb + 2 * ffh
            si2 = si1 + 1
            wv = w1v(si1 % 2)
            ev_gel = None
            for mm in range(16):
                pr = mm % 2
                pp = wpairB[pr]
                evm = {}
                for kb in range(FC):
                    for cg in range(2):
                        evm[cg] = b.emit("T", lambda e, wv=wv, mm=mm, cg=cg,
                                         kb=kb, pp=pp, ffh=ffh: e.matmul(
                            pp[cg][:, 0:512],
                            wv[:, kb, 128 * mm:128 * mm + 128],
                            h2B[:, kb, 512 * cg:512 * (cg + 1)],
                            start=(kb == 0), stop=(kb == FC - 1)),
                            waits=(bld[si1], ev_h2_b, *wpB_free[pr][0],
                                   *wpB_free[pr][1])
                            if (kb == 0 and cg == 0) else (),
                            inc=1 if kb == FC - 1 else 0)
                for cg in range(2):
                    def gel(e, mm=mm, cg=cg, pp=pp):
                        return e.activation(
                            hidT[:, mm, 512 * cg:512 * (cg + 1)],
                            pp[cg][:, 0:512], AF.Gelu)
                    ev_gel = b.emit("S", gel, waits=(evm[cg],), inc=1)
                    wpB_free[pr][cg] = [ev_gel]
            bfree[si1 % 2] = ev_gel
            if si1 + 2 < len(bseq):
                b_load(si1 + 2)
            wv2b = w2v(si2 % 2)
            ev_v2 = None
            for m in range(FC):
                psm = PSA if m % 2 == 0 else PSU
                evm = None
                for kb in range(16):
                    for cg in range(2):
                        evm = b.emit("T", lambda e, wv2b=wv2b, m=m, psm=psm,
                                     cg=cg, kb=kb: e.matmul(
                            psm[:, 512 * cg:512 * (cg + 1)],
                            wv2b[:, kb, 128 * m:128 * m + 128],
                            hidT[:, kb, 512 * cg:512 * (cg + 1)],
                            start=(kb == 0), stop=(kb == 15)),
                            waits=(bld[si2], ev_gel,
                                   ev_psa_free if m % 2 == 0
                                   else ev_psu_free)
                            if (kb == 0 and cg == 0) else (),
                            inc=1 if (kb == 15 and cg == 1) else 0)
                def v2e(e, m=m, psm=psm):
                    return TT(e, fm(xpB, m), fm(xpB, m), psm[:], ALU.add)
                ev_v2 = b.emit("V", v2e, waits=(evm, ev_lx), inc=1)
                if m % 2 == 0:
                    ev_psa_free = ev_v2
                else:
                    ev_psu_free = ev_v2
            bfree[si2 % 2] = ev_v2
            if si2 + 2 < len(bseq):
                b_load(si2 + 2)
            ev_xp_w = ev_v2
        sqv = h2B[:, :, 0:512]
        cbv = h2B[:, :, 512:1024]
        for g in range(2):
            xslice = xpB[:, :, 512 * g:512 * (g + 1)]
            def sq2(e, xslice=xslice):
                return TT(e, sqv, xslice, xslice, ALU.mult)
            b.emit("V", sq2, waits=(ev_xp_w,) if g == 0 else ())
            def v2c(e, xslice=xslice):
                return TT(e, cbv, xslice, xslice, ALU.bypass)
            ev_v2b = b.emit("V", v2c, inc=1)
            ev_s2 = None
            for c in range(FC):
                b.emit("T", lambda e, c=c: e.matmul(
                    PSD[0:1, 0:512], ones_c_bf, cbv[:, c, :],
                    start=(c == 0), stop=(c == FC - 1)),
                    waits=(ev_v2b, *wpB_free[1][1]) if c == 0 else ())
            for c in range(FC):
                ev_s2 = b.emit("T", lambda e, c=c: e.matmul(
                    PSTF[0:1, 0:512], ones_c_bf, sqv[:, c, :],
                    start=(c == 0), stop=(c == FC - 1)),
                    waits=(*wpB_free[1][0],) if c == 0 else (),
                    inc=1 if c == FC - 1 else 0)
            def lmean(e):
                return e.tensor_scalar(rowB[0:1, 0, :], PSD[0:1, 0:512],
                                       1.0 / D, None, ALU.mult)
            b.emit("V", lmean, waits=(ev_s2,))
            def lm2(e):
                return TT(e, rowB[0:1, 1, :], rowB[0:1, 0, :],
                          rowB[0:1, 0, :], ALU.mult)
            b.emit("V", lm2)
            def lvar(e):
                return e.scalar_tensor_tensor(rowB[0:1, 2, :],
                                              PSTF[0:1, 0:512],
                                              1.0 / D, rowB[0:1, 1, :],
                                              ALU.mult, ALU.subtract)
            ev_lv = b.emit("V", lvar, inc=1)
            wpB_free[1][1] = [ev_lv]
            wpB_free[1][0] = [ev_lv]
            def lln(e):
                return e.activation(rowB[0:1, 3, :], rowB[0:1, 2, :], AF.Ln,
                                    bias=epsc[0:1, 2:3])
            ev_ll = b.emit("S", lln, waits=(ev_lv,), inc=1)
            def lex(e):
                return e.activation(rowB[0:1, 3, :], rowB[0:1, 3, :],
                                    AF.Exp, scale=-0.5)
            ev_lr = b.emit("S", lex, waits=(ev_ll,), inc=1)
            b.emit("T", lambda e: e.matmul(
                PSC[:, 0:512], ones_r, rowB[0:1, 0, :],
                start=True, stop=True),
                waits=(ev_lr, *wpB_free[0][1]))
            ev_bb2 = b.emit("T", lambda e: e.matmul(
                PSB[:, 0:512], ones_r, rowB[0:1, 3, :],
                start=True, stop=True),
                waits=(*wpB_free[0][0],), inc=1)
            ev_y1t = None
            for c in range(FC):
                def za(e, c=c, g=g):
                    return TT(e, fm(y1B, c), xpB[:, c, 512 * g:512 * (g + 1)],
                              PSC[:, 0:512], ALU.subtract)
                b.emit("V", za, waits=(ev_bb2, ev_y1B_free) if c == 0
                       else ())
                def zb(e, c=c):
                    return TT(e, fm(y1B, c), fm(y1B, c), PSB[:, 0:512],
                              ALU.mult)
                ev_y1t = b.emit("V", zb, inc=1)
            wpB_free[0][1] = [ev_y1t]
            wpB_free[0][0] = [ev_y1t]
            ev_y1B_free = dma_y(y1_scr[:, :, t0 + 512 * g:t0 + 512 * g + 512],
                                y1B[:], waits=(ev_y1t,), chan="Ys")
        ev_y1_w_b[bb] = ev_y1B_free

    ev_psb_free[0] = wpB_free[0][0][-1]
    ev_psb_free[1] = wpB_free[0][1][-1]
    ev_pstB = wpB_free[1][0][-1]
    ev_psd_free = wpB_free[1][1][-1]
    ev_phaseB_T = ("T", b.counts["T"])

    ar.release()
    ar.mark()
    gw_sb = ar.bf16((FC, NE))
    dma_g(gw_sb, dr["gate_w"].rearrange("(kb p) o -> p kb o", p=128),
          waits=(ev_phaseB_T, ev_y1_w_b[NBB - 1]), chan="Ga")
    ld["gw"] = ("Ga", b.counts["Ga"])
    wbuf = [slots[i].bitcast(BF16)[:, 0:8192].rearrange(
        "p (a b) -> p a b", a=FC) for i in range(2)]
    hbuf = ar.bf16((FF // 128, HALF))
    accC = ar.f32((FC, HALF))
    gbcf = ar.f32((HALF,))
    tmpC = ar.f32((HALF,))
    lgC = ar.f32((HALF,))
    gtm = ar.f32((HALF // 128, 16))
    gden = ar.f32((16,))
    sqC = ar.bf16((FC, TILE))
    cbC = ar.bf16((FC, TILE))
    y2t = tmpC.rearrange("p (c t) -> p c t", c=FC)
    y1t2 = lgC.rearrange("p (c t) -> p c t", c=FC)
    y2s = gbcf.rearrange("p (c t) -> p c t", c=FC)

    NPART = 12
    wseq = []
    for half in range(2):
        for e_i in range(NE + 1):
            for part in range(NPART):
                wseq.append((half, e_i, part))

    def w_src(e_i, part):
        if part < 4:
            base = dr["mw1"][e_i] if e_i < NE else dr["ws1"]
            src = base.rearrange("(kb p) m -> p kb m", p=128)
            return src[:, :, 1024 * part:1024 * (part + 1)]
        m = part - 4
        base = dr["mw2r"][e_i, m] if e_i < NE else dr["ws2r"][m]
        return base.rearrange("(kb p) o -> p kb o", p=128)

    wld = {}
    wfree = dict(bfree)

    def wv2(i):
        return wbuf[i].rearrange("p a b -> p (a b)")[:, 0:32 * 128].rearrange(
            "p (a b) -> p a b", a=32)

    def load_chunk(si):
        half, e_i, part = wseq[si]
        dst = wbuf[si % 2] if part < 4 else wv2(si % 2)
        wld[si] = dma_g(dst, w_src(e_i, part), waits=(wfree[si % 2],),
                        chan="Gm%d" % (si % 2))

    load_chunk(0)
    load_chunk(1)

    ev_accw = None
    ev_y2done = None
    ev_lgt = None
    ev_y1bf_free = None
    wpair = [(pslot[0], pslot[1]), (PSTF, PSD)]
    wpair_free = [[[ev_psb_free[0]], [ev_psb_free[1]]],
                  [[ev_pstB], [ev_psd_free]]]
    for half in range(2):
        h0 = half * HALF
        ev_y1bf = dma_g(y1bf, y1_scr[:, :, h0:h0 + HALF],
                        waits=(ev_y1_w_b[half], ev_y1bf_free), chan="Gy")
        ev_lg = None
        for cg in range(2):
            evl = None
            for kb in range(FC):
                evl = b.emit("T", lambda e, kb=kb, cg=cg: e.matmul(
                    PSD[0:NE, 0:512], gw_sb[:, kb, :],
                    y1bf[:, kb, 512 * cg:512 * (cg + 1)],
                    start=(kb == 0), stop=(kb == FC - 1)),
                    waits=(ld["gw"], ev_y1bf, ev_psd_free)
                    if kb == 0 else (), inc=1 if kb == FC - 1 else 0)
            def lgb(e, cg=cg):
                return e.activation(lgC[0:NE, cg * 512:(cg + 1) * 512],
                                    PSD[0:NE, 0:512],
                                    AF.Identity, bias=gate_b[0:NE, :])
            ev_lg = b.emit("S", lgb, waits=(evl,), inc=1)
            ev_psd_free = ev_lg
        ev_lgt = None
        for tb in range(HALF // 128):
            evt = b.emit("T", lambda e, tb=tb: e.transpose(
                PSD[0:128, 0:NE], lgC[0:NE, 128 * tb:128 * tb + 128],
                id_f[0:NE, 0:NE]),
                waits=(ev_lg, ev_psd_free) if tb == 0 else (ev_lgt,), inc=1)
            def lcp(e, tb=tb):
                return TT(e, gtm[:, tb, 0:NE], PSD[0:128, 0:NE],
                          PSD[0:128, 0:NE], ALU.bypass)
            ev_lgt = b.emit("V", lcp, waits=(evt,), inc=1)
        ev_psd_free = ev_lgt
        NB = HALF // 128
        eq1v = tmpC.rearrange("p (bb e) -> p bb e", e=NE)[:, 0:NB, :]
        l2v = gbcf.rearrange("p (bb e) -> p bb e", e=NE)[:, 0:NB, :]
        def c_m1(e):
            return e.tensor_reduce(gtm[:, :, 8:9], gtm[:, :, 0:NE], AX.X,
                                   ALU.max)
        b.emit("V", c_m1, waits=(ev_lgt,))
        def c_eq1(e):
            return TT(e, eq1v, gtm[:, :, 0:NE],
                      gtm[:, :, 8:9].to_broadcast((128, NB, NE)),
                      ALU.is_equal)
        b.emit("V", c_eq1)
        def c_l2(e):
            return e.scalar_tensor_tensor(
                l2v, eq1v, -1e30, gtm[:, :, 0:NE], ALU.mult, ALU.add)
        b.emit("V", c_l2)
        def c_m2(e):
            return e.tensor_reduce(gtm[:, :, 9:10], l2v, AX.X, ALU.max)
        b.emit("V", c_m2)
        def c_eq2(e):
            return TT(e, l2v, l2v,
                      gtm[:, :, 9:10].to_broadcast((128, NB, NE)),
                      ALU.is_equal)
        b.emit("V", c_eq2)
        def c_dm(e):
            return TT(e, gtm[:, :, 10:11], gtm[:, :, 9:10], gtm[:, :, 8:9],
                      ALU.subtract)
        ev_cd = b.emit("V", c_dm, inc=1)
        def c_ex(e):
            return e.activation(
                gtm[:, :, 11:12].rearrange("p bb o -> p (bb o)"),
                gtm[:, :, 10:11].rearrange("p bb o -> p (bb o)"), AF.Tanh,
                scale=0.5)
        ev_ce = b.emit("S", c_ex, waits=(ev_cd,), inc=1)
        def c_g1(e):
            return e.tensor_scalar(gtm[:, :, 10:11], gtm[:, :, 11:12],
                                   -0.5, 0.5, ALU.mult, ALU.add)
        b.emit("V", c_g1, waits=(ev_ce,))
        def c_g2(e):
            return e.tensor_scalar(gtm[:, :, 11:12], gtm[:, :, 11:12],
                                   0.5, 0.5, ALU.mult, ALU.add)
        b.emit("V", c_g2)
        def c_w1(e):
            return TT(e, eq1v, eq1v,
                      gtm[:, :, 10:11].to_broadcast((128, NB, NE)), ALU.mult)
        b.emit("V", c_w1)
        def c_w2(e):
            return TT(e, l2v, l2v,
                      gtm[:, :, 11:12].to_broadcast((128, NB, NE)), ALU.mult)
        b.emit("V", c_w2)
        def c_cb(e):
            return TT(e, gtm[:, :, 0:NE], eq1v, l2v, ALU.add)
        ev_cb = b.emit("V", c_cb, inc=1)
        ev_cfm = None
        for tb in range(NB):
            evt = b.emit("T", lambda e, tb=tb: e.transpose(
                PSD[0:NE, 0:128], gtm[:, tb, 0:NE], id_f),
                waits=(ev_cb, ev_psd_free) if tb == 0 else (ev_cfm,), inc=1)
            def ccp(e, tb=tb):
                return TT(e, comb[0:NE, 128 * tb:128 * tb + 128],
                          PSD[0:NE, 0:128], PSD[0:NE, 0:128], ALU.bypass)
            ev_cfm = b.emit("V", ccp, waits=(evt,), inc=1)
        ev_psd_free = ev_cfm

        if KDEBUG:
            dma_y(dbg_gtm[:, 8 * half:8 * half + 8, :], gtm[:],
                  waits=(ev_cb,), chan="Ys")
            dma_y(dbg_lg[:, h0:h0 + HALF], lgC[0:NE, :], waits=(ev_lg,),
                  chan="Ys")
            dma_y(dbg_comb[:, h0:h0 + HALF], comb[0:NE, 0:HALF],
                  waits=(ev_cfm,), chan="Ys")
        def zacc(e):
            return e.memset(accC, 0.0)
        ev_acc0 = b.emit("V", zacc, waits=(ev_y2done,), inc=1)
        ev_accw = ev_acc0
        wpair_free[0][0].append(ev_psb_free[0])
        wpair_free[0][1].append(ev_psb_free[1])
        wpair_free[1][1] = [ev_psd_free]
        ev_gel = None
        for e_i in range(NE + 1):
            shared = e_i == NE
            base_si = half * (NE + 1) * NPART + e_i * NPART
            if not shared:
                b.emit("T", lambda e, e_i=e_i: e.matmul(
                    PSA[:, 0:512], esel[0:NE, e_i, :], comb[0:NE, 0:512],
                    start=True, stop=True),
                    waits=(ev_cfm, ev_psa_free,), inc=0)
                ev_gb = b.emit("T", lambda e, e_i=e_i: e.matmul(
                    PSA[:, 512:1024], esel[0:NE, e_i, :],
                    comb[0:NE, 512:1024], start=True, stop=True), inc=1)
                def gcp(e):
                    return TT(e, gbcf, PSA[:], PSA[:], ALU.bypass)
                ev_gbc = b.emit("V", gcp,
                                waits=(ev_gb, ev_y2done) if e_i == 0
                                else (ev_gb,), inc=1)
                ev_psa_free = ev_gbc
            for part in range(4):
                si = base_si + part
                wv = wbuf[si % 2]
                for mm in range(FC):
                    m = FC * part + mm
                    pr = mm % 2
                    pp = wpair[pr]
                    evm = {}
                    for kb in range(FC):
                        for cg in range(2):
                            evm[cg] = b.emit("T", lambda e, wv=wv, mm=mm,
                                             cg=cg, kb=kb, pp=pp: e.matmul(
                                pp[cg][:, 0:512],
                                wv[:, kb, 128 * mm:128 * mm + 128],
                                y1bf[:, kb, 512 * cg:512 * (cg + 1)],
                                start=(kb == 0), stop=(kb == FC - 1)),
                                waits=(wld[si], ev_y1bf, *wpair_free[pr][0],
                                       *wpair_free[pr][1])
                                if (kb == 0 and cg == 0) else (),
                                inc=1 if kb == FC - 1 else 0)
                    ev_y1bf_free = evm[1]
                    for cg in range(2):
                        def gel(e, m=m, cg=cg, e_i=e_i, shared=shared,
                                pp=pp):
                            bias_ap = (bs1[:, m:m + 1] if shared
                                       else b1v[e_i][:, m:m + 1])
                            return e.activation(
                                hbuf[:, m, 512 * cg:512 * (cg + 1)],
                                pp[cg][:, 0:512], AF.Gelu, bias=bias_ap)
                        ev_gel = b.emit("S", gel, waits=(evm[cg],), inc=1)
                        wpair_free[pr][cg] = [ev_gel]
                wfree[si % 2] = ev_gel
                if si + 2 < len(wseq):
                    load_chunk(si + 2)
            for m in range(FC):
                si = base_si + 4 + m
                wv = wv2(si % 2)
                psm = PSA if m % 2 == 0 else PSU
                evm = None
                for kb in range(32):
                    for cg in range(2):
                        evm = b.emit("T", lambda e, wv=wv, m=m, psm=psm,
                                     cg=cg, kb=kb: e.matmul(
                            psm[:, 512 * cg:512 * (cg + 1)],
                            wv[:, kb, :],
                            hbuf[:, kb, 512 * cg:512 * (cg + 1)],
                            start=(kb == 0), stop=(kb == 31)),
                            waits=(wld[si], ev_gel,
                                   ev_psa_free if m % 2 == 0
                                   else ev_psu_free)
                            if (kb == 0 and cg == 0) else (),
                            inc=1 if (kb == 31 and cg == 1) else 0)
                if shared:
                    def accs(e, m=m, psm=psm):
                        return e.scalar_tensor_tensor(
                            fm(accC, m), psm[:], bs2[:, m:m + 1],
                            fm(accC, m), ALU.add, ALU.add)
                    ev_accw = b.emit("V", accs, waits=(evm, ev_accw), inc=1)
                else:
                    def acc1(e, m=m, e_i=e_i, psm=psm):
                        return e.scalar_tensor_tensor(
                            tmpC, psm[:], b2v[e_i][:, m:m + 1], gbcf,
                            ALU.add, ALU.mult)
                    b.emit("V", acc1, waits=(evm, ev_gbc, ev_accw))
                    def acc2(e, m=m):
                        return TT(e, fm(accC, m), fm(accC, m), tmpC, ALU.add)
                    ev_accw = b.emit("V", acc2, inc=1)
                if m % 2 == 0:
                    ev_psa_free = ev_accw
                else:
                    ev_psu_free = ev_accw
                wfree[si % 2] = ev_accw
                if si + 2 < len(wseq):
                    load_chunk(si + 2)

        ev_psd_free = wpair_free[1][1][-1]
        if KDEBUG:
            dma_y(dbg_acc[:, :, h0:h0 + HALF], accC, waits=(ev_accw,),
                  chan="Ys")
        ev_ydma = None
        ev_y1t2_free = None
        for tt in range(HALF // TILE):
            s0 = tt * TILE
            ev_ly = dma_y(y1t2, y1_scr[:, :, h0 + s0:h0 + s0 + TILE],
                          waits=(ev_accw, ev_y1t2_free), chan="Yl")
            def v3(e, s0=s0):
                return TT(e, y2s, accC[:, :, s0:s0 + TILE], y1t2, ALU.add)
            ev_v3 = b.emit("V", v3, waits=(ev_ly, ev_accw, ev_ydma), inc=1)
            ev_y1t2_free = ev_v3
            def sqx(e):
                return TT(e, sqC.rearrange("p c t -> p (c t)"),
                          y2s.rearrange("p c t -> p (c t)"),
                          y2s.rearrange("p c t -> p (c t)"), ALU.mult)
            b.emit("V", sqx)
            def cbx(e):
                return TT(e, cbC.rearrange("p c t -> p (c t)"),
                          y2s.rearrange("p c t -> p (c t)"),
                          y2s.rearrange("p c t -> p (c t)"), ALU.bypass)
            ev_c = b.emit("V", cbx, inc=1)
            ev_s = None
            for c in range(FC):
                b.emit("T", lambda e, c=c: e.matmul(
                    PSD[0:1, 0:TILE], ones_c_bf, cbC[:, c, :],
                    start=(c == 0), stop=(c == FC - 1)),
                    waits=(ev_c, ev_psd_free) if c == 0 else ())
            for c in range(FC):
                ev_s = b.emit("T", lambda e, c=c: e.matmul(
                    PSD[0:1, TILE:2 * TILE], ones_c_bf, sqC[:, c, :],
                    start=(c == 0), stop=(c == FC - 1)),
                    inc=1 if c == FC - 1 else 0)
            def pm(e):
                return e.tensor_scalar(rrow[0:1, 0, :], PSD[0:1, 0:TILE],
                                       1.0 / D, None, ALU.mult)
            b.emit("V", pm, waits=(ev_s,))
            def pm2(e):
                return TT(e, rrow[0:1, 1, :], rrow[0:1, 0, :],
                          rrow[0:1, 0, :], ALU.mult)
            b.emit("V", pm2)
            def pv(e):
                return e.scalar_tensor_tensor(rrow[0:1, 2, :],
                                              PSD[0:1, TILE:2 * TILE],
                                              1.0 / D, rrow[0:1, 1, :],
                                              ALU.mult, ALU.subtract)
            ev_pv = b.emit("V", pv, inc=1)
            ev_psd_free = ev_pv
            def pl(e):
                return e.activation(rrow[0:1, 3, :], rrow[0:1, 2, :], AF.Ln,
                                    bias=epsc[0:1, 2:3])
            ev_pl = b.emit("S", pl, waits=(ev_pv,), inc=1)
            def pe(e):
                return e.activation(rrow[0:1, 3, :], rrow[0:1, 3, :],
                                    AF.Exp, scale=-0.5)
            ev_pr = b.emit("S", pe, waits=(ev_pl,), inc=1)
            b.emit("T", lambda e: e.matmul(PSC[:, 0:TILE], ones_r,
                                           rrow[0:1, 0, :], start=True,
                                           stop=True),
                   waits=(ev_pr, ev_psb_free[1]))
            ev_bb = b.emit("T", lambda e: e.matmul(
                PSC[:, TILE:2 * TILE], ones_r, rrow[0:1, 3, :],
                start=True, stop=True), inc=1)
            ev_y2 = None
            for c in range(FC):
                def za(e, c=c):
                    return TT(e, y2t[:, c, :], y2s[:, c, :],
                              PSC[:, 0:TILE], ALU.subtract)
                b.emit("V", za, waits=(ev_bb,))
                def zb(e, c=c):
                    return TT(e, y2t[:, c, :], y2t[:, c, :],
                              PSC[:, TILE:2 * TILE], ALU.mult)
                ev_y2 = b.emit("V", zb, inc=1)
            ev_psb_free[1] = ev_y2
            evt = None
            for c in range(FC):
                evt = b.emit("T", lambda e, c=c: e.transpose(
                    PSA[:, 128 * c:128 * c + 128], y2t[:, c, :], id_f),
                    waits=(ev_y2, ev_psa_free) if c == 0 else (),
                    inc=1 if c == FC - 1 else 0)
            def ytc(e):
                return TT(e, y2s.rearrange("p c t -> p (c t)"), PSA[:],
                          PSA[:], ALU.bypass)
            ev_yc = b.emit("V", ytc, waits=(evt,), inc=1)
            ev_psa_free = ev_yc
            r0 = h0 + s0
            ev_ydma = dma_y(y_d[r0:r0 + 128, :],
                            y2s.rearrange("p c t -> p (c t)"),
                            waits=(ev_yc,), chan="Yo")
            ev_y2done = ev_ydma

    b.emit("Y", lambda e: e.nop(), waits=(("Yo", b.counts["Yo"]),
                                          ("Ys", b.counts["Ys"]),))

    with nc.Block() as block:
        b.materialize(block)
    ar.ap = None
    ar.t.__exit__(None, None, None)
    for c in reversed(ctxs):
        c.__exit__(None, None, None)
    return nc


def _host_consts():
    c = {}
    c["ident"] = np.eye(128, dtype=np.float32)
    c["ones_r"] = np.ones((1, 128), np.float32)
    c["ones_c"] = np.ones((128, 1), np.float32)
    hs = np.zeros((128, FC, H), np.float32)
    hm = np.zeros((H, FC, 128), np.float32)
    for cc in range(FC):
        for p in range(128):
            hh = (128 * cc + p) // DK
            hs[p, cc, hh] = 1.0
            hm[hh, cc, p] = 1.0
    c["headsel"] = hs
    c["headmap"] = hm
    jm = np.zeros((128, 128), np.float32)
    jm[:64, :64] = 1.0
    jm[64:, 64:] = 1.0
    c["jmat"] = jm
    es = np.zeros((NE, NE, 128), np.float32)
    for e in range(NE):
        es[e, e, :] = 1.0
    c["esel"] = es
    r1 = np.zeros((128, FC * TILE), np.float32)
    r1[:, ::CH] = 1.0
    c["reset1"] = r1
    return c


def _vec_tile(v):
    return np.ascontiguousarray(np.asarray(v, np.float32).reshape(FC, 128).T)


_CACHE = {}


def kernel(**inp):
    if "nc" not in _CACHE:
        _CACHE["nc"] = build_program()
    nc = _CACHE["nc"]

    x = np.ascontiguousarray(np.asarray(inp["L0_attn_x"], np.float32)[0])
    base = {
        "Wr": inp["L0_attn_Wr"], "Wk": inp["L0_attn_Wk"],
        "Wv": inp["L0_attn_Wv"], "Wg": inp["L0_attn_Wg"],
        "Wo": inp["L0_attn_Wo"], "wA": inp["L0_attn_wA"],
        "wB": inp["L0_attn_wB"], "fw1": inp["L0_attn_ffn_w1"],
        "fw2": inp["L0_attn_ffn_w2"], "mw1": inp["L0_ffn_w1"],
        "mw2r": np.ascontiguousarray(
            np.asarray(inp["L0_ffn_w2"], np.float32).reshape(
                NE, FF, FC, 128).transpose(0, 2, 1, 3)),
        "ws1": inp["L0_ffn_ws1"],
        "ws2r": np.ascontiguousarray(
            np.asarray(inp["L0_ffn_ws2"], np.float32).reshape(
                FF, FC, 128).transpose(1, 0, 2)),
        "gate_w": inp["L0_ffn_gate_w"],
        "ln1w": _vec_tile(inp["L0_attn_ln1_w"]),
        "ln2w": _vec_tile(inp["L0_attn_ln2_w"]),
        "mur": _vec_tile(inp["L0_attn_mu_r"]),
        "muk": _vec_tile(inp["L0_attn_mu_k"]),
        "muv": _vec_tile(inp["L0_attn_mu_v"]),
        "muw": _vec_tile(inp["L0_attn_mu_w"]),
        "mug": _vec_tile(inp["L0_attn_mu_g"]),
        "w0": _vec_tile(inp["L0_attn_w0"]),
        "uvec": _vec_tile(np.asarray(inp["L0_attn_u"],
                                     np.float32).reshape(-1)),
        "gnw": _vec_tile(np.asarray(inp["L0_attn_gn_w"],
                                    np.float32).reshape(-1)),
        "gnb": _vec_tile(np.asarray(inp["L0_attn_gn_b"],
                                    np.float32).reshape(-1)),
        "gate_b": np.asarray(inp["L0_ffn_gate_b"],
                             np.float32).reshape(NE, 1),
        "b1": np.ascontiguousarray(
            np.asarray(inp["L0_ffn_b1"], np.float32).reshape(
                NE, FF // 128, 128).transpose(0, 2, 1)),
        "b2": np.ascontiguousarray(
            np.asarray(inp["L0_ffn_b2"], np.float32).reshape(
                NE, FC, 128).transpose(0, 2, 1)),
        "bs1": np.ascontiguousarray(
            np.asarray(inp["L0_ffn_bs1"], np.float32).reshape(
                FF // 128, 128).T),
        "bs2": np.ascontiguousarray(
            np.asarray(inp["L0_ffn_bs2"], np.float32).reshape(FC, 128).T),
    }
    base.update(_host_consts())
    base = {k: np.ascontiguousarray(np.asarray(v, np.float32))
            for k, v in base.items()}

    in_maps = []
    for ci in range(NCORES):
        s0 = ci * T_CORE - T_WARM
        if s0 < 0:
            xs = np.concatenate(
                [np.zeros((-s0, D), np.float32), x[0:s0 + T_ALL]], axis=0)
        else:
            xs = x[s0:s0 + T_ALL]
        m = dict(base)
        m["x"] = np.ascontiguousarray(xs)
        in_maps.append(m)

    res = run_bass_kernel_spmd(nc, in_maps, core_ids=list(range(NCORES)))
    out = np.concatenate([r["y"] for r in res.results], axis=0)
    return out[None, :, :]
